# version 03
# baseline (speedup 1.0000x reference)
import numpy as np
import ml_dtypes
import concourse.bass as bass
import concourse.mybir as mybir
from concourse.bass_utils import run_bass_kernel_spmd
from concourse.ap import AP

F32 = mybir.dt.float32
BF16 = mybir.dt.bfloat16
BFnp = ml_dtypes.bfloat16
AF = mybir.ActivationFunctionType
ALU = mybir.AluOpType

SEG, C, NH, HD, DM, DH = 2048, 64, 16, 64, 1024, 4096
NCH, NPAIR, NKC, NWIN = 32, 16, 8, 4
FFN_W = (768, 640, 640)
DEBUG = ()
PHASE = "full"
NPAIR_RUN = NPAIR


def bcast(ap, n):
    return AP(ap.tensor, ap.offset, [list(x) for x in ap.ap] + [[0, n]])


def rep(ap, n, m):
    return AP(ap.tensor, ap.offset, [list(ap.ap[0])] + [[0, n]] + [[1, m]])


class Emitter:
    ENGINES = ("sync", "tensor", "vector", "scalar", "gpsimd")

    def __init__(self, nc):
        self.nc = nc
        self.ops = {e: [] for e in self.ENGINES}
        self.sems, self.count = {}, {}
        for name in ("pe", "dv", "ac", "gp", "spd", "gpd", "spd0", "spd1",
                     "gpd0", "gpd1", "cst"):
            self.sems[name] = nc.alloc_semaphore("sem_" + name)
            self.count[name] = 0

    def emit(self, eng, fn, waits=(), inc=None, inc_by=1):
        for (s, v) in waits:
            assert v <= self.count[s], f"{eng}: wait {s}>={v} > {self.count[s]}"
        tok = None
        if inc is not None:
            self.count[inc] += inc_by
            tok = self.count[inc]
        wl = [(self.sems[s], v) for (s, v) in waits if v > 0]
        sem = self.sems[inc] if inc is not None else None

        def closure(e, wl=wl, sem=sem, inc_by=inc_by, fn=fn):
            for (sh, v) in wl:
                e.wait_ge(sh, v)
            ins = fn(e)
            if sem is not None:
                ins.then_inc(sem, inc_by)
        self.ops[eng].append(closure)
        return tok

    def tok(self, s):
        return self.count[s]

    def finalize(self):
        with self.nc.Block() as block:
            for eng in self.ENGINES:
                ops = self.ops[eng]
                getattr(block, eng)(lambda e, ops=ops: [f(e) for f in ops])


class Bank:
    def __init__(self):
        self.w, self.r = [], []

    def for_write(self):
        return list(self.r) + list(self.w)

    def for_read(self):
        return list(self.w)

    def wrote(self, sem, val):
        self.w, self.r = [(sem, val)], []

    def read(self, sem, val):
        self.r = [(sem, val)]


class Arena:
    def __init__(self, ap):
        self.ap = ap
        self.off = 0

    def al(self, shape, dt):
        esz = {F32: 4, BF16: 2}[dt]
        size = shape[1] * esz
        sl = self.ap[0:shape[0], self.off:self.off + size].bitcast(dt)
        self.off += (size + 63) // 64 * 64
        return sl


def build_program():
    nc = bass.Bass("TRN2", target_bir_lowering=False, num_devices=8)
    E = Emitter(nc)

    din = lambda n, shp, dt=F32: nc.dram_tensor(n, shp, dt, kind="ExternalInput").ap()
    xT = din("xT", [DM, SEG])
    sel = din("sel", [128, 4])
    wq, wk, wv, wo = din("wq", [DM, DM]), din("wk", [DM, DM]), din("wv", [DM, DM]), din("wo", [DM, DM])
    wsm = din("wsm", [DM, 96])
    wg2 = din("wg2", [64, DM])
    normw = din("normw", [128, 8])
    w1T, w2T = din("w1T", [DM, DH]), din("w2T", [DM, DH])
    h1T, h2T = din("h1T", [DH, DH]), din("h2T", [DH, DH])
    w3T = din("w3T", [DH, DM])
    scl = din("scl", [128, 136])
    yT = nc.dram_tensor("yT", [DM, SEG], F32, kind="ExternalOutput").ap()

    d_A = nc.dram_tensor("d_A", [NCH, 128, 512], BF16, kind="Internal").ap()
    d_Qt = nc.dram_tensor("d_Qt", [NCH, 128, 512], BF16, kind="Internal").ap()
    d_out = nc.dram_tensor("d_out", [NCH, 128, 512], BF16, kind="Internal").ap()
    d_x1f = nc.dram_tensor("d_x1f", [NKC, 128, SEG], F32, kind="Internal").ap()
    d_x1b = nc.dram_tensor("d_x1b", [NKC, 128, SEG], BF16, kind="Internal").ap()
    ccin = nc.dram_tensor("ccin", [128, 1024], BF16, kind="Internal").ap()
    d_w1b = nc.dram_tensor("d_w1b", [DM, DH], BF16, kind="Internal").ap()
    d_w2b = nc.dram_tensor("d_w2b", [DM, DH], BF16, kind="Internal").ap()
    d_h1b = nc.dram_tensor("d_h1b", [DH, DH], BF16, kind="Internal").ap()
    d_h2b = nc.dram_tensor("d_h2b", [DH, DH], BF16, kind="Internal").ap()
    d_w3b = nc.dram_tensor("d_w3b", [DH, DM], BF16, kind="Internal").ap()
    ccout = nc.dram_tensor("ccout", [512, 1024], BF16, kind="Internal").ap()

    dbg = {}
    for name in DEBUG:
        shp = {"t_q": [128, 1024], "t_kn": [128, 1024], "t_nlc": [128, 16],
               "t_G": [128, 1024], "t_MT": [128, 1024], "t_U": [128, 2048],
               "t_AT": [128, 1024], "t_B": [128, 1024], "t_Qt": [128, 1024],
               "t_S": [128, 512], "t_sin": [128, 512], "t_gate": [128, 4096],
               "t_x1": [128, 4096], "t_a": [128, 4096], "t_fcx": [128, 256],
               }[name]
        dt = F32
        dbg[name] = nc.dram_tensor(name, shp, dt, kind="ExternalOutput").ap()

    def inline(name, arr):
        if arr.dtype == BFnp:
            return nc.inline_tensor(np.ascontiguousarray(arr).view(np.uint16),
                                    name=name).ap().bitcast(BF16)
        return nc.inline_tensor(np.ascontiguousarray(arr), name=name).ap()

    tri_np = np.zeros((128, 64), np.float32)
    for p in range(128):
        tri_np[p, p % 64:] = 1.0
    mG = np.zeros((128, 1024), np.float32)
    mH = np.zeros((128, 1024), np.float32)
    mM = np.zeros((128, 1024), np.float32)
    ii = np.arange(64)
    for p in range(128):
        j = p % 64
        for blk in range(16):
            mG[p, blk * 64:(blk + 1) * 64] = np.where(j < ii, -1.0, 0.0)
            mH[p, blk * 64:(blk + 1) * 64] = np.where(ii < j, -1.0, 0.0)
            mM[p, blk * 64:(blk + 1) * 64] = np.where(j <= ii, 1.0, 0.0)
    c_tri = inline("c_tri", tri_np)
    c_id = inline("c_id", np.eye(128, dtype=np.float32).astype(BFnp))
    c_irep = inline("c_irep", np.tile(np.eye(64, dtype=np.float32), (2, 8)).astype(BFnp))
    c_ones = inline("c_ones", np.ones((128, 128), np.float32))
    c_mG = inline("c_mG", mG.astype(BFnp))
    c_mH = inline("c_mH", mH.astype(BFnp))
    c_mM = inline("c_mM", mM.astype(BFnp))

    sb = lambda n, shp, dt=F32: nc.alloc_sbuf_tensor(n, shp, dt).ap()
    tri_s = sb("tri_s", [128, 64])
    id_s = sb("id_s", [128, 128], BF16)
    irep_s = sb("irep_s", [128, 512], BF16)
    ones_s = sb("ones_s", [128, 128])
    mG_s = sb("mG_s", [128, 1024], BF16)
    mH_s = sb("mH_s", [128, 1024], BF16)
    mM_s = sb("mM_s", [128, 1024], BF16)
    sel_s = sb("sel_s", [128, 4])
    normw_s = sb("normw_s", [128, 8])
    scl_s = sb("scl_s", [128, 136])
    wsm_s = sb("wsm_s", [128, 8 * 96], BF16)
    wg2_s = sb("wg2_s", [64, DM], BF16)
    fcx_p = sb("fcx_p", [128, NCH * 8])
    g1_tm = sb("g1_tm", [128, NPAIR * 64], BF16)
    S_ab = [sb(f"S_ab{i}", [128, 512], BF16) for i in range(2)]
    P_ab = [sb(f"P_ab{i}", [128, 512], BF16) for i in range(2)]
    SF_t = sb("SF_t", [128, 512], BF16)
    PF_t = sb("PF_t", [128, 512], BF16)
    o2_t = sb("o2_t", [128, 512], BF16)
    ostg = sb("ostg", [128, 512], BF16)
    ptseg = sb("ptseg", [128, 512], BF16)
    gat_s = sb("gat_s", [128, 4096], BF16)
    rpj = [sb(f"rpj{i}", [128, 512], BF16) for i in range(4)]
    sin_s = sb("sin_s", [128, 512], BF16)
    dlt = [sb(f"dlt{i}", [128, 512], BF16) for i in range(2)]
    qtc = [sb(f"qtc{i}", [128, 512], BF16) for i in range(2)]
    atc = [sb(f"atc{i}", [128, 512], BF16) for i in range(2)]
    och = [sb(f"och{i}", [128, 512], BF16) for i in range(2)]
    row_t = sb("row_t", [1, 3 * 512])
    rn16 = sb("rn16", [128, 16])
    sg_f = sb("sg_f", [128, 16])
    nlf_s = sb("nlf_s", [128, 16])
    nspg_s = sb("nspg_s", [128, 16])
    nlc_s = sb("nlc_s", [128, 16])
    nn_s = sb("nn_s", [128, 16])
    dd_s = sb("dd_s", [128, 16])
    edec_s = sb("edec_s", [128, 16])
    rowm_s = sb("rowm_s", [128, 16])
    rowg_s = sb("rowg_s", [128, 16])
    el_s = sb("el_s", [128, 16])
    bcol_s = sb("bcol_s", [128, 16])
    nlcc_s = sb("nlcc_s", [128, 32])
    fcrow_s = sb("fcrow_s", [128, 32])

    dbg_stg = sb("dbg_stg", [128, 1024]) if DEBUG else None

    AR_BYTES = 150 * 1024 - (4608 if DEBUG else 0)
    ar_t = nc.alloc_sbuf_tensor("arena", [128, AR_BYTES], mybir.dt.uint8).ap()

    A = Arena(ar_t)
    wq_s = A.al([128, 8 * DM], BF16)
    wk_s = A.al([128, 8 * DM], BF16)
    wv_s = A.al([128, 8 * DM], BF16)
    xwb_ = [A.al([128, NKC * 128], BF16) for i in range(2)]
    q_f = A.al([128, 1024], F32)
    k_f = A.al([128, 1024], F32)
    ksq = A.al([128, 1024], F32)
    kn_f = A.al([128, 1024], F32)
    ktil_tm = A.al([128, 1024], BF16)
    kg_tm = A.al([128, 1024], BF16)
    qm_tm = A.al([128, 1024], BF16)
    qf_tm = A.al([128, 1024], BF16)
    ktil_f = A.al([128, 1024], BF16)
    kg_f = A.al([128, 1024], BF16)
    qm_f = A.al([128, 1024], BF16)
    qf_f = A.al([128, 1024], BF16)
    U_s = A.al([128, 2048], BF16)
    G_sb = [A.al([128, 1024], BF16) for i in range(2)]
    H_sb = [A.al([128, 1024], BF16) for i in range(2)]
    MT_sb = A.al([128, 1024], BF16)
    B_sb = A.al([128, 1024], F32)
    AT_sb = A.al([128, 1024], BF16)
    Qt_sb = A.al([128, 1024], BF16)
    wo_s = A.al([128, 8 * DM], BF16)
    assert A.off <= AR_BYTES, A.off
    wo_off = A.off - 16 * 1024

    T = Arena(ar_t)
    gate_w = T.al([128, 8 * 512], F32)
    og_w = T.al([128, 8 * 512], F32)
    xf_w = T.al([128, 8 * 512], F32)
    x1f_w = T.al([128, 8 * 512], F32)
    sq_w = T.al([128, 8 * 512], F32)
    outw = T.al([128, 8 * 512], BF16)
    on_w = T.al([128, 8 * 512], BF16)
    x1b_w = T.al([128, 8 * 512], BF16)
    g1f_w = T.al([64, 512], BF16)
    tail_end = T.off
    assert tail_end <= wo_off, (tail_end, wo_off)
    F = Arena(ar_t)
    FW = 512
    hstr = [F.al([128, 32 * 256], BF16) for i in range(2)]
    a_sb = F.al([128, 32 * FW], BF16)
    b_sb = F.al([128, 32 * FW], BF16)
    x1w_b = F.al([128, 8 * FW], BF16)
    r2_w = F.al([128, 8 * FW], F32)
    sq2_w = F.al([128, 8 * FW], F32)
    x1fr = [F.al([128, 512], F32) for i in range(2)]
    y_stg = F.al([128, 512], F32)
    silu_t = F.al([128, 512], F32)
    assert max(F.off, tail_end) <= AR_BYTES, (F.off, tail_end)

    p0 = nc.alloc_psum_tensor("p0", [128, 512], F32).ap()
    p1 = nc.alloc_psum_tensor("p1", [128, 512], F32).ap()
    p2 = nc.alloc_psum_tensor("p2", [128, 512], F32).ap()
    p3 = nc.alloc_psum_tensor("p3", [128, 1024], BF16).ap()
    p4 = nc.alloc_psum_tensor("p4", [128, 1024], F32).ap()
    p5 = nc.alloc_psum_tensor("p5", [128, 1024], F32).ap()
    BK = {n: Bank() for n in ("p0", "p1", "p2", "p3", "p4", "p5")}

    def dma(q, dst, src, waits=()):
        eng = "sync" if q.startswith("spd") else "gpsimd"
        return E.emit(eng, lambda e, d=dst, s=src: e.dma_start(out=d, in_=s),
                      waits=waits, inc=q, inc_by=16)

    def mm(lhsT, rhs, out, start, stop, tp=None, waits=()):
        return E.emit("tensor",
                      lambda e, l=lhsT, r=rhs, o=out, s1=start, s2=stop, t=tp:
                      e.matmul(out=o, lhsT=l, rhs=r, start=s1, stop=s2,
                               tile_position=t),
                      waits=waits, inc="pe")

    def trans(in_, out, ident, tp=None, waits=()):
        return E.emit("tensor",
                      lambda e, i=in_, o=out, d=ident, t=tp:
                      e.transpose(out=o, in_=i, identity=d, tile_position=t),
                      waits=waits, inc="pe")

    act = lambda fn, waits=(): E.emit("scalar", fn, waits=waits, inc="ac")
    dve = lambda fn, waits=(): E.emit("vector", fn, waits=waits, inc="dv")
    gpe = lambda fn, waits=(): E.emit("gpsimd", fn, waits=waits, inc="gp")

    def tap(name, src, waits):
        if name not in dbg:
            return
        if src.dtype == BF16:
            return
            n = src.shape[1]
            for c0 in range(0, n, 1024):
                cw = min(1024, n - c0)
                t = act(lambda e, src=src, c0=c0, cw=cw: e.activation(
                    out=dbg_stg[:, 0:cw], in_=src[:, c0:c0 + cw], func=AF.Copy),
                    waits=waits)
                td = dma("gpd", dbg[name][:, c0:c0 + cw],
                         dbg_stg[0:src.shape[0], 0:cw], waits=[("ac", t)])
                waits = [("gpd", td)]
        else:
            dma("gpd", dbg[name][:, :], src, waits=waits)

    rg = lambda h: (h % 2) * 64
    hp = lambda h: h // 2
    g8 = lambda t: t[:].rearrange("p (g i) -> p g i", g=8)
    g16 = lambda t: t[:].rearrange("p (g i) -> p g i", g=16)

    for dst, src in [(tri_s, c_tri), (id_s, c_id), (irep_s, c_irep),
                     (ones_s, c_ones), (mG_s, c_mG), (mH_s, c_mH), (mM_s, c_mM),
                     (sel_s, sel), (normw_s, normw), (scl_s, scl)]:
        dma("spd", dst[:, :], src[:, :])
    ld_c = E.tok("spd")
    for dst, src in [(wq_s, wq), (wk_s, wk), (wv_s, wv), (wo_s, wo)]:
        dma("gpd", dst[:].rearrange("p (kc c) -> p kc c", kc=8),
            src.rearrange("(kc p) c -> p kc c", p=128))
    dma("gpd", wsm_s[:].rearrange("p (kc c) -> p kc c", kc=8),
        wsm.rearrange("(kc p) c -> p kc c", p=128))
    dma("gpd", wg2_s[:, :], wg2[:, :])
    ld_w = E.tok("gpd")

    xw_tok = [None, None]
    for i in range(2):
        xw_tok[i] = dma(f"gpd{i}", xwb_[i][:].rearrange("p (kc t) -> p kc t", kc=NKC),
                        xT.rearrange("(kc p) t -> p kc t", p=128)
                        [:, :, i * 128:(i + 1) * 128], waits=[("gpd", ld_w)])

    def cast_chain():
        tok = 0
        i = 0
        for (wsrc, wdst, ncol) in [(w1T, d_w1b, DH), (w2T, d_w2b, DH),
                                   (w3T, d_w3b, DM), (h1T, d_h1b, DH),
                                   (h2T, d_h2b, DH)]:
            nkcw = wsrc.shape[0] // 128
            for kc in range(nkcw):
                for c0 in range(0, ncol, 2048):
                    cw = min(2048, ncol - c0)
                    half = gat_s[:, (i % 2) * 2048:(i % 2) * 2048 + cw]
                    t1 = dma("cst", half, wsrc[kc * 128:(kc + 1) * 128, c0:c0 + cw],
                             waits=[("cst", tok)] if i >= 2 else [("gpd", ld_w)])
                    tok = dma("cst", wdst[kc * 128:(kc + 1) * 128, c0:c0 + cw],
                              half, waits=[("cst", t1)])
                    i += 1
        return tok

    cast_done = cast_chain()

    dve(lambda e: e.memset(S_ab[0][:, :], 0.0))
    dve(lambda e: e.tensor_copy(out=P_ab[0][:, :], in_=irep_s[:, :]),
        waits=[("spd", ld_c)])
    sp_tok = E.tok("dv")

    g_pe, g_dv, g_ac, g_spd = 0, 0, 0, 0
    s_evac = sp_tok
    p_evac = 0

    for pr in range(NPAIR_RUN):
        xwb = xwb_[pr % 2]
        Wg = [("pe", g_pe), ("dv", g_dv), ("ac", g_ac),
              (f"gpd{pr % 2}", xw_tok[pr % 2]),
              ("spd", E.tok("spd")), ("gpd", ld_w)]

        def proj(wt):
            for half, pm, bk in ((0, p0, "p0"), (1, p1, "p1")):
                for kc in range(NKC):
                    w = (Wg + BK[bk].for_write()) if kc == 0 else ()
                    t = mm(xwb[:, kc * 128:(kc + 1) * 128],
                           wt[:, kc * DM + half * 512: kc * DM + half * 512 + 512],
                           pm[:, :], kc == 0, kc == NKC - 1, waits=w)
                BK[bk].wrote("pe", t)

        proj(wq_s)
        tq1 = act(lambda e: e.activation(out=q_f[:, 0:512], in_=p0[:, :], func=AF.Silu),
                  waits=BK["p0"].for_read() + [("dv", g_dv)])
        BK["p0"].read("ac", tq1)
        tq2 = act(lambda e: e.activation(out=q_f[:, 512:1024], in_=p1[:, :], func=AF.Silu),
                  waits=BK["p1"].for_read())
        BK["p1"].read("ac", tq2)

        if PHASE == "projq" and pr == NPAIR_RUN - 1:
            E.ops["sync"].append(lambda e, n=E.count["spd"], sem=E.sems["spd"]: e.wait_ge(sem, n))
            E.ops["gpsimd"].append(lambda e, n=E.count["gpd"], sem=E.sems["gpd"]: e.wait_ge(sem, n))
            E.finalize()
            return nc
        proj(wk_s)
        tk1 = act(lambda e: e.activation(out=k_f[:, 0:512], in_=p0[:, :], func=AF.Silu),
                  waits=BK["p0"].for_read())
        BK["p0"].read("ac", tk1)
        tk2 = act(lambda e: e.activation(out=k_f[:, 512:1024], in_=p1[:, :], func=AF.Silu),
                  waits=BK["p1"].for_read())
        BK["p1"].read("ac", tk2)
        tap("t_q", q_f[:, :], [("ac", tq2)])
        tsq = act(lambda e: e.activation(out=ksq[:, :], in_=k_f[:, :], func=AF.Square), waits=[("ac", tk2)])
        trd = dve(lambda e: e.tensor_reduce(
            out=rn16[:].rearrange("p (g o) -> p g o", o=1),
            in_=g16(ksq), axis=mybir.AxisListType.X, op=ALU.add),
            waits=[("ac", tsq)])
        tmx = dve(lambda e: e.tensor_scalar(out=rn16[:, :], in0=rn16[:, :],
                                            scalar1=1e-24, scalar2=None, op0=ALU.max),
                  waits=[("dv", trd)])
        proj(wv_s)
        Uv3 = U_s[:].rearrange("p (h c) -> p h c", h=NH)
        tv1 = act(lambda e: e.activation(
            out=Uv3[:, 0:8, 0:64], in_=p0[:].rearrange("p (h c) -> p h c", c=64),
            func=AF.Silu), waits=BK["p0"].for_read())
        BK["p0"].read("ac", tv1)
        tv2 = act(lambda e: e.activation(
            out=Uv3[:, 8:16, 0:64], in_=p1[:].rearrange("p (h c) -> p h c", c=64),
            func=AF.Silu), waits=BK["p1"].for_read())
        BK["p1"].read("ac", tv2)

        if PHASE == "projv" and pr == NPAIR_RUN - 1:
            E.ops["sync"].append(lambda e, n=E.count["spd"], sem=E.sems["spd"]: e.wait_ge(sem, n))
            E.ops["gpsimd"].append(lambda e, n=E.count["gpd"], sem=E.sems["gpd"]: e.wait_ge(sem, n))
            E.finalize()
            return nc
        for kc in range(NKC):
            w = (Wg + BK["p2"].for_write()) if kc == 0 else ()
            tsm = mm(xwb[:, kc * 128:(kc + 1) * 128],
                     wsm_s[:, kc * 96:(kc + 1) * 96], p2[:, 0:96],
                     kc == 0, kc == NKC - 1, waits=w)
        BK["p2"].wrote("pe", tsm)
        tsg1 = act(lambda e: e.activation(out=sg_f[:, :], in_=p2[:, 0:16],
                                          func=AF.Sigmoid), waits=BK["p2"].for_read())
        tsg2 = act(lambda e: e.activation(out=nspg_s[:, :], in_=p2[:, 16:32],
                                          func=AF.Sigmoid))
        tg1 = act(lambda e, pr=pr: e.activation(out=g1_tm[:, pr * 64:(pr + 1) * 64],
                                                in_=p2[:, 32:96], func=AF.Copy))
        BK["p2"].read("ac", tg1)

        if PHASE == "projsm" and pr == NPAIR_RUN - 1:
            E.ops["sync"].append(lambda e, n=E.count["spd"], sem=E.sems["spd"]: e.wait_ge(sem, n))
            E.ops["gpsimd"].append(lambda e, n=E.count["gpd"], sem=E.sems["gpd"]: e.wait_ge(sem, n))
            E.finalize()
            return nc
        tln1 = act(lambda e: e.activation(out=nlf_s[:, :], in_=sg_f[:, :], func=AF.Ln),
                   waits=[("ac", tsg1)])
        tln2 = act(lambda e: e.activation(out=nspg_s[:, :], in_=nspg_s[:, :], func=AF.Ln),
                   waits=[("ac", tsg2)])
        tln3 = act(lambda e: e.activation(out=rn16[:, :], in_=rn16[:, :], func=AF.Ln),
                   waits=[("dv", tmx)])
        tneg1 = dve(lambda e: e.tensor_scalar(out=nlf_s[:, :], in0=nlf_s[:, :],
                                              scalar1=-1.0, scalar2=None, op0=ALU.mult),
                    waits=[("ac", tln1)])
        tneg2 = dve(lambda e: e.tensor_scalar(out=nspg_s[:, :], in0=nspg_s[:, :],
                                              scalar1=-1.0, scalar2=None, op0=ALU.mult),
                    waits=[("ac", tln2)])
        w0 = [("dv", tneg1)] + BK["p2"].for_write()
        mm(tri_s[0:64, :], nlf_s[0:64, :], p2[0:64, 96:112], True, True, waits=w0)
        mm(tri_s[64:128, :], nlf_s[64:128, :], p2[64:128, 96:112], True, True,
           tp=(64, 64))
        mm(ones_s[0:64, 0:1], nlf_s[0:64, :], p2[0:1, 112:128], True, True)
        cm4 = mm(ones_s[64:128, 0:1], nlf_s[64:128, :], p2[64:65, 128:144], True, True,
                 tp=(64, 64))
        BK["p2"].wrote("pe", cm4)
        tnlc = act(lambda e: e.activation(out=nlc_s[:, :], in_=p2[:, 96:112],
                                          func=AF.Copy), waits=BK["p2"].for_read())
        tncc = act(lambda e: e.activation(out=nlcc_s[0:1, 0:16], in_=p2[0:1, 112:128],
                                          func=AF.Copy))
        tncc = act(lambda e: e.activation(out=nlcc_s[64:65, 16:32],
                                          in_=p2[64:65, 128:144], func=AF.Copy))
        tnmv = dma("gpd", nlcc_s[0:1, 16:32], nlcc_s[64:65, 16:32],
                   waits=[("ac", tncc)])
        BK["p2"].read("ac", tncc)
        tap("t_nlc", nlc_s[:, :], [("ac", tnlc)])

        if PHASE == "cums" and pr == NPAIR_RUN - 1:
            E.ops["sync"].append(lambda e, n=E.count["spd"], sem=E.sems["spd"]: e.wait_ge(sem, n))
            E.ops["gpsimd"].append(lambda e, n=E.count["gpd"], sem=E.sems["gpd"]: e.wait_ge(sem, n))
            E.finalize()
            return nc

        wb = BK["p2"].for_write() + [("ac", tncc), ("gpd", tnmv)]
        mm(ones_s[0:1, 0:64], nlcc_s[0:1, 0:16], p2[0:64, 112:128], True, True, waits=wb)
        bc2 = mm(ones_s[0:1, 0:64], nlcc_s[0:1, 16:32], p2[64:128, 112:128], True,
                 True, tp=(0, 64))
        BK["p2"].wrote("pe", bc2)
        tdd = dve(lambda e: e.tensor_tensor(out=dd_s[:, :], in0=p2[:, 112:128],
                                            in1=nlc_s[:, :], op=ALU.subtract),
                  waits=BK["p2"].for_read() + [("ac", tncc)])
        tnn1 = dve(lambda e: e.tensor_tensor(out=nn_s[:, :], in0=nlc_s[:, :],
                                             in1=nlf_s[:, :], op=ALU.add),
                   waits=[("dv", tneg2)])
        tnn2 = dve(lambda e: e.tensor_tensor(out=nn_s[:, :], in0=nn_s[:, :],
                                             in1=nspg_s[:, :], op=ALU.add),
                   waits=[("dv", tnn1)])
        trg = dve(lambda e: e.tensor_tensor(out=rowg_s[:, :], in0=p2[:, 112:128],
                                            in1=nn_s[:, :], op=ALU.subtract),
                  waits=[("dv", tnn2)])
        BK["p2"].read("dv", trg)

        if PHASE == "rows" and pr == NPAIR_RUN - 1:
            E.ops["sync"].append(lambda e, n=E.count["spd"], sem=E.sems["spd"]: e.wait_ge(sem, n))
            E.ops["gpsimd"].append(lambda e, n=E.count["gpd"], sem=E.sems["gpd"]: e.wait_ge(sem, n))
            E.finalize()
            return nc
        te1 = act(lambda e: e.activation(out=edec_s[:, :], in_=dd_s[:, :], func=AF.Exp,
                                         scale=-1.0), waits=[("dv", trg), ("ac", tln3)])
        te2 = act(lambda e: e.activation(out=rowm_s[:, :], in_=dd_s[:, :], func=AF.Exp))
        te3 = act(lambda e: e.activation(out=el_s[:, :], in_=nlc_s[:, :], func=AF.Exp,
                                         scale=-1.0))
        te4 = act(lambda e: e.activation(out=bcol_s[:, :], in_=nn_s[:, :], func=AF.Exp,
                                         scale=-1.0))
        te5 = act(lambda e: e.activation(out=rowg_s[:, :], in_=rowg_s[:, :], func=AF.Exp))
        te6 = act(lambda e: e.activation(out=rn16[:, :], in_=rn16[:, :], func=AF.Exp,
                                         scale=-0.5))
        tfce = act(lambda e: e.activation(out=nlcc_s[0:1, 0:32], in_=nlcc_s[0:1, 0:32],
                                          func=AF.Exp, scale=-1.0),
                   waits=[("gpd", tnmv)])
        tbneg = dve(lambda e: e.tensor_scalar(out=bcol_s[:, :], in0=bcol_s[:, :],
                                              scalar1=-1.0, scalar2=None, op0=ALU.mult),
                    waits=[("ac", te4)])

        if PHASE == "exp" and pr == NPAIR_RUN - 1:
            E.ops["sync"].append(lambda e, n=E.count["spd"], sem=E.sems["spd"]: e.wait_ge(sem, n))
            E.ops["gpsimd"].append(lambda e, n=E.count["gpd"], sem=E.sems["gpd"]: e.wait_ge(sem, n))
            E.finalize()
            return nc
        tfr = dve(lambda e: e.tensor_copy(
            out=fcrow_s[0:1, 0:32].rearrange("p (c u e) -> p c u e", c=2, u=2),
            in_=nlcc_s[0:1, 0:32].rearrange("p (c e u) -> p c u e", c=2, e=8)),
            waits=[("ac", tfce)])

        if PHASE == "fcx1" and pr == NPAIR_RUN - 1:
            E.ops["sync"].append(lambda e, n=E.count["spd"], sem=E.sems["spd"]: e.wait_ge(sem, n))
            E.ops["gpsimd"].append(lambda e, n=E.count["gpd"], sem=E.sems["gpd"]: e.wait_ge(sem, n))
            E.finalize()
            return nc
        wf = BK["p2"].for_write() + [("dv", tfr)]
        mm(ones_s[0:1, 0:64], fcrow_s[0:1, 0:8], p2[0:64, 96:104], True, True,
           waits=wf)
        mm(ones_s[0:1, 0:64], fcrow_s[0:1, 8:16], p2[64:128, 96:104], True, True,
           tp=(0, 64))

        if PHASE == "fcx2" and pr == NPAIR_RUN - 1:
            E.ops["sync"].append(lambda e, n=E.count["spd"], sem=E.sems["spd"]: e.wait_ge(sem, n))
            E.ops["gpsimd"].append(lambda e, n=E.count["gpd"], sem=E.sems["gpd"]: e.wait_ge(sem, n))
            E.finalize()
            return nc
        mm(ones_s[0:1, 0:64], fcrow_s[0:1, 16:24], p2[0:64, 104:112], True, True)
        fb4 = mm(ones_s[0:1, 0:64], fcrow_s[0:1, 24:32], p2[64:128, 104:112],
                 True, True, tp=(0, 64))
        BK["p2"].wrote("pe", fb4)

        if PHASE == "fcx3" and pr == NPAIR_RUN - 1:
            E.ops["sync"].append(lambda e, n=E.count["spd"], sem=E.sems["spd"]: e.wait_ge(sem, n))
            E.ops["gpsimd"].append(lambda e, n=E.count["gpd"], sem=E.sems["gpd"]: e.wait_ge(sem, n))
            E.finalize()
            return nc
        tfc = dve(lambda e, pr=pr: e.tensor_copy(out=fcx_p[:, pr * 16:pr * 16 + 16],
                                                 in_=p2[:, 96:112]),
                  waits=BK["p2"].for_read())
        BK["p2"].read("dv", tfc)

        if PHASE == "proj" and pr == NPAIR_RUN - 1:
            E.ops["sync"].append(lambda e, n=E.count["spd"], sem=E.sems["spd"]: e.wait_ge(sem, n))
            E.ops["gpsimd"].append(lambda e, n=E.count["gpd"], sem=E.sems["gpd"]: e.wait_ge(sem, n))
            E.finalize()
            return nc
        tkn = dve(lambda e: e.tensor_tensor(out=g16(kn_f), in0=g16(k_f),
                                            in1=bcast(rn16[:], HD), op=ALU.mult),
                  waits=[("ac", te6)])
        dve(lambda e: e.tensor_tensor(out=g16(ktil_tm), in0=g16(kn_f),
                                      in1=bcast(edec_s[:], HD), op=ALU.mult),
            waits=[("ac", te1), ("dv", tkn)])
        dve(lambda e: e.tensor_tensor(out=g16(kg_tm), in0=g16(kn_f),
                                      in1=bcast(rowg_s[:], HD), op=ALU.mult),
            waits=[("ac", te5), ("dv", tkn)])
        tbk = dve(lambda e: e.tensor_tensor(out=Uv3[:, :, 64:128], in0=g16(kn_f),
                                            in1=bcast(bcol_s[:], HD), op=ALU.mult),
                  waits=[("dv", tbneg), ("dv", tkn), ("ac", tv2)])
        dve(lambda e: e.tensor_tensor(out=g16(qm_tm), in0=g16(q_f),
                                      in1=bcast(rowm_s[:], HD), op=ALU.mult),
            waits=[("ac", te2)])
        dve(lambda e: e.tensor_tensor(out=g16(qf_tm), in0=g16(q_f),
                                      in1=bcast(el_s[:], HD), op=ALU.mult),
            waits=[("ac", te3)])
        tscal = E.tok("dv")
        tap("t_kn", kn_f[:, :], [("dv", tkn)])

        if PHASE == "scal" and pr == NPAIR_RUN - 1:
            E.ops["sync"].append(lambda e, n=E.count["spd"], sem=E.sems["spd"]: e.wait_ge(sem, n))
            E.ops["gpsimd"].append(lambda e, n=E.count["gpd"], sem=E.sems["gpd"]: e.wait_ge(sem, n))
            E.finalize()
            return nc

        for src, dstf in [(ktil_tm, ktil_f), (kg_tm, kg_f), (qm_tm, qm_f),
                          (qf_tm, qf_f)]:
            for hh in range(8):
                w = ([("dv", tscal)] + BK["p3"].for_write()) if hh == 0 else ()
                tt = trans(src[:, hh * 128:(hh + 1) * 128],
                           p3[:, hh * 128:(hh + 1) * 128], id_s[:, :], waits=w)
            BK["p3"].wrote("pe", tt)
            tev = act(lambda e, d=dstf: e.activation(out=d[:, :], in_=p3[:, :],
                                                     func=AF.Copy),
                      waits=BK["p3"].for_read())
            BK["p3"].read("ac", tev)
        t_fm = E.tok("ac")

        if PHASE == "tmfm" and pr == NPAIR_RUN - 1:
            E.ops["sync"].append(lambda e, n=E.count["spd"], sem=E.sems["spd"]: e.wait_ge(sem, n))
            E.ops["gpsimd"].append(lambda e, n=E.count["gpd"], sem=E.sems["gpd"]: e.wait_ge(sem, n))
            E.finalize()
            return nc

        def ghm(lf, rf, pm, pname, extra):
            first = True
            for par in range(2):
                for ci in range(2):
                    for h in range(par, NH, 2):
                        w = (extra + BK[pname].for_write()) if first else ()
                        first = False
                        t = mm(lf[rg(h):rg(h) + 64,
                                  hp(h) * 128 + ci * 64: hp(h) * 128 + ci * 64 + 64],
                               rf[rg(h):rg(h) + 64,
                                  hp(h) * 128 + ci * 64: hp(h) * 128 + ci * 64 + 64],
                               pm[ci * 64:ci * 64 + 64, h * 64:h * 64 + 64],
                               True, True, tp=(rg(h), ci * 64), waits=w)
            BK[pname].wrote("pe", t)
            return t

        ghm(ktil_f, kg_f, p4, "p4", [("ac", t_fm)])

        if PHASE == "gmm" and pr == NPAIR_RUN - 1:
            E.ops["sync"].append(lambda e, n=E.count["spd"], sem=E.sems["spd"]: e.wait_ge(sem, n))
            E.ops["gpsimd"].append(lambda e, n=E.count["gpd"], sem=E.sems["gpd"]: e.wait_ge(sem, n))
            E.finalize()
            return nc
        ghm(kg_f, ktil_f, p5, "p5", [])

        if PHASE == "hmm" and pr == NPAIR_RUN - 1:
            E.ops["sync"].append(lambda e, n=E.count["spd"], sem=E.sems["spd"]: e.wait_ge(sem, n))
            E.ops["gpsimd"].append(lambda e, n=E.count["gpd"], sem=E.sems["gpd"]: e.wait_ge(sem, n))
            E.finalize()
            return nc
        tge = dve(lambda e: e.tensor_tensor(
            out=G_sb[0][:, :], in0=p4[:, :],
            in1=mG_s[:, :], op=ALU.mult),
            waits=BK["p4"].for_read())
        BK["p4"].read("dv", tge)
        the = dve(lambda e: e.tensor_tensor(
            out=H_sb[0][:, :], in0=p5[:, :],
            in1=mH_s[:, :], op=ALU.mult),
            waits=BK["p5"].for_read())
        BK["p5"].read("dv", the)

        if PHASE == "gev2" and pr == NPAIR_RUN - 1:
            E.ops["sync"].append(lambda e, n=E.count["spd"], sem=E.sems["spd"]: e.wait_ge(sem, n))
            E.ops["gpsimd"].append(lambda e, n=E.count["gpd"], sem=E.sems["gpd"]: e.wait_ge(sem, n))
            E.finalize()
            return nc
        ghm(ktil_f, kg_f if PHASE == "gev4" else qm_f, p4, "p4", [])

        if PHASE == "gev" and pr == NPAIR_RUN - 1:
            E.ops["sync"].append(lambda e, n=E.count["spd"], sem=E.sems["spd"]: e.wait_ge(sem, n))
            E.ops["gpsimd"].append(lambda e, n=E.count["gpd"], sem=E.sems["gpd"]: e.wait_ge(sem, n))
            E.finalize()
            return nc

        if PHASE in ("gev3", "gev4") and pr == NPAIR_RUN - 1:
            E.ops["sync"].append(lambda e, n=E.count["spd"], sem=E.sems["spd"]: e.wait_ge(sem, n))
            E.ops["gpsimd"].append(lambda e, n=E.count["gpd"], sem=E.sems["gpd"]: e.wait_ge(sem, n))
            E.finalize()
            return nc
        tme = dve(lambda e: e.tensor_tensor(
            out=MT_sb[:, :], in0=p4[:, :],
            in1=mM_s[:, :], op=ALU.mult),
            waits=BK["p4"].for_read())
        BK["p4"].read("dv", tme)
        tap("t_G", G_sb[0][:, :], [("dv", tge)])
        tap("t_MT", MT_sb[:, :], [("dv", tme)])

        if PHASE == "ghm" and pr == NPAIR_RUN - 1:
            E.ops["sync"].append(lambda e, n=E.count["spd"], sem=E.sems["spd"]: e.wait_ge(sem, n))
            E.ops["gpsimd"].append(lambda e, n=E.count["gpd"], sem=E.sems["gpd"]: e.wait_ge(sem, n))
            E.finalize()
            return nc


        cur = 0
        u_dep = [("dv", tge), ("dv", the), ("dv", tbk), ("ac", tv2)]
        for s in range(6):
            first = True
            for ci in range(2):
                for h in range(NH):
                    pm = p4 if h < 8 else p5
                    col = (h % 8) * 128
                    w = ()
                    if first:
                        w = u_dep + BK["p4"].for_write() + BK["p5"].for_write()
                        first = False
                    t = mm(G_sb[cur][ci * 64:ci * 64 + 64, h * 64:h * 64 + 64],
                           U_s[ci * 64:ci * 64 + 64, h * 128:h * 128 + 128],
                           pm[ci * 64:ci * 64 + 64, col:col + 128],
                           True, True, tp=(ci * 64, ci * 64), waits=w)
            BK["p4"].wrote("pe", t)
            BK["p5"].wrote("pe", t)
            ua = dve(lambda e: e.tensor_tensor(out=U_s[:, 0:1024], in0=p4[:, :],
                                               in1=U_s[:, 0:1024], op=ALU.add),
                     waits=BK["p4"].for_read())
            BK["p4"].read("dv", ua)
            ub = dve(lambda e: e.tensor_tensor(out=U_s[:, 1024:2048], in0=p5[:, :],
                                               in1=U_s[:, 1024:2048], op=ALU.add),
                     waits=BK["p5"].for_read())
            BK["p5"].read("dv", ub)
            u_dep = [("dv", ub)]
            if s < 5:
                first = True
                for ci in range(2):
                    for h in range(NH):
                        w = ()
                        if first:
                            w = [("dv", ua), ("dv", ub)] + BK["p4"].for_write()
                            first = False
                        t = mm(H_sb[cur][ci * 64:ci * 64 + 64, h * 64:h * 64 + 64],
                               G_sb[cur][ci * 64:ci * 64 + 64, h * 64:h * 64 + 64],
                               p4[ci * 64:ci * 64 + 64, h * 64:h * 64 + 64],
                               True, True, tp=(ci * 64, ci * 64), waits=w)
                BK["p4"].wrote("pe", t)
                tgs = act(lambda e, nx=1 - cur: e.activation(out=G_sb[nx][:, :],
                                                             in_=p4[:, :], func=AF.Copy),
                          waits=BK["p4"].for_read())
                BK["p4"].read("ac", tgs)
                if s < 4:
                    first = True
                    for ci in range(2):
                        for h in range(NH):
                            w = ()
                            if first:
                                w = BK["p5"].for_write()
                                first = False
                            t = mm(G_sb[cur][ci * 64:ci * 64 + 64, h * 64:h * 64 + 64],
                                   H_sb[cur][ci * 64:ci * 64 + 64, h * 64:h * 64 + 64],
                                   p5[ci * 64:ci * 64 + 64, h * 64:h * 64 + 64],
                                   True, True, tp=(ci * 64, ci * 64), waits=w)
                    BK["p5"].wrote("pe", t)
                    ths = act(lambda e, nx=1 - cur: e.activation(out=H_sb[nx][:, :],
                                                                 in_=p5[:, :],
                                                                 func=AF.Copy),
                              waits=BK["p5"].for_read())
                    BK["p5"].read("ac", ths)
                cur = 1 - cur
        u_done = u_dep[0][1]
        tap("t_U", U_s[:, :], [("dv", u_done)])

        if PHASE == "solve" and pr == NPAIR_RUN - 1:
            E.ops["sync"].append(lambda e, n=E.count["spd"], sem=E.sems["spd"]: e.wait_ge(sem, n))
            E.ops["gpsimd"].append(lambda e, n=E.count["gpd"], sem=E.sems["gpd"]: e.wait_ge(sem, n))
            E.finalize()
            return nc


        def bak(lsl, rsl, pm, pname, extra):
            first = True
            for ci in range(2):
                for h in range(NH):
                    w = (extra + BK[pname].for_write()) if first else ()
                    first = False
                    t = mm(lsl(ci, h), rsl(ci, h),
                           pm[rg(h):rg(h) + 64,
                              hp(h) * 128 + ci * 64: hp(h) * 128 + ci * 64 + 64],
                           True, True, tp=(ci * 64, rg(h)), waits=w)
            BK[pname].wrote("pe", t)
            return t

        bak(lambda ci, h: ktil_tm[ci * 64:ci * 64 + 64, h * 64:h * 64 + 64],
            lambda ci, h: U_s[ci * 64:ci * 64 + 64, h * 128:h * 128 + 64],
            p4, "p4", [("dv", u_done)])
        tbe = act(lambda e: e.activation(out=B_sb[:, :], in_=p4[:, :], func=AF.Copy),
                  waits=BK["p4"].for_read())
        BK["p4"].read("ac", tbe)
        bak(lambda ci, h: U_s[ci * 64:ci * 64 + 64, h * 128 + 64:h * 128 + 128],
            lambda ci, h: ktil_tm[ci * 64:ci * 64 + 64, h * 64:h * 64 + 64],
            p5, "p5", [])
        tae = act(lambda e: e.activation(out=AT_sb[:, :], in_=p5[:, :], func=AF.Copy),
                  waits=BK["p5"].for_read() + [("spd", E.tok("spd"))])
        BK["p5"].read("ac", tae)
        for ci in range(2):
            dma("spd", d_A[2 * pr + ci, :, :].rearrange("p (g i) -> p g i", g=8),
                AT_sb[:].rearrange("p (g c i) -> p g c i", g=8, c=2)[:, :, ci, :],
                waits=[("ac", tae)])
        bak(lambda ci, h: U_s[ci * 64:ci * 64 + 64, h * 128 + 64:h * 128 + 128],
            lambda ci, h: MT_sb[ci * 64:ci * 64 + 64, h * 64:h * 64 + 64],
            p4, "p4", [("dv", tme)])
        tqe = dve(lambda e: e.tensor_tensor(out=Qt_sb[:, :], in0=p4[:, :],
                                            in1=qf_f[:, :], op=ALU.add),
                  waits=BK["p4"].for_read() + [("ac", t_fm), ("spd", E.tok("spd"))])
        BK["p4"].read("dv", tqe)
        for ci in range(2):
            dma("spd", d_Qt[2 * pr + ci, :, :].rearrange("p (g i) -> p g i", g=8),
                Qt_sb[:].rearrange("p (g c i) -> p g c i", g=8, c=2)[:, :, ci, :],
                waits=[("dv", tqe)])
        tap("t_AT", AT_sb[:, :], [("ac", tae)])
        tap("t_B", B_sb[:, :], [("ac", tbe)])
        tap("t_Qt", Qt_sb[:, :], [("dv", tqe)])

        if PHASE == "bak" and pr == NPAIR_RUN - 1:
            E.ops["sync"].append(lambda e, n=E.count["spd"], sem=E.sems["spd"]: e.wait_ge(sem, n))
            E.ops["gpsimd"].append(lambda e, n=E.count["gpd"], sem=E.sems["gpd"]: e.wait_ge(sem, n))
            E.finalize()
            return nc


        for ci in range(2):
            c = 2 * pr + ci
            Scur, Snxt = S_ab[c % 2], S_ab[(c + 1) % 2]
            Pcur, Pnxt = P_ab[c % 2], P_ab[(c + 1) % 2]
            wS = [("dv", s_evac), ("ac", p_evac), ("dv", tfc)]
            fsl = fcx_p[:, c * 8:(c + 1) * 8]
            tSF = dve(lambda e, S=Scur, f=fsl: e.tensor_tensor(
                out=g8(SF_t), in0=g8(S), in1=bcast(f, 64), op=ALU.mult), waits=wS)
            tPF = dve(lambda e, P=Pcur, f=fsl: e.tensor_tensor(
                out=g8(PF_t), in0=g8(P), in1=bcast(f, 64), op=ALU.mult))
            first = True
            for h in range(NH):
                a_sl = AT_sb[rg(h):rg(h) + 64,
                             hp(h) * 128 + ci * 64: hp(h) * 128 + ci * 64 + 64]
                w = ([("dv", tPF), ("ac", tae)] + BK["p2"].for_write() +
                     BK["p0"].for_write()) if first else ()
                first = False
                mm(a_sl, Scur[rg(h):rg(h) + 64, hp(h) * 64: hp(h) * 64 + 64],
                   p2[rg(h):rg(h) + 64, hp(h) * 64: hp(h) * 64 + 64],
                   True, False, tp=(rg(h), rg(h)), waits=w)
                mm(id_s[rg(h):rg(h) + 64, rg(h):rg(h) + 64],
                   SF_t[rg(h):rg(h) + 64, hp(h) * 64: hp(h) * 64 + 64],
                   p2[rg(h):rg(h) + 64, hp(h) * 64: hp(h) * 64 + 64],
                   False, True, tp=(rg(h), rg(h)))
                mm(a_sl, Pcur[rg(h):rg(h) + 64, hp(h) * 64: hp(h) * 64 + 64],
                   p0[rg(h):rg(h) + 64, hp(h) * 64: hp(h) * 64 + 64],
                   True, False, tp=(rg(h), rg(h)))
                tpl = mm(id_s[rg(h):rg(h) + 64, rg(h):rg(h) + 64],
                         PF_t[rg(h):rg(h) + 64, hp(h) * 64: hp(h) * 64 + 64],
                         p0[rg(h):rg(h) + 64, hp(h) * 64: hp(h) * 64 + 64],
                         False, True, tp=(rg(h), rg(h)))
            BK["p2"].wrote("pe", tpl)
            BK["p0"].wrote("pe", tpl)
            s_evac = dve(lambda e, S=Snxt, ci=ci: e.tensor_tensor(
                out=g8(S), in0=g8(p2),
                in1=B_sb[:].rearrange("p (g c i) -> p g c i", g=8, c=2)[:, :, ci, :],
                op=ALU.add), waits=BK["p2"].for_read() + [("ac", tbe)])
            BK["p2"].read("dv", s_evac)
            p_evac = act(lambda e, P=Pnxt: e.activation(out=P[:, :], in_=p0[:, :],
                                                        func=AF.Copy),
                         waits=BK["p0"].for_read())
            BK["p0"].read("ac", p_evac)
            first = True
            for h in range(NH):
                w = ([("dv", tqe)] + BK["p5"].for_write()) if first else ()
                first = False
                mm(Scur[rg(h):rg(h) + 64, hp(h) * 64: hp(h) * 64 + 64],
                   Qt_sb[rg(h):rg(h) + 64,
                         hp(h) * 128 + ci * 64: hp(h) * 128 + ci * 64 + 64],
                   p5[rg(h):rg(h) + 64, hp(h) * 64: hp(h) * 64 + 64],
                   True, True, tp=(rg(h), rg(h)), waits=w)
            for h in range(NH):
                to2 = mm(U_s[ci * 64:ci * 64 + 64, h * 128:h * 128 + 64],
                         MT_sb[ci * 64:ci * 64 + 64, h * 64:h * 64 + 64],
                         p5[rg(h):rg(h) + 64,
                            512 + hp(h) * 64: 512 + hp(h) * 64 + 64],
                         True, True, tp=(ci * 64, rg(h)))
            BK["p5"].wrote("pe", to2)
            to2e = act(lambda e: e.activation(out=o2_t[:, :], in_=p5[:, 512:1024],
                                              func=AF.Copy),
                       waits=BK["p5"].for_read())
            toe = dve(lambda e: e.tensor_tensor(out=ostg[:, :], in0=p5[:, 0:512],
                                                in1=o2_t[:, :], op=ALU.add),
                      waits=BK["p5"].for_read() + [("ac", to2e), ("spd", E.tok("spd"))])
            BK["p5"].read("dv", toe)
            tos = dma("spd", d_out[c, :, :], ostg[:, :], waits=[("dv", toe)])
            g_spd = tos
        tap("t_S", S_ab[NCH % 2][:, :] if pr == NPAIR - 1 else S_ab[0][:, :],
            [("dv", s_evac)] if pr == NPAIR - 1 else [("dv", s_evac)])

        if pr + 2 < NPAIR:
            xw_tok[pr % 2] = dma(
                f"gpd{pr % 2}", xwb_[pr % 2][:].rearrange("p (kc t) -> p kc t", kc=NKC),
                xT.rearrange("(kc p) t -> p kc t", p=128)
                [:, :, (pr + 2) * 128:(pr + 3) * 128],
                waits=[("pe", E.tok("pe"))])
        g_pe, g_dv, g_ac = E.tok("pe"), E.tok("dv"), E.tok("ac")

    tap("t_fcx", fcx_p[:, :], [("ac", g_ac)])

    if PHASE == "pairs":
        E.ops["sync"].append(lambda e, n=E.count["spd"], sem=E.sems["spd"]: e.wait_ge(sem, n))
        E.ops["gpsimd"].append(lambda e, n=E.count["gpd"], sem=E.sems["gpd"]: e.wait_ge(sem, n))
        E.finalize()
        return nc

    Pfin = P_ab[NCH % 2]
    first = True
    for h in range(NH):
        w = ([("ac", p_evac)] + BK["p3"].for_write()) if first else ()
        first = False
        tl = trans(Pfin[rg(h):rg(h) + 64, hp(h) * 64: hp(h) * 64 + 64],
                   p3[rg(h):rg(h) + 64, hp(h) * 64: hp(h) * 64 + 64],
                   id_s[rg(h):rg(h) + 64, rg(h):rg(h) + 64],
                   tp=(rg(h), rg(h)), waits=w)
    BK["p3"].wrote("pe", tl)
    tpt = act(lambda e: e.activation(out=ptseg[:, :], in_=p3[:, 0:512], func=AF.Copy),
              waits=BK["p3"].for_read())
    BK["p3"].read("ac", tpt)
    dma("spd", ccin[:, 0:512], ptseg[:, :], waits=[("ac", tpt)])
    dma("spd", ccin[:, 512:1024], S_ab[NCH % 2][:, :], waits=[("dv", s_evac)])
    cc_w = E.tok("spd")
    tcc = gpe(lambda e: e.collective_compute(
        "AllGather", ALU.bypass, replica_groups=[[0, 1, 2, 3], [4, 5, 6, 7]],
        ins=[ccin[:, :]], outs=[ccout[:, :]]), waits=[("spd", cc_w)])
    tgl = dma("spd", gat_s[:].rearrange("p (r c) -> p r c", r=4),
              ccout.rearrange("(r p) c -> p r c", p=128),
              waits=[("gp", tcc), ("cst", cast_done)])
    tz = dve(lambda e: e.memset(rpj[0][:, :], 0.0))
    last_dv = tz
    for j in range(3):
        first = True
        for h in range(NH):
            w = ([("spd", tgl), ("dv", last_dv)] + BK["p2"].for_write()) if first else ()
            first = False
            tcj = mm(gat_s[rg(h):rg(h) + 64,
                           j * 1024 + hp(h) * 64: j * 1024 + hp(h) * 64 + 64],
                     rpj[j][rg(h):rg(h) + 64, hp(h) * 64: hp(h) * 64 + 64],
                     p2[rg(h):rg(h) + 64, hp(h) * 64: hp(h) * 64 + 64],
                     True, True, tp=(rg(h), rg(h)), waits=w)
        BK["p2"].wrote("pe", tcj)
        last_dv = dve(lambda e, j=j: e.tensor_tensor(
            out=rpj[j + 1][:, :], in0=p2[:, :],
            in1=gat_s[:, j * 1024 + 512: j * 1024 + 1024], op=ALU.add),
            waits=BK["p2"].for_read() + [("spd", tgl)])
        BK["p2"].read("dv", last_dv)
    dve(lambda e: e.tensor_scalar(out=sin_s[:, :], in0=rpj[1][:, :],
                                  scalar1=sel_s[:, 1:2], scalar2=None, op0=ALU.mult),
        waits=[("dv", last_dv), ("spd", ld_c)])
    dve(lambda e: e.scalar_tensor_tensor(out=sin_s[:, :], in0=rpj[2][:, :],
                                         scalar=sel_s[:, 2:3], in1=sin_s[:, :],
                                         op0=ALU.mult, op1=ALU.add),
        waits=[("dv", E.tok("dv"))])
    dve(lambda e: e.scalar_tensor_tensor(out=sin_s[:, :], in0=rpj[3][:, :],
                                         scalar=sel_s[:, 3:4], in1=sin_s[:, :],
                                         op0=ALU.mult, op1=ALU.add),
        waits=[("dv", E.tok("dv"))])
    tsin = dve(lambda e: e.tensor_copy(out=dlt[0][:, :], in_=sin_s[:, :]),
               waits=[("dv", E.tok("dv"))])
    tap("t_sin", sin_s[:, :], [("dv", tsin)])

    if PHASE == "gather":
        E.ops["sync"].append(lambda e, n=E.count["spd"], sem=E.sems["spd"]: e.wait_ge(sem, n))
        E.ops["gpsimd"].append(lambda e, n=E.count["gpd"], sem=E.sems["gpd"]: e.wait_ge(sem, n))
        E.finalize()
        return nc

    qa_tok = [0, 0]
    dl_ev = tsin
    for c in range(2):
        dma(f"spd{c}", qtc[c][:, :], d_Qt[c, :, :], waits=[("spd", E.tok("spd"))])
        dma(f"spd{c}", atc[c][:, :], d_A[c, :, :])
        qa_tok[c] = dma(f"spd{c}", och[c][:, :], d_out[c, :, :])
    for c in range(NCH):
        ri = c % 2
        first = True
        for h in range(NH):
            w = ([(f"spd{ri}", qa_tok[ri]), ("dv", dl_ev if c == 0 else 0),
                  ("ac", 0 if c == 0 else dl_ev)] + BK["p5"].for_write()) if first else ()
            first = False
            tocm = mm(dlt[c % 2][rg(h):rg(h) + 64, hp(h) * 64: hp(h) * 64 + 64],
                      qtc[ri][rg(h):rg(h) + 64, hp(h) * 64: hp(h) * 64 + 64],
                      p5[rg(h):rg(h) + 64, hp(h) * 64: hp(h) * 64 + 64],
                      True, True, tp=(rg(h), rg(h)), waits=w)
        BK["p5"].wrote("pe", tocm)
        tca = dve(lambda e, ri=ri: e.tensor_tensor(out=och[ri][:, :], in0=p5[:, 0:512],
                                                   in1=och[ri][:, :], op=ALU.add),
                  waits=BK["p5"].for_read())
        BK["p5"].read("dv", tca)
        tcs = dma(f"spd{ri}", d_out[c, :, :], och[ri][:, :], waits=[("dv", tca)])
        tdf = dve(lambda e, c=c: e.tensor_tensor(
            out=g8(PF_t), in0=g8(dlt[c % 2]),
            in1=bcast(fcx_p[:, c * 8:(c + 1) * 8], 64), op=ALU.mult),
            waits=[("dv", tsin)] if c == 0 else [("ac", dl_ev), ("dv", tca)])
        first = True
        for h in range(NH):
            w = ([("dv", tdf)] + BK["p2"].for_write()) if first else ()
            first = False
            mm(atc[ri][rg(h):rg(h) + 64, hp(h) * 64: hp(h) * 64 + 64],
               dlt[c % 2][rg(h):rg(h) + 64, hp(h) * 64: hp(h) * 64 + 64],
               p2[rg(h):rg(h) + 64, hp(h) * 64: hp(h) * 64 + 64],
               True, False, tp=(rg(h), rg(h)), waits=w)
            tdm = mm(id_s[rg(h):rg(h) + 64, rg(h):rg(h) + 64],
                     PF_t[rg(h):rg(h) + 64, hp(h) * 64: hp(h) * 64 + 64],
                     p2[rg(h):rg(h) + 64, hp(h) * 64: hp(h) * 64 + 64],
                     False, True, tp=(rg(h), rg(h)))
        BK["p2"].wrote("pe", tdm)
        dl_ev = act(lambda e, nx=(c + 1) % 2: e.activation(out=dlt[nx][:, :],
                                                           in_=p2[:, :], func=AF.Copy),
                    waits=BK["p2"].for_read())
        BK["p2"].read("ac", dl_ev)
        if c + 2 < NCH:
            dma(f"spd{ri}", qtc[ri][:, :], d_Qt[c + 2, :, :],
                waits=[("pe", tocm), ("dv", tca), ("ac", dl_ev),
                       (f"spd{ri}", tcs)])
            dma(f"spd{ri}", atc[ri][:, :], d_A[c + 2, :, :])
            qa_tok[ri] = dma(f"spd{ri}", och[ri][:, :], d_out[c + 2, :, :])
    corr_dma = E.tok("spd")
    corr_dma01 = (E.tok("spd0"), E.tok("spd1"))
    if PHASE == "corr":
        E.ops["sync"].append(lambda e, n=E.count["spd"], sem=E.sems["spd"]: e.wait_ge(sem, n))
        E.ops["gpsimd"].append(lambda e, n=E.count["gpd"], sem=E.sems["gpd"]: e.wait_ge(sem, n))
        E.finalize()
        return nc
    ph1 = [("pe", E.tok("pe")), ("dv", E.tok("dv")), ("ac", E.tok("ac")),
           ("spd", E.tok("spd")), ("gpd", E.tok("gpd")), ("gp", E.tok("gp"))]

    def ln_stats(buf, n, sq_ap, apply_list, wtok_pe, prev_reads):
        first = True
        for kc in range(n):
            w = (prev_reads + BK["p0"].for_write()) if first else ()
            first = False
            t1 = mm(ones_s[:, 0:1], buf[:, kc * 512:(kc + 1) * 512],
                    p0[0:1, 0:512], kc == 0, kc == n - 1, waits=w)
        BK["p0"].wrote("pe", t1)
        first = True
        for kc in range(n):
            w = (BK["p1"].for_write() + [("ac", sq_ap)]) if first else ()
            first = False
            t2 = mm(ones_s[:, 0:1], sq_w[:, kc * 512:(kc + 1) * 512],
                    p1[0:1, 0:512], kc == 0, kc == n - 1, waits=w)
        BK["p1"].wrote("pe", t2)
        tmu = dve(lambda e: e.tensor_scalar(out=row_t[0:1, 0:512], in0=p0[0:1, :],
                                            scalar1=1.0 / DM, scalar2=None,
                                            op0=ALU.mult), waits=BK["p0"].for_read())
        BK["p0"].read("dv", tmu)
        tv_ = dve(lambda e: e.tensor_scalar(out=row_t[0:1, 512:1024], in0=p1[0:1, :],
                                            scalar1=1.0 / DM, scalar2=None,
                                            op0=ALU.mult), waits=BK["p1"].for_read())
        BK["p1"].read("dv", tv_)
        t_r = dve(lambda e: e.tensor_tensor(out=row_t[0:1, 1024:1536],
                                            in0=row_t[0:1, 0:512], in1=row_t[0:1, 0:512],
                                            op=ALU.mult), waits=[("dv", tmu)])
        t_r = dve(lambda e: e.tensor_tensor(out=row_t[0:1, 512:1024],
                                            in0=row_t[0:1, 512:1024],
                                            in1=row_t[0:1, 1024:1536], op=ALU.subtract),
                  waits=[("dv", t_r)])
        tva = dve(lambda e: e.tensor_scalar(out=row_t[0:1, 512:1024],
                                            in0=row_t[0:1, 512:1024], scalar1=1e-5,
                                            scalar2=None, op0=ALU.add),
                  waits=[("dv", t_r)])
        tsd = act(lambda e: e.activation(out=row_t[0:1, 1024:1536],
                                         in_=row_t[0:1, 512:1024], func=AF.Sqrt),
                  waits=[("dv", tva)])
        tiv = dve(lambda e: e.reciprocal(out=row_t[0:1, 512:1024],
                                         in_=row_t[0:1, 1024:1536]),
                  waits=[("ac", tsd)])
        wb = BK["p4"].for_write() + [("dv", tiv)]
        mm(ones_s[0:1, :], row_t[0:1, 0:512], p4[:, 0:512], True, True, waits=wb)
        tb2 = mm(ones_s[0:1, :], row_t[0:1, 512:1024], p4[:, 512:1024], True, True)
        BK["p4"].wrote("pe", tb2)
        return tb2

    x1_dma = []
    for w in range(NWIN):
        t0 = w * 512
        tx = dma("spd", xf_w[:].rearrange("p (kc t) -> p kc t", kc=NKC),
                 xT.rearrange("(kc p) t -> p kc t", p=128)[:, :, t0:t0 + 512],
                 waits=(ph1 + [("spd0", corr_dma01[0]), ("spd1", corr_dma01[1])])
                 if w == 0 else [("spd", corr_dma)])
        tow = dma("spd", outw[:].rearrange("p (kc c i) -> p kc c i", kc=NKC, c=8),
                  d_out[w * 8:(w + 1) * 8, :, :]
                  .rearrange("c p (kc i) -> p kc c i", kc=8),
                  waits=[("spd", corr_dma)])
        first = True
        for i in range(4):
            wt = ((ph1 if w == 0 else []) + BK["p3"].for_write()) if first else ()
            first = False
            tg = trans(g1_tm[:, (w * 4 + i) * 64:(w * 4 + i + 1) * 64],
                       p3[0:64, i * 128:(i + 1) * 128], id_s[:, :], waits=wt)
        BK["p3"].wrote("pe", tg)
        tgf = act(lambda e: e.activation(out=g1f_w[:, :], in_=p3[0:64, 0:512],
                                         func=AF.Copy), waits=BK["p3"].for_read())
        BK["p3"].read("ac", tgf)
        for kc in range(NKC):
            pm, bk = (p0, "p0") if kc % 2 == 0 else (p1, "p1")
            wt = [("ac", tgf)] + BK[bk].for_write()
            tgm = mm(wg2_s[:, kc * 128:(kc + 1) * 128], g1f_w[:, :], pm[:, :],
                     True, True, waits=wt)
            BK[bk].wrote("pe", tgm)
            tsg = act(lambda e, kc=kc, pm=pm: e.activation(
                out=gate_w[:, kc * 512:(kc + 1) * 512], in_=pm[:, :],
                func=AF.Sigmoid), waits=BK[bk].for_read())
            BK[bk].read("ac", tsg)
        tgate = E.tok("ac")
        tap("t_gate", gate_w[:, :], [("ac", tgate)]) if w == 0 else None
        tog = dve(lambda e: e.tensor_tensor(out=og_w[:, :], in0=outw[:, :],
                                            in1=gate_w[:, :], op=ALU.mult),
                  waits=[("ac", tgate), ("spd", tow)])
        tsq = act(lambda e: e.activation(out=sq_w[:, :], in_=og_w[:, :],
                                         func=AF.Square), waits=[("dv", tog)])
        tb2 = ln_stats(og_w, 8, tsq, None, None, [("dv", tog)])
        for kc in range(NKC):
            t_ = dve(lambda e, kc=kc: e.tensor_tensor(
                out=og_w[:, kc * 512:(kc + 1) * 512],
                in0=og_w[:, kc * 512:(kc + 1) * 512], in1=p4[:, 0:512],
                op=ALU.subtract),
                waits=(BK["p4"].for_read() + [("ac", tsq)]) if kc == 0 else ())
            t_ = dve(lambda e, kc=kc: e.scalar_tensor_tensor(
                out=on_w[:, kc * 512:(kc + 1) * 512],
                in0=og_w[:, kc * 512:(kc + 1) * 512],
                scalar=normw_s[:, kc:kc + 1], in1=p4[:, 512:1024],
                op0=ALU.mult, op1=ALU.mult), waits=[("dv", t_)])
        BK["p4"].read("dv", t_)
        ton = t_
        for m in range(NKC):
            pm, bk = (p2, "p2") if m % 2 == 0 else (p5, "p5")
            pslice = pm[:, 0:512]
            first = True
            for kc in range(NKC):
                wt = ([("dv", ton), ("spd", tow)] + BK[bk].for_write()) if first else ()
                first = False
                twm = mm(wo_s[:, kc * DM + m * 128: kc * DM + m * 128 + 128],
                         on_w[:, kc * 512:(kc + 1) * 512], pslice,
                         kc == 0, kc == NKC - 1, waits=wt)
            BK[bk].wrote("pe", twm)
            tr1 = dve(lambda e, m=m, pslice=pslice: e.tensor_tensor(
                out=x1f_w[:, m * 512:(m + 1) * 512], in0=pslice,
                in1=xf_w[:, m * 512:(m + 1) * 512], op=ALU.add),
                waits=BK[bk].for_read())
            BK[bk].read("dv", tr1)
        tsq2 = act(lambda e: e.activation(out=sq_w[:, :], in_=x1f_w[:, :],
                                          func=AF.Square), waits=[("dv", tr1)])
        tb2 = ln_stats(x1f_w, 8, tsq2, None, None, [("dv", tr1)])
        for kc in range(NKC):
            t_ = dve(lambda e, kc=kc: e.tensor_tensor(
                out=x1f_w[:, kc * 512:(kc + 1) * 512],
                in0=x1f_w[:, kc * 512:(kc + 1) * 512], in1=p4[:, 0:512],
                op=ALU.subtract),
                waits=(BK["p4"].for_read() + [("ac", tsq2)]) if kc == 0 else ())
            t_ = dve(lambda e, kc=kc: e.tensor_tensor(
                out=x1f_w[:, kc * 512:(kc + 1) * 512],
                in0=x1f_w[:, kc * 512:(kc + 1) * 512], in1=p4[:, 512:1024],
                op=ALU.mult), waits=[("dv", t_)])
        BK["p4"].read("dv", t_)
        tx1 = t_
        tx1b = act(lambda e: e.activation(out=x1b_w[:, :], in_=x1f_w[:, :],
                                          func=AF.Copy), waits=[("dv", tx1)])
        tap("t_x1", x1f_w[:, 0:4096], [("dv", tx1)]) if w == 0 else None
        d1 = dma("spd", d_x1f[:, :, t0:t0 + 512].rearrange("kc p t -> p kc t"),
                 x1f_w[:].rearrange("p (kc t) -> p kc t", kc=NKC),
                 waits=[("dv", tx1)])
        d2 = dma("spd", d_x1b[:, :, t0:t0 + 512].rearrange("kc p t -> p kc t"),
                 x1b_w[:].rearrange("p (kc t) -> p kc t", kc=NKC),
                 waits=[("ac", tx1b)])
        x1_dma.append(d2)
        g_ac = E.tok("ac")
    tail_dma = E.tok("spd")
    if PHASE == "tail":
        E.ops["sync"].append(lambda e, n=E.count["spd"], sem=E.sems["spd"]: e.wait_ge(sem, n))
        E.ops["gpsimd"].append(lambda e, n=E.count["gpd"], sem=E.sems["gpd"]: e.wait_ge(sem, n))
        E.finalize()
        return nc

    ph2 = [("pe", E.tok("pe")), ("dv", E.tok("dv")), ("ac", E.tok("ac")),
           ("spd", E.tok("spd")), ("gpd", E.tok("gpd"))]
    y_dma = []
    y_w = sq2_w
    g_pe2, g_dv2, g_ac2 = E.tok("pe"), E.tok("dv"), E.tok("ac")
    for wi in range(4):
        woff = wi * 512
        txb = dma("spd", x1w_b[:].rearrange("p (kc t) -> p kc t", kc=NKC),
                  d_x1b[:, :, woff:woff + 512].rearrange("kc p t -> p kc t"),
                  waits=(ph2 if wi == 0 else [("pe", g_pe2), ("dv", g_dv2),
                                              ("ac", g_ac2)]))
        dmw = [0, 0]

        def l1_load(g, mg2, w0):
            dma(f"gpd{g}", hstr[g][:, 0:2048].rearrange("p (kc c) -> p kc c", kc=8),
                d_w1b.rearrange("(kc p) c -> p kc c", p=128)
                [:, :, mg2 * 256:(mg2 + 1) * 256], waits=w0)
            tok = dma(f"gpd{g}", hstr[g][:, 2048:4096].rearrange("p (kc c) -> p kc c", kc=8),
                      d_w2b.rearrange("(kc p) c -> p kc c", p=128)
                      [:, :, mg2 * 256:(mg2 + 1) * 256])
            return tok

        for g in range(2):
            dmw[g] = l1_load(g, g, ((ph2 if wi == 0 and g == 0 else [("gpd", ld_w)]) +
                                    [("cst", cast_done),
                                     (f"gpd{g}", E.tok(f"gpd{g}")), ("pe", E.tok("pe"))]))
        prev_tac = 0
        for mg in range(16):
            ri = mg % 2
            for mi in range(2):
                m = mg * 2 + mi
                pm1, bk1 = (p4, "p4") if m % 2 == 0 else (p5, "p5")
                first = True
                for kc in range(NKC):
                    wt = ([(f"gpd{ri}", dmw[ri]), ("spd", txb)] +
                          BK[bk1].for_write()) if first else ()
                    first = False
                    mm(hstr[ri][:, kc * 256 + mi * 128: kc * 256 + mi * 128 + 128],
                       x1w_b[:, kc * 512:(kc + 1) * 512],
                       pm1[:, 0:512], kc == 0, kc == NKC - 1, waits=wt)
                    t1 = mm(hstr[ri][:, 2048 + kc * 256 + mi * 128:
                                     2048 + kc * 256 + mi * 128 + 128],
                            x1w_b[:, kc * 512:(kc + 1) * 512],
                            pm1[:, 512:1024], kc == 0, kc == NKC - 1)
                BK[bk1].wrote("pe", t1)
                tsl = act(lambda e, m=m, pm1=pm1: e.activation(
                    out=silu_t[:, :], in_=pm1[:, 0:512], func=AF.Silu,
                    scale=scl_s[:, m:m + 1]),
                    waits=BK[bk1].for_read() + [("dv", prev_tac)])
                tac = dve(lambda e, m=m, pm1=pm1: e.scalar_tensor_tensor(
                    out=a_sb[:, m * 512:(m + 1) * 512], in0=pm1[:, 512:1024],
                    scalar=scl_s[:, 32 + m:33 + m], in1=silu_t[:, :],
                    op0=ALU.mult, op1=ALU.mult), waits=[("ac", tsl)])
                BK[bk1].read("dv", tac)
                prev_tac = tac
            if mg + 2 < 16:
                dmw[ri] = l1_load(ri, mg + 2,
                                  [("pe", E.tok("pe")), ("dv", E.tok("dv")),
                                   (f"gpd{ri}", E.tok(f"gpd{ri}"))])
        a_done = E.tok("dv")

        def hload(hT, hB, g, mg2, w0):
            return dma(f"gpd{g}", hstr[g][:].rearrange("p (kc c) -> p kc c", kc=32),
                       hB.rearrange("(kc p) c -> p kc c", p=128)
                       [:, :, mg2 * 256:(mg2 + 1) * 256], waits=w0)

        def hlayer(hT, hB, src, dst, scol, dep):
            dmh = [0, 0]
            for g in range(2):
                dmh[g] = hload(hT, hB, g, g,
                               [("pe", E.tok("pe")), ("dv", E.tok("dv")),
                                ("ac", E.tok("ac")),
                                (f"gpd{g}", E.tok(f"gpd{g}"))])
            for mg in range(16):
                ri = mg % 2
                for mi in range(2):
                    m = mg * 2 + mi
                    pm1, bk1 = (p4, "p4") if mi == 0 else (p5, "p5")
                    first = True
                    for kc in range(32):
                        wt = ([(f"gpd{ri}", dmh[ri])] + dep +
                              BK[bk1].for_write()) if first else ()
                        first = False
                        t1 = mm(hstr[ri][:, kc * 256 + mi * 128:
                                         kc * 256 + mi * 128 + 128],
                                src[:, kc * 512:(kc + 1) * 512],
                                pm1[:, 0:512], kc == 0, kc == 31, waits=wt)
                    BK[bk1].wrote("pe", t1)
                    tgl = act(lambda e, pm1=pm1, m=m, scol=scol: e.activation(
                        out=dst[:, m * 512:(m + 1) * 512], in_=pm1[:, 0:512],
                        func=AF.Gelu, scale=scl_s[:, scol + m: scol + m + 1]),
                        waits=BK[bk1].for_read())
                    BK[bk1].read("ac", tgl)
                if mg + 2 < 16:
                    dmh[ri] = hload(hT, hB, ri, mg + 2,
                                    [("pe", E.tok("pe")),
                                     (f"gpd{ri}", E.tok(f"gpd{ri}"))])
            return E.tok("ac")

        h1_done = hlayer(h1T, d_h1b, a_sb, b_sb, 64, [("dv", a_done)])
        h2_done = hlayer(h2T, d_h2b, b_sb, a_sb, 96, [("ac", h1_done)])

        dmh = [0, 0]
        for g in range(2):
            dmh[g] = hload(w3T, d_w3b, g, g,
                           [("pe", E.tok("pe")), ("ac", E.tok("ac")),
                            (f"gpd{g}", E.tok(f"gpd{g}"))])
        x1t = [0, 0]
        for g in range(2):
            x1t[g] = dma(f"spd{g}", x1fr[g][:, :], d_x1f[g, :, woff:woff + 512],
                         waits=[("dv", E.tok("dv")), ("spd", E.tok("spd"))])
        for mg in range(4):
            ri = mg % 2
            for mi in range(2):
                m = mg * 2 + mi
                pm1, bk1 = (p4, "p4") if mi == 0 else (p5, "p5")
                first = True
                for kc in range(32):
                    wt = ([(f"gpd{ri}", dmh[ri]), ("ac", h2_done),
                           (f"spd{m % 2}", x1t[m % 2])] + BK[bk1].for_write()) if first else ()
                    first = False
                    t1 = mm(hstr[ri][:, kc * 256 + mi * 128: kc * 256 + mi * 128 + 128],
                            a_sb[:, kc * 512:(kc + 1) * 512],
                            pm1[:, 0:512], kc == 0, kc == 31, waits=wt)
                BK[bk1].wrote("pe", t1)
                tr2 = dve(lambda e, pm1=pm1, m=m: e.scalar_tensor_tensor(
                    out=r2_w[:, m * 512:(m + 1) * 512], in0=pm1[:, 0:512],
                    scalar=scl_s[:, 128 + m:129 + m], in1=x1fr[m % 2][:, :],
                    op0=ALU.mult, op1=ALU.add), waits=BK[bk1].for_read())
                BK[bk1].read("dv", tr2)
                if m + 2 < 8:
                    x1t[m % 2] = dma(f"spd{m % 2}", x1fr[m % 2][:, :],
                                     d_x1f[m + 2, :, woff:woff + 512],
                                     waits=[("dv", tr2)])
            if mg + 2 < 4:
                dmh[ri] = hload(w3T, d_w3b, ri, mg + 2,
                                [("pe", E.tok("pe")),
                                 (f"gpd{ri}", E.tok(f"gpd{ri}"))])
        r2_done = E.tok("dv")

        tq3 = act(lambda e: e.activation(out=sq2_w[:, :], in_=r2_w[:, :],
                                         func=AF.Square), waits=[("dv", r2_done)])
        first = True
        for kc in range(NKC):
            wt = ([("dv", r2_done)] + BK["p0"].for_write()) if first else ()
            first = False
            t1 = mm(ones_s[:, 0:1], r2_w[:, kc * 512:(kc + 1) * 512],
                    p0[0:1, 0:512], kc == 0, kc == NKC - 1, waits=wt)
        BK["p0"].wrote("pe", t1)
        first = True
        for kc in range(NKC):
            wt = ([("ac", tq3)] + BK["p1"].for_write()) if first else ()
            first = False
            t2 = mm(ones_s[:, 0:1], sq2_w[:, kc * 512:(kc + 1) * 512],
                    p1[0:1, 0:512], kc == 0, kc == NKC - 1, waits=wt)
        BK["p1"].wrote("pe", t2)
        tmu = dve(lambda e: e.tensor_scalar(out=row_t[0:1, 0:512], in0=p0[0:1, :],
                                            scalar1=1.0 / DM, scalar2=None,
                                            op0=ALU.mult), waits=BK["p0"].for_read())
        BK["p0"].read("dv", tmu)
        tvv = dve(lambda e: e.tensor_scalar(out=row_t[0:1, 512:1024], in0=p1[0:1, :],
                                            scalar1=1.0 / DM, scalar2=None,
                                            op0=ALU.mult), waits=BK["p1"].for_read())
        BK["p1"].read("dv", tvv)
        t_r = dve(lambda e: e.tensor_tensor(out=row_t[0:1, 1024:1536], in0=row_t[0:1, 0:512],
                                            in1=row_t[0:1, 0:512], op=ALU.mult),
                  waits=[("dv", tmu)])
        t_r = dve(lambda e: e.tensor_tensor(out=row_t[0:1, 512:1024],
                                            in0=row_t[0:1, 512:1024],
                                            in1=row_t[0:1, 1024:1536], op=ALU.subtract),
                  waits=[("dv", t_r)])
        tva = dve(lambda e: e.tensor_scalar(out=row_t[0:1, 512:1024],
                                            in0=row_t[0:1, 512:1024], scalar1=1e-5,
                                            scalar2=None, op0=ALU.add),
                  waits=[("dv", t_r)])
        tsd = act(lambda e: e.activation(out=row_t[0:1, 1024:1536],
                                         in_=row_t[0:1, 512:1024], func=AF.Sqrt),
                  waits=[("dv", tva)])
        tiv = dve(lambda e: e.reciprocal(out=row_t[0:1, 512:1024],
                                         in_=row_t[0:1, 1024:1536]),
                  waits=[("ac", tsd)])
        wb = BK["p4"].for_write() + [("dv", tiv)]
        mm(ones_s[0:1, :], row_t[0:1, 0:512], p4[:, 0:512], True, True, waits=wb)
        tb2 = mm(ones_s[0:1, :], row_t[0:1, 512:1024], p4[:, 512:1024], True, True)
        BK["p4"].wrote("pe", tb2)
        for kc in range(NKC):
            t_ = dve(lambda e, kc=kc: e.tensor_tensor(
                out=y_w[:, kc * 512:(kc + 1) * 512],
                in0=r2_w[:, kc * 512:(kc + 1) * 512], in1=p4[:, 0:512],
                op=ALU.subtract),
                waits=(BK["p4"].for_read() + [("pe", t2)]) if kc == 0 else ())
            t_ = dve(lambda e, kc=kc: e.tensor_tensor(
                out=y_w[:, kc * 512:(kc + 1) * 512],
                in0=y_w[:, kc * 512:(kc + 1) * 512], in1=p4[:, 512:1024],
                op=ALU.mult), waits=[("dv", t_)])
        BK["p4"].read("dv", t_)
        ty = E.tok("dv")
        y_dma.append(dma("spd",
                         yT.rearrange("(kc p) t -> p kc t", p=128)[:, :, woff:woff + 512],
                         y_w[:].rearrange("p (kc t) -> p kc t", kc=NKC),
                         waits=[("dv", ty)]))
        g_pe2, g_dv2, g_ac2 = E.tok("pe"), E.tok("dv"), E.tok("ac")

    final_spd = E.count["spd"]
    def _final_wait(e, n=final_spd, sem=E.sems["spd"]):
        e.wait_ge(sem, n)
    E.ops["sync"].append(lambda e: _final_wait(e))
    E.finalize()
    return nc


_NC_CACHE = {}


def _get_nc():
    key = ("prog", tuple(DEBUG), PHASE, NPAIR_RUN)
    if key not in _NC_CACHE:
        _NC_CACHE[key] = build_program()
    return _NC_CACHE[key]


def kernel(L0_attn_x, L0_attn_Wq, L0_attn_Wk, L0_attn_Wv, L0_attn_Wgamma,
           L0_attn_Wf, L0_attn_Wg1, L0_attn_Wg2, L0_attn_Wo, L0_attn_norm_w,
           L0_ffn_w1_w, L0_ffn_w1_s, L0_ffn_w2_w, L0_ffn_w2_s, L0_ffn_h1_w,
           L0_ffn_h1_s, L0_ffn_h2_w, L0_ffn_h2_s, L0_ffn_w3_w, L0_ffn_w3_s):
    nc = _get_nc()
    x = np.asarray(L0_attn_x)
    cc = np.ascontiguousarray
    wsm_np = cc(np.concatenate([np.asarray(L0_attn_Wf), np.asarray(L0_attn_Wgamma),
                                np.asarray(L0_attn_Wg1)], axis=1))
    scl_np = np.zeros((128, 136), np.float32)
    scl_np[:, 0:32] = np.asarray(L0_ffn_w1_s).reshape(32, 128).T
    scl_np[:, 32:64] = np.asarray(L0_ffn_w2_s).reshape(32, 128).T
    scl_np[:, 64:96] = np.asarray(L0_ffn_h1_s).reshape(32, 128).T
    scl_np[:, 96:128] = np.asarray(L0_ffn_h2_s).reshape(32, 128).T
    scl_np[:, 128:136] = np.asarray(L0_ffn_w3_s).reshape(8, 128).T
    shared = {
        "wq": cc(np.asarray(L0_attn_Wq)), "wk": cc(np.asarray(L0_attn_Wk)),
        "wv": cc(np.asarray(L0_attn_Wv)), "wo": cc(np.asarray(L0_attn_Wo)),
        "wsm": wsm_np, "wg2": cc(np.asarray(L0_attn_Wg2)),
        "normw": cc(np.asarray(L0_attn_norm_w).reshape(8, 128).T),
        "w1T": cc(np.asarray(L0_ffn_w1_w).T), "w2T": cc(np.asarray(L0_ffn_w2_w).T),
        "h1T": cc(np.asarray(L0_ffn_h1_w).T), "h2T": cc(np.asarray(L0_ffn_h2_w).T),
        "w3T": cc(np.asarray(L0_ffn_w3_w).T), "scl": scl_np,
    }
    in_maps = []
    xTb = [cc(x[b].T) for b in range(2)]
    for core in range(8):
        b, s = core // 4, core % 4
        selv = np.zeros((128, 4), np.float32)
        selv[:, s] = 1.0
        m = dict(shared)
        m["xT"] = cc(xTb[b][:, s * SEG:(s + 1) * SEG])
        m["sel"] = selv
        in_maps.append(m)
    res = run_bass_kernel_spmd(nc, in_maps, core_ids=list(range(8)))
    out = np.zeros((2, 8192, DM), np.float32)
    for core in range(8):
        b, s = core // 4, core % 4
        out[b, s * SEG:(s + 1) * SEG, :] = np.asarray(res.results[core]["yT"]).T
    if DEBUG:
        global LAST_TAPS
        LAST_TAPS = [{k: np.asarray(v) for k, v in res.results[c].items()
                      if k.startswith("t_")} for c in range(8)]
    return out


# version 04
# speedup vs baseline: 1.0261x; 1.0261x over previous
import numpy as np
import ml_dtypes
import concourse.bass as bass
import concourse.mybir as mybir
from concourse.bass_utils import run_bass_kernel_spmd
from concourse.ap import AP

F32 = mybir.dt.float32
BF16 = mybir.dt.bfloat16
BFnp = ml_dtypes.bfloat16
AF = mybir.ActivationFunctionType
ALU = mybir.AluOpType

SEG, C, NH, HD, DM, DH = 2048, 64, 16, 64, 1024, 4096
NCH, NPAIR, NKC, NWIN = 32, 16, 8, 4
FFN_W = (768, 640, 640)
DEBUG = ()
PHASE = "full"
NPAIR_RUN = NPAIR


def bcast(ap, n):
    return AP(ap.tensor, ap.offset, [list(x) for x in ap.ap] + [[0, n]])


def rep(ap, n, m):
    return AP(ap.tensor, ap.offset, [list(ap.ap[0])] + [[0, n]] + [[1, m]])


class Emitter:
    ENGINES = ("sync", "tensor", "vector", "scalar", "gpsimd")

    def __init__(self, nc):
        self.nc = nc
        self.ops = {e: [] for e in self.ENGINES}
        self.sems, self.count = {}, {}
        for name in ("pe", "dv", "ac", "gp", "spd", "gpd", "spd0", "spd1",
                     "gpd0", "gpd1", "cst", "cst2"):
            self.sems[name] = nc.alloc_semaphore("sem_" + name)
            self.count[name] = 0

    def emit(self, eng, fn, waits=(), inc=None, inc_by=1):
        for (s, v) in waits:
            assert v <= self.count[s], f"{eng}: wait {s}>={v} > {self.count[s]}"
        tok = None
        if inc is not None:
            self.count[inc] += inc_by
            tok = self.count[inc]
        wl = [(self.sems[s], v) for (s, v) in waits if v > 0]
        sem = self.sems[inc] if inc is not None else None

        def closure(e, wl=wl, sem=sem, inc_by=inc_by, fn=fn):
            for (sh, v) in wl:
                e.wait_ge(sh, v)
            ins = fn(e)
            if sem is not None:
                ins.then_inc(sem, inc_by)
        self.ops[eng].append(closure)
        return tok

    def tok(self, s):
        return self.count[s]

    def finalize(self):
        with self.nc.Block() as block:
            for eng in self.ENGINES:
                ops = self.ops[eng]
                getattr(block, eng)(lambda e, ops=ops: [f(e) for f in ops])


class Bank:
    def __init__(self):
        self.w, self.r = [], []

    def for_write(self):
        return list(self.r) + list(self.w)

    def for_read(self):
        return list(self.w)

    def wrote(self, sem, val):
        self.w, self.r = [(sem, val)], []

    def read(self, sem, val):
        self.r = [(sem, val)]


class Arena:
    def __init__(self, ap):
        self.ap = ap
        self.off = 0

    def al(self, shape, dt):
        esz = {F32: 4, BF16: 2}[dt]
        size = shape[1] * esz
        sl = self.ap[0:shape[0], self.off:self.off + size].bitcast(dt)
        self.off += (size + 63) // 64 * 64
        return sl


def build_program():
    nc = bass.Bass("TRN2", target_bir_lowering=False, num_devices=8)
    E = Emitter(nc)

    din = lambda n, shp, dt=F32: nc.dram_tensor(n, shp, dt, kind="ExternalInput").ap()
    xT = din("xT", [DM, SEG])
    sel = din("sel", [128, 4])
    wq, wk, wv, wo = din("wq", [DM, DM]), din("wk", [DM, DM]), din("wv", [DM, DM]), din("wo", [DM, DM])
    wsm = din("wsm", [DM, 96])
    wg2 = din("wg2", [64, DM])
    normw = din("normw", [128, 8])
    w1T, w2T = din("w1T", [DM, DH]), din("w2T", [DM, DH])
    h1T, h2T = din("h1T", [DH, DH]), din("h2T", [DH, DH])
    w3T = din("w3T", [DH, DM])
    scl = din("scl", [128, 136])
    yT = nc.dram_tensor("yT", [DM, SEG], F32, kind="ExternalOutput").ap()

    d_A = nc.dram_tensor("d_A", [NCH, 128, 512], BF16, kind="Internal").ap()
    d_Qt = nc.dram_tensor("d_Qt", [NCH, 128, 512], BF16, kind="Internal").ap()
    d_out = nc.dram_tensor("d_out", [NCH, 128, 512], BF16, kind="Internal").ap()
    d_x1f = nc.dram_tensor("d_x1f", [NKC, 128, SEG], F32, kind="Internal").ap()
    d_x1b = nc.dram_tensor("d_x1b", [NKC, 128, SEG], BF16, kind="Internal").ap()
    ccin = nc.dram_tensor("ccin", [128, 1024], BF16, kind="Internal").ap()
    d_w1b = nc.dram_tensor("d_w1b", [DM, DH], BF16, kind="Internal").ap()
    d_w2b = nc.dram_tensor("d_w2b", [DM, DH], BF16, kind="Internal").ap()
    d_h1b = nc.dram_tensor("d_h1b", [DH, DH], BF16, kind="Internal").ap()
    d_h2b = nc.dram_tensor("d_h2b", [DH, DH], BF16, kind="Internal").ap()
    d_w3b = nc.dram_tensor("d_w3b", [DH, DM], BF16, kind="Internal").ap()
    ccout = nc.dram_tensor("ccout", [512, 1024], BF16, kind="Internal").ap()

    dbg = {}
    for name in DEBUG:
        shp = {"t_q": [128, 1024], "t_kn": [128, 1024], "t_nlc": [128, 16],
               "t_G": [128, 1024], "t_MT": [128, 1024], "t_U": [128, 2048],
               "t_AT": [128, 1024], "t_B": [128, 1024], "t_Qt": [128, 1024],
               "t_S": [128, 512], "t_sin": [128, 512], "t_gate": [128, 4096],
               "t_x1": [128, 4096], "t_a": [128, 4096], "t_fcx": [128, 256],
               }[name]
        dt = F32
        dbg[name] = nc.dram_tensor(name, shp, dt, kind="ExternalOutput").ap()

    def inline(name, arr):
        if arr.dtype == BFnp:
            return nc.inline_tensor(np.ascontiguousarray(arr).view(np.uint16),
                                    name=name).ap().bitcast(BF16)
        return nc.inline_tensor(np.ascontiguousarray(arr), name=name).ap()

    tri_np = np.zeros((128, 64), np.float32)
    for p in range(128):
        tri_np[p, p % 64:] = 1.0
    mG = np.zeros((128, 1024), np.float32)
    mH = np.zeros((128, 1024), np.float32)
    mM = np.zeros((128, 1024), np.float32)
    ii = np.arange(64)
    for p in range(128):
        j = p % 64
        for blk in range(16):
            mG[p, blk * 64:(blk + 1) * 64] = np.where(j < ii, -1.0, 0.0)
            mH[p, blk * 64:(blk + 1) * 64] = np.where(ii < j, -1.0, 0.0)
            mM[p, blk * 64:(blk + 1) * 64] = np.where(j <= ii, 1.0, 0.0)
    c_tri = inline("c_tri", tri_np)
    c_id = inline("c_id", np.eye(128, dtype=np.float32).astype(BFnp))
    c_irep = inline("c_irep", np.tile(np.eye(64, dtype=np.float32), (2, 8)).astype(BFnp))
    c_ones = inline("c_ones", np.ones((128, 128), np.float32))
    c_mG = inline("c_mG", mG.astype(BFnp))
    c_mH = inline("c_mH", mH.astype(BFnp))
    c_mM = inline("c_mM", mM.astype(BFnp))

    sb = lambda n, shp, dt=F32: nc.alloc_sbuf_tensor(n, shp, dt).ap()
    tri_s = sb("tri_s", [128, 64])
    id_s = sb("id_s", [128, 128], BF16)
    irep_s = sb("irep_s", [128, 512], BF16)
    ones_s = sb("ones_s", [128, 128])
    mG_s = sb("mG_s", [128, 1024], BF16)
    mH_s = sb("mH_s", [128, 1024], BF16)
    mM_s = sb("mM_s", [128, 1024], BF16)
    sel_s = sb("sel_s", [128, 4])
    normw_s = sb("normw_s", [128, 8])
    scl_s = sb("scl_s", [128, 136])
    wsm_s = sb("wsm_s", [128, 8 * 96], BF16)
    wg2_s = sb("wg2_s", [64, DM], BF16)
    fcx_p = sb("fcx_p", [128, NCH * 8])
    g1_tm = sb("g1_tm", [128, NPAIR * 64], BF16)
    S_ab = [sb(f"S_ab{i}", [128, 512], BF16) for i in range(2)]
    P_ab = [sb(f"P_ab{i}", [128, 512], BF16) for i in range(2)]
    SF_t = sb("SF_t", [128, 512], BF16)
    PF_t = sb("PF_t", [128, 512], BF16)
    o2_t = sb("o2_t", [128, 512], BF16)
    ostg = sb("ostg", [128, 512], BF16)
    ptseg = sb("ptseg", [128, 512], BF16)
    gat_s = sb("gat_s", [128, 4096], BF16)
    rpj = [sb(f"rpj{i}", [128, 512], BF16) for i in range(4)]
    sin_s = sb("sin_s", [128, 512], BF16)
    dlt = [sb(f"dlt{i}", [128, 512], BF16) for i in range(2)]
    qtc = [sb(f"qtc{i}", [128, 512], BF16) for i in range(2)]
    atc = [sb(f"atc{i}", [128, 512], BF16) for i in range(2)]
    och = [sb(f"och{i}", [128, 512], BF16) for i in range(2)]
    row_t = sb("row_t", [1, 3 * 512])
    rn16 = sb("rn16", [128, 16])
    sg_f = sb("sg_f", [128, 16])
    nlf_s = sb("nlf_s", [128, 16])
    nspg_s = sb("nspg_s", [128, 16])
    nlc_s = sb("nlc_s", [128, 16])
    nn_s = sb("nn_s", [128, 16])
    dd_s = sb("dd_s", [128, 16])
    edec_s = sb("edec_s", [128, 16])
    rowm_s = sb("rowm_s", [128, 16])
    rowg_s = sb("rowg_s", [128, 16])
    el_s = sb("el_s", [128, 16])
    bcol_s = sb("bcol_s", [128, 16])
    nlcc_s = sb("nlcc_s", [128, 32])
    fcrow_s = sb("fcrow_s", [128, 32])

    dbg_stg = sb("dbg_stg", [128, 1024]) if DEBUG else None

    AR_BYTES = 150 * 1024 - (4608 if DEBUG else 0)
    ar_t = nc.alloc_sbuf_tensor("arena", [128, AR_BYTES], mybir.dt.uint8).ap()

    A = Arena(ar_t)
    wq_s = A.al([128, 8 * DM], BF16)
    wk_s = A.al([128, 8 * DM], BF16)
    wv_s = A.al([128, 8 * DM], BF16)
    xwb_ = [A.al([128, NKC * 128], BF16) for i in range(2)]
    q_f = A.al([128, 1024], F32)
    k_f = A.al([128, 1024], F32)
    ksq = A.al([128, 1024], F32)
    kn_f = A.al([128, 1024], F32)
    ktil_tm = A.al([128, 1024], BF16)
    kg_tm = A.al([128, 1024], BF16)
    qm_tm = A.al([128, 1024], BF16)
    qf_tm = A.al([128, 1024], BF16)
    ktil_f = A.al([128, 1024], BF16)
    kg_f = A.al([128, 1024], BF16)
    qm_f = A.al([128, 1024], BF16)
    qf_f = A.al([128, 1024], BF16)
    U_s = A.al([128, 2048], BF16)
    G_sb = [A.al([128, 1024], BF16) for i in range(2)]
    H_sb = [A.al([128, 1024], BF16) for i in range(2)]
    MT_sb = A.al([128, 1024], BF16)
    B_sb = A.al([128, 1024], F32)
    AT_sb = A.al([128, 1024], BF16)
    Qt_sb = A.al([128, 1024], BF16)
    wo_s = A.al([128, 8 * DM], BF16)
    assert A.off <= AR_BYTES, A.off
    wo_off = A.off - 16 * 1024

    T = Arena(ar_t)
    gate_w = T.al([128, 8 * 512], F32)
    og_w = T.al([128, 8 * 512], F32)
    xf_w = T.al([128, 8 * 512], F32)
    x1f_w = T.al([128, 8 * 512], F32)
    sq_w = T.al([128, 8 * 512], F32)
    outw = T.al([128, 8 * 512], BF16)
    on_w = T.al([128, 8 * 512], BF16)
    x1b_w = T.al([128, 8 * 512], BF16)
    g1f_w = T.al([64, 512], BF16)
    tail_end = T.off
    assert tail_end <= wo_off, (tail_end, wo_off)
    F = Arena(ar_t)
    FW = 512
    hstr = [F.al([128, 32 * 256], BF16) for i in range(2)]
    a_sb = F.al([128, 32 * FW], BF16)
    b_sb = F.al([128, 32 * FW], BF16)
    x1w_b = F.al([128, 8 * FW], BF16)
    r2_w = F.al([128, 8 * FW], F32)
    sq2_w = F.al([128, 8 * FW], F32)
    x1fr = [F.al([128, 512], F32) for i in range(2)]
    y_stg = F.al([128, 512], F32)
    silu_t = F.al([128, 512], F32)
    assert max(F.off, tail_end) <= AR_BYTES, (F.off, tail_end)

    p0 = nc.alloc_psum_tensor("p0", [128, 512], F32).ap()
    p1 = nc.alloc_psum_tensor("p1", [128, 512], F32).ap()
    p2 = nc.alloc_psum_tensor("p2", [128, 512], F32).ap()
    p3 = nc.alloc_psum_tensor("p3", [128, 1024], BF16).ap()
    p4 = nc.alloc_psum_tensor("p4", [128, 1024], F32).ap()
    p5 = nc.alloc_psum_tensor("p5", [128, 1024], F32).ap()
    BK = {n: Bank() for n in ("p0", "p1", "p2", "p3", "p4", "p5")}

    def dma(q, dst, src, waits=()):
        eng = "sync" if q.startswith("spd") else "gpsimd"
        return E.emit(eng, lambda e, d=dst, s=src: e.dma_start(out=d, in_=s),
                      waits=waits, inc=q, inc_by=16)

    def mm(lhsT, rhs, out, start, stop, tp=None, waits=()):
        return E.emit("tensor",
                      lambda e, l=lhsT, r=rhs, o=out, s1=start, s2=stop, t=tp:
                      e.matmul(out=o, lhsT=l, rhs=r, start=s1, stop=s2,
                               tile_position=t),
                      waits=waits, inc="pe")

    def trans(in_, out, ident, tp=None, waits=()):
        return E.emit("tensor",
                      lambda e, i=in_, o=out, d=ident, t=tp:
                      e.transpose(out=o, in_=i, identity=d, tile_position=t),
                      waits=waits, inc="pe")

    act = lambda fn, waits=(): E.emit("scalar", fn, waits=waits, inc="ac")
    dve = lambda fn, waits=(): E.emit("vector", fn, waits=waits, inc="dv")
    gpe = lambda fn, waits=(): E.emit("gpsimd", fn, waits=waits, inc="gp")

    def tap(name, src, waits):
        if name not in dbg:
            return
        if src.dtype == BF16:
            return
            n = src.shape[1]
            for c0 in range(0, n, 1024):
                cw = min(1024, n - c0)
                t = act(lambda e, src=src, c0=c0, cw=cw: e.activation(
                    out=dbg_stg[:, 0:cw], in_=src[:, c0:c0 + cw], func=AF.Copy),
                    waits=waits)
                td = dma("gpd", dbg[name][:, c0:c0 + cw],
                         dbg_stg[0:src.shape[0], 0:cw], waits=[("ac", t)])
                waits = [("gpd", td)]
        else:
            dma("gpd", dbg[name][:, :], src, waits=waits)

    rg = lambda h: (h % 2) * 64
    hp = lambda h: h // 2
    g8 = lambda t: t[:].rearrange("p (g i) -> p g i", g=8)
    g16 = lambda t: t[:].rearrange("p (g i) -> p g i", g=16)

    for dst, src in [(tri_s, c_tri), (id_s, c_id), (irep_s, c_irep),
                     (ones_s, c_ones), (mG_s, c_mG), (mH_s, c_mH), (mM_s, c_mM),
                     (sel_s, sel), (normw_s, normw), (scl_s, scl)]:
        dma("spd", dst[:, :], src[:, :])
    ld_c = E.tok("spd")
    for dst, src in [(wq_s, wq), (wk_s, wk), (wv_s, wv), (wo_s, wo)]:
        dma("gpd", dst[:].rearrange("p (kc c) -> p kc c", kc=8),
            src.rearrange("(kc p) c -> p kc c", p=128))
    dma("gpd", wsm_s[:].rearrange("p (kc c) -> p kc c", kc=8),
        wsm.rearrange("(kc p) c -> p kc c", p=128))
    dma("gpd", wg2_s[:, :], wg2[:, :])
    ld_w = E.tok("gpd")

    xw_tok = [None, None]
    for i in range(2):
        xw_tok[i] = dma(f"gpd{i}", xwb_[i][:].rearrange("p (kc t) -> p kc t", kc=NKC),
                        xT.rearrange("(kc p) t -> p kc t", p=128)
                        [:, :, i * 128:(i + 1) * 128], waits=[("gpd", ld_w)])

    def cast_chain():
        toks = [0, 0]
        chans = ("cst", "cst2")
        i = 0
        for (wsrc, wdst, ncol) in [(w1T, d_w1b, DH), (w2T, d_w2b, DH),
                                   (w3T, d_w3b, DM), (h1T, d_h1b, DH),
                                   (h2T, d_h2b, DH)]:
            nkcw = wsrc.shape[0] // 128
            for kc in range(nkcw):
                for c0 in range(0, ncol, 2048):
                    cw = min(2048, ncol - c0)
                    ch = chans[i % 2]
                    half = gat_s[:, (i % 2) * 2048:(i % 2) * 2048 + cw]
                    t1 = dma(ch, half, wsrc[kc * 128:(kc + 1) * 128, c0:c0 + cw],
                             waits=[(ch, toks[i % 2])] if toks[i % 2] else [("gpd", ld_w)])
                    toks[i % 2] = dma(ch, wdst[kc * 128:(kc + 1) * 128, c0:c0 + cw],
                                      half, waits=[(ch, t1)])
                    i += 1
        return toks

    cast_done = cast_chain()

    dve(lambda e: e.memset(S_ab[0][:, :], 0.0))
    dve(lambda e: e.tensor_copy(out=P_ab[0][:, :], in_=irep_s[:, :]),
        waits=[("spd", ld_c)])
    sp_tok = E.tok("dv")

    g_pe, g_dv, g_ac, g_spd = 0, 0, 0, 0
    s_evac = sp_tok
    p_evac = 0

    for pr in range(NPAIR_RUN):
        xwb = xwb_[pr % 2]
        Wg = [("pe", g_pe), ("dv", g_dv), ("ac", g_ac),
              (f"gpd{pr % 2}", xw_tok[pr % 2]),
              ("spd", E.tok("spd")), ("gpd", ld_w)]

        def proj(wt):
            for half, pm, bk in ((0, p0, "p0"), (1, p1, "p1")):
                for kc in range(NKC):
                    w = (Wg + BK[bk].for_write()) if kc == 0 else ()
                    t = mm(xwb[:, kc * 128:(kc + 1) * 128],
                           wt[:, kc * DM + half * 512: kc * DM + half * 512 + 512],
                           pm[:, :], kc == 0, kc == NKC - 1, waits=w)
                BK[bk].wrote("pe", t)

        proj(wq_s)
        tq1 = act(lambda e: e.activation(out=q_f[:, 0:512], in_=p0[:, :], func=AF.Silu),
                  waits=BK["p0"].for_read() + [("dv", g_dv)])
        BK["p0"].read("ac", tq1)
        tq2 = act(lambda e: e.activation(out=q_f[:, 512:1024], in_=p1[:, :], func=AF.Silu),
                  waits=BK["p1"].for_read())
        BK["p1"].read("ac", tq2)

        if PHASE == "projq" and pr == NPAIR_RUN - 1:
            E.ops["sync"].append(lambda e, n=E.count["spd"], sem=E.sems["spd"]: e.wait_ge(sem, n))
            E.ops["gpsimd"].append(lambda e, n=E.count["gpd"], sem=E.sems["gpd"]: e.wait_ge(sem, n))
            E.finalize()
            return nc
        proj(wk_s)
        tk1 = act(lambda e: e.activation(out=k_f[:, 0:512], in_=p0[:, :], func=AF.Silu),
                  waits=BK["p0"].for_read())
        BK["p0"].read("ac", tk1)
        tk2 = act(lambda e: e.activation(out=k_f[:, 512:1024], in_=p1[:, :], func=AF.Silu),
                  waits=BK["p1"].for_read())
        BK["p1"].read("ac", tk2)
        tap("t_q", q_f[:, :], [("ac", tq2)])
        tsq = act(lambda e: e.activation(out=ksq[:, :], in_=k_f[:, :], func=AF.Square), waits=[("ac", tk2)])
        trd = dve(lambda e: e.tensor_reduce(
            out=rn16[:].rearrange("p (g o) -> p g o", o=1),
            in_=g16(ksq), axis=mybir.AxisListType.X, op=ALU.add),
            waits=[("ac", tsq)])
        tmx = dve(lambda e: e.tensor_scalar(out=rn16[:, :], in0=rn16[:, :],
                                            scalar1=1e-24, scalar2=None, op0=ALU.max),
                  waits=[("dv", trd)])
        proj(wv_s)
        Uv3 = U_s[:].rearrange("p (h c) -> p h c", h=NH)
        tv1 = act(lambda e: e.activation(
            out=Uv3[:, 0:8, 0:64], in_=p0[:].rearrange("p (h c) -> p h c", c=64),
            func=AF.Silu), waits=BK["p0"].for_read())
        BK["p0"].read("ac", tv1)
        tv2 = act(lambda e: e.activation(
            out=Uv3[:, 8:16, 0:64], in_=p1[:].rearrange("p (h c) -> p h c", c=64),
            func=AF.Silu), waits=BK["p1"].for_read())
        BK["p1"].read("ac", tv2)

        if PHASE == "projv" and pr == NPAIR_RUN - 1:
            E.ops["sync"].append(lambda e, n=E.count["spd"], sem=E.sems["spd"]: e.wait_ge(sem, n))
            E.ops["gpsimd"].append(lambda e, n=E.count["gpd"], sem=E.sems["gpd"]: e.wait_ge(sem, n))
            E.finalize()
            return nc
        for kc in range(NKC):
            w = (Wg + BK["p2"].for_write()) if kc == 0 else ()
            tsm = mm(xwb[:, kc * 128:(kc + 1) * 128],
                     wsm_s[:, kc * 96:(kc + 1) * 96], p2[:, 0:96],
                     kc == 0, kc == NKC - 1, waits=w)
        BK["p2"].wrote("pe", tsm)
        tsg1 = act(lambda e: e.activation(out=sg_f[:, :], in_=p2[:, 0:16],
                                          func=AF.Sigmoid), waits=BK["p2"].for_read())
        tsg2 = act(lambda e: e.activation(out=nspg_s[:, :], in_=p2[:, 16:32],
                                          func=AF.Sigmoid))
        tg1 = act(lambda e, pr=pr: e.activation(out=g1_tm[:, pr * 64:(pr + 1) * 64],
                                                in_=p2[:, 32:96], func=AF.Copy))
        BK["p2"].read("ac", tg1)

        if PHASE == "projsm" and pr == NPAIR_RUN - 1:
            E.ops["sync"].append(lambda e, n=E.count["spd"], sem=E.sems["spd"]: e.wait_ge(sem, n))
            E.ops["gpsimd"].append(lambda e, n=E.count["gpd"], sem=E.sems["gpd"]: e.wait_ge(sem, n))
            E.finalize()
            return nc
        tln1 = act(lambda e: e.activation(out=nlf_s[:, :], in_=sg_f[:, :], func=AF.Ln),
                   waits=[("ac", tsg1)])
        tln2 = act(lambda e: e.activation(out=nspg_s[:, :], in_=nspg_s[:, :], func=AF.Ln),
                   waits=[("ac", tsg2)])
        tln3 = act(lambda e: e.activation(out=rn16[:, :], in_=rn16[:, :], func=AF.Ln),
                   waits=[("dv", tmx)])
        tneg1 = dve(lambda e: e.tensor_scalar(out=nlf_s[:, :], in0=nlf_s[:, :],
                                              scalar1=-1.0, scalar2=None, op0=ALU.mult),
                    waits=[("ac", tln1)])
        tneg2 = dve(lambda e: e.tensor_scalar(out=nspg_s[:, :], in0=nspg_s[:, :],
                                              scalar1=-1.0, scalar2=None, op0=ALU.mult),
                    waits=[("ac", tln2)])
        w0 = [("dv", tneg1)] + BK["p2"].for_write()
        mm(tri_s[0:64, :], nlf_s[0:64, :], p2[0:64, 96:112], True, True, waits=w0)
        mm(tri_s[64:128, :], nlf_s[64:128, :], p2[64:128, 96:112], True, True,
           tp=(64, 64))
        mm(ones_s[0:64, 0:1], nlf_s[0:64, :], p2[0:1, 112:128], True, True)
        cm4 = mm(ones_s[64:128, 0:1], nlf_s[64:128, :], p2[64:65, 128:144], True, True,
                 tp=(64, 64))
        BK["p2"].wrote("pe", cm4)
        tnlc = act(lambda e: e.activation(out=nlc_s[:, :], in_=p2[:, 96:112],
                                          func=AF.Copy), waits=BK["p2"].for_read())
        tncc = act(lambda e: e.activation(out=nlcc_s[0:1, 0:16], in_=p2[0:1, 112:128],
                                          func=AF.Copy))
        tncc = act(lambda e: e.activation(out=nlcc_s[64:65, 16:32],
                                          in_=p2[64:65, 128:144], func=AF.Copy))
        tnmv = dma("gpd", nlcc_s[0:1, 16:32], nlcc_s[64:65, 16:32],
                   waits=[("ac", tncc)])
        BK["p2"].read("ac", tncc)
        tap("t_nlc", nlc_s[:, :], [("ac", tnlc)])

        if PHASE == "cums" and pr == NPAIR_RUN - 1:
            E.ops["sync"].append(lambda e, n=E.count["spd"], sem=E.sems["spd"]: e.wait_ge(sem, n))
            E.ops["gpsimd"].append(lambda e, n=E.count["gpd"], sem=E.sems["gpd"]: e.wait_ge(sem, n))
            E.finalize()
            return nc

        wb = BK["p2"].for_write() + [("ac", tncc), ("gpd", tnmv)]
        mm(ones_s[0:1, 0:64], nlcc_s[0:1, 0:16], p2[0:64, 112:128], True, True, waits=wb)
        bc2 = mm(ones_s[0:1, 0:64], nlcc_s[0:1, 16:32], p2[64:128, 112:128], True,
                 True, tp=(0, 64))
        BK["p2"].wrote("pe", bc2)
        tdd = dve(lambda e: e.tensor_tensor(out=dd_s[:, :], in0=p2[:, 112:128],
                                            in1=nlc_s[:, :], op=ALU.subtract),
                  waits=BK["p2"].for_read() + [("ac", tncc)])
        tnn1 = dve(lambda e: e.tensor_tensor(out=nn_s[:, :], in0=nlc_s[:, :],
                                             in1=nlf_s[:, :], op=ALU.add),
                   waits=[("dv", tneg2)])
        tnn2 = dve(lambda e: e.tensor_tensor(out=nn_s[:, :], in0=nn_s[:, :],
                                             in1=nspg_s[:, :], op=ALU.add),
                   waits=[("dv", tnn1)])
        trg = dve(lambda e: e.tensor_tensor(out=rowg_s[:, :], in0=p2[:, 112:128],
                                            in1=nn_s[:, :], op=ALU.subtract),
                  waits=[("dv", tnn2)])
        BK["p2"].read("dv", trg)

        if PHASE == "rows" and pr == NPAIR_RUN - 1:
            E.ops["sync"].append(lambda e, n=E.count["spd"], sem=E.sems["spd"]: e.wait_ge(sem, n))
            E.ops["gpsimd"].append(lambda e, n=E.count["gpd"], sem=E.sems["gpd"]: e.wait_ge(sem, n))
            E.finalize()
            return nc
        te1 = act(lambda e: e.activation(out=edec_s[:, :], in_=dd_s[:, :], func=AF.Exp,
                                         scale=-1.0), waits=[("dv", trg), ("ac", tln3)])
        te2 = act(lambda e: e.activation(out=rowm_s[:, :], in_=dd_s[:, :], func=AF.Exp))
        te3 = act(lambda e: e.activation(out=el_s[:, :], in_=nlc_s[:, :], func=AF.Exp,
                                         scale=-1.0))
        te4 = act(lambda e: e.activation(out=bcol_s[:, :], in_=nn_s[:, :], func=AF.Exp,
                                         scale=-1.0))
        te5 = act(lambda e: e.activation(out=rowg_s[:, :], in_=rowg_s[:, :], func=AF.Exp))
        te6 = act(lambda e: e.activation(out=rn16[:, :], in_=rn16[:, :], func=AF.Exp,
                                         scale=-0.5))
        tfce = act(lambda e: e.activation(out=nlcc_s[0:1, 0:32], in_=nlcc_s[0:1, 0:32],
                                          func=AF.Exp, scale=-1.0),
                   waits=[("gpd", tnmv)])
        tbneg = dve(lambda e: e.tensor_scalar(out=bcol_s[:, :], in0=bcol_s[:, :],
                                              scalar1=-1.0, scalar2=None, op0=ALU.mult),
                    waits=[("ac", te4)])

        if PHASE == "exp" and pr == NPAIR_RUN - 1:
            E.ops["sync"].append(lambda e, n=E.count["spd"], sem=E.sems["spd"]: e.wait_ge(sem, n))
            E.ops["gpsimd"].append(lambda e, n=E.count["gpd"], sem=E.sems["gpd"]: e.wait_ge(sem, n))
            E.finalize()
            return nc
        tfr = dve(lambda e: e.tensor_copy(
            out=fcrow_s[0:1, 0:32].rearrange("p (c u e) -> p c u e", c=2, u=2),
            in_=nlcc_s[0:1, 0:32].rearrange("p (c e u) -> p c u e", c=2, e=8)),
            waits=[("ac", tfce)])

        if PHASE == "fcx1" and pr == NPAIR_RUN - 1:
            E.ops["sync"].append(lambda e, n=E.count["spd"], sem=E.sems["spd"]: e.wait_ge(sem, n))
            E.ops["gpsimd"].append(lambda e, n=E.count["gpd"], sem=E.sems["gpd"]: e.wait_ge(sem, n))
            E.finalize()
            return nc
        wf = BK["p2"].for_write() + [("dv", tfr)]
        mm(ones_s[0:1, 0:64], fcrow_s[0:1, 0:8], p2[0:64, 96:104], True, True,
           waits=wf)
        mm(ones_s[0:1, 0:64], fcrow_s[0:1, 8:16], p2[64:128, 96:104], True, True,
           tp=(0, 64))

        if PHASE == "fcx2" and pr == NPAIR_RUN - 1:
            E.ops["sync"].append(lambda e, n=E.count["spd"], sem=E.sems["spd"]: e.wait_ge(sem, n))
            E.ops["gpsimd"].append(lambda e, n=E.count["gpd"], sem=E.sems["gpd"]: e.wait_ge(sem, n))
            E.finalize()
            return nc
        mm(ones_s[0:1, 0:64], fcrow_s[0:1, 16:24], p2[0:64, 104:112], True, True)
        fb4 = mm(ones_s[0:1, 0:64], fcrow_s[0:1, 24:32], p2[64:128, 104:112],
                 True, True, tp=(0, 64))
        BK["p2"].wrote("pe", fb4)

        if PHASE == "fcx3" and pr == NPAIR_RUN - 1:
            E.ops["sync"].append(lambda e, n=E.count["spd"], sem=E.sems["spd"]: e.wait_ge(sem, n))
            E.ops["gpsimd"].append(lambda e, n=E.count["gpd"], sem=E.sems["gpd"]: e.wait_ge(sem, n))
            E.finalize()
            return nc
        tfc = dve(lambda e, pr=pr: e.tensor_copy(out=fcx_p[:, pr * 16:pr * 16 + 16],
                                                 in_=p2[:, 96:112]),
                  waits=BK["p2"].for_read())
        BK["p2"].read("dv", tfc)

        if PHASE == "proj" and pr == NPAIR_RUN - 1:
            E.ops["sync"].append(lambda e, n=E.count["spd"], sem=E.sems["spd"]: e.wait_ge(sem, n))
            E.ops["gpsimd"].append(lambda e, n=E.count["gpd"], sem=E.sems["gpd"]: e.wait_ge(sem, n))
            E.finalize()
            return nc
        tkn = dve(lambda e: e.tensor_tensor(out=g16(kn_f), in0=g16(k_f),
                                            in1=bcast(rn16[:], HD), op=ALU.mult),
                  waits=[("ac", te6)])
        dve(lambda e: e.tensor_tensor(out=g16(ktil_tm), in0=g16(kn_f),
                                      in1=bcast(edec_s[:], HD), op=ALU.mult),
            waits=[("ac", te1), ("dv", tkn)])
        dve(lambda e: e.tensor_tensor(out=g16(kg_tm), in0=g16(kn_f),
                                      in1=bcast(rowg_s[:], HD), op=ALU.mult),
            waits=[("ac", te5), ("dv", tkn)])
        tbk = dve(lambda e: e.tensor_tensor(out=Uv3[:, :, 64:128], in0=g16(kn_f),
                                            in1=bcast(bcol_s[:], HD), op=ALU.mult),
                  waits=[("dv", tbneg), ("dv", tkn), ("ac", tv2)])
        dve(lambda e: e.tensor_tensor(out=g16(qm_tm), in0=g16(q_f),
                                      in1=bcast(rowm_s[:], HD), op=ALU.mult),
            waits=[("ac", te2)])
        dve(lambda e: e.tensor_tensor(out=g16(qf_tm), in0=g16(q_f),
                                      in1=bcast(el_s[:], HD), op=ALU.mult),
            waits=[("ac", te3)])
        tscal = E.tok("dv")
        tap("t_kn", kn_f[:, :], [("dv", tkn)])

        if PHASE == "scal" and pr == NPAIR_RUN - 1:
            E.ops["sync"].append(lambda e, n=E.count["spd"], sem=E.sems["spd"]: e.wait_ge(sem, n))
            E.ops["gpsimd"].append(lambda e, n=E.count["gpd"], sem=E.sems["gpd"]: e.wait_ge(sem, n))
            E.finalize()
            return nc

        for src, dstf in [(ktil_tm, ktil_f), (kg_tm, kg_f), (qm_tm, qm_f),
                          (qf_tm, qf_f)]:
            for hh in range(8):
                w = ([("dv", tscal)] + BK["p3"].for_write()) if hh == 0 else ()
                tt = trans(src[:, hh * 128:(hh + 1) * 128],
                           p3[:, hh * 128:(hh + 1) * 128], id_s[:, :], waits=w)
            BK["p3"].wrote("pe", tt)
            tev = act(lambda e, d=dstf: e.activation(out=d[:, :], in_=p3[:, :],
                                                     func=AF.Copy),
                      waits=BK["p3"].for_read())
            BK["p3"].read("ac", tev)
        t_fm = E.tok("ac")

        if PHASE == "tmfm" and pr == NPAIR_RUN - 1:
            E.ops["sync"].append(lambda e, n=E.count["spd"], sem=E.sems["spd"]: e.wait_ge(sem, n))
            E.ops["gpsimd"].append(lambda e, n=E.count["gpd"], sem=E.sems["gpd"]: e.wait_ge(sem, n))
            E.finalize()
            return nc

        def ghm(lf, rf, pm, pname, extra):
            first = True
            for par in range(2):
                for ci in range(2):
                    for h in range(par, NH, 2):
                        w = (extra + BK[pname].for_write()) if first else ()
                        first = False
                        t = mm(lf[rg(h):rg(h) + 64,
                                  hp(h) * 128 + ci * 64: hp(h) * 128 + ci * 64 + 64],
                               rf[rg(h):rg(h) + 64,
                                  hp(h) * 128 + ci * 64: hp(h) * 128 + ci * 64 + 64],
                               pm[ci * 64:ci * 64 + 64, h * 64:h * 64 + 64],
                               True, True, tp=(rg(h), ci * 64), waits=w)
            BK[pname].wrote("pe", t)
            return t

        ghm(ktil_f, kg_f, p4, "p4", [("ac", t_fm)])

        if PHASE == "gmm" and pr == NPAIR_RUN - 1:
            E.ops["sync"].append(lambda e, n=E.count["spd"], sem=E.sems["spd"]: e.wait_ge(sem, n))
            E.ops["gpsimd"].append(lambda e, n=E.count["gpd"], sem=E.sems["gpd"]: e.wait_ge(sem, n))
            E.finalize()
            return nc
        ghm(kg_f, ktil_f, p5, "p5", [])

        if PHASE == "hmm" and pr == NPAIR_RUN - 1:
            E.ops["sync"].append(lambda e, n=E.count["spd"], sem=E.sems["spd"]: e.wait_ge(sem, n))
            E.ops["gpsimd"].append(lambda e, n=E.count["gpd"], sem=E.sems["gpd"]: e.wait_ge(sem, n))
            E.finalize()
            return nc
        tge = dve(lambda e: e.tensor_tensor(
            out=G_sb[0][:, :], in0=p4[:, :],
            in1=mG_s[:, :], op=ALU.mult),
            waits=BK["p4"].for_read())
        BK["p4"].read("dv", tge)
        the = dve(lambda e: e.tensor_tensor(
            out=H_sb[0][:, :], in0=p5[:, :],
            in1=mH_s[:, :], op=ALU.mult),
            waits=BK["p5"].for_read())
        BK["p5"].read("dv", the)

        if PHASE == "gev2" and pr == NPAIR_RUN - 1:
            E.ops["sync"].append(lambda e, n=E.count["spd"], sem=E.sems["spd"]: e.wait_ge(sem, n))
            E.ops["gpsimd"].append(lambda e, n=E.count["gpd"], sem=E.sems["gpd"]: e.wait_ge(sem, n))
            E.finalize()
            return nc
        ghm(ktil_f, kg_f if PHASE == "gev4" else qm_f, p4, "p4", [])

        if PHASE == "gev" and pr == NPAIR_RUN - 1:
            E.ops["sync"].append(lambda e, n=E.count["spd"], sem=E.sems["spd"]: e.wait_ge(sem, n))
            E.ops["gpsimd"].append(lambda e, n=E.count["gpd"], sem=E.sems["gpd"]: e.wait_ge(sem, n))
            E.finalize()
            return nc

        if PHASE in ("gev3", "gev4") and pr == NPAIR_RUN - 1:
            E.ops["sync"].append(lambda e, n=E.count["spd"], sem=E.sems["spd"]: e.wait_ge(sem, n))
            E.ops["gpsimd"].append(lambda e, n=E.count["gpd"], sem=E.sems["gpd"]: e.wait_ge(sem, n))
            E.finalize()
            return nc
        tme = dve(lambda e: e.tensor_tensor(
            out=MT_sb[:, :], in0=p4[:, :],
            in1=mM_s[:, :], op=ALU.mult),
            waits=BK["p4"].for_read())
        BK["p4"].read("dv", tme)
        tap("t_G", G_sb[0][:, :], [("dv", tge)])
        tap("t_MT", MT_sb[:, :], [("dv", tme)])

        if PHASE == "ghm" and pr == NPAIR_RUN - 1:
            E.ops["sync"].append(lambda e, n=E.count["spd"], sem=E.sems["spd"]: e.wait_ge(sem, n))
            E.ops["gpsimd"].append(lambda e, n=E.count["gpd"], sem=E.sems["gpd"]: e.wait_ge(sem, n))
            E.finalize()
            return nc


        cur = 0
        u_dep = [("dv", tge), ("dv", the), ("dv", tbk), ("ac", tv2)]
        for s in range(6):
            first = True
            for ci in range(2):
                for h in range(NH):
                    pm = p4 if h < 8 else p5
                    col = (h % 8) * 128
                    w = ()
                    if first:
                        w = u_dep + BK["p4"].for_write() + BK["p5"].for_write()
                        first = False
                    t = mm(G_sb[cur][ci * 64:ci * 64 + 64, h * 64:h * 64 + 64],
                           U_s[ci * 64:ci * 64 + 64, h * 128:h * 128 + 128],
                           pm[ci * 64:ci * 64 + 64, col:col + 128],
                           True, True, tp=(ci * 64, ci * 64), waits=w)
            BK["p4"].wrote("pe", t)
            BK["p5"].wrote("pe", t)
            ua = dve(lambda e: e.tensor_tensor(out=U_s[:, 0:1024], in0=p4[:, :],
                                               in1=U_s[:, 0:1024], op=ALU.add),
                     waits=BK["p4"].for_read())
            BK["p4"].read("dv", ua)
            ub = dve(lambda e: e.tensor_tensor(out=U_s[:, 1024:2048], in0=p5[:, :],
                                               in1=U_s[:, 1024:2048], op=ALU.add),
                     waits=BK["p5"].for_read())
            BK["p5"].read("dv", ub)
            u_dep = [("dv", ub)]
            if s < 5:
                first = True
                for ci in range(2):
                    for h in range(NH):
                        w = ()
                        if first:
                            w = [("dv", ua), ("dv", ub)] + BK["p4"].for_write()
                            first = False
                        t = mm(H_sb[cur][ci * 64:ci * 64 + 64, h * 64:h * 64 + 64],
                               G_sb[cur][ci * 64:ci * 64 + 64, h * 64:h * 64 + 64],
                               p4[ci * 64:ci * 64 + 64, h * 64:h * 64 + 64],
                               True, True, tp=(ci * 64, ci * 64), waits=w)
                BK["p4"].wrote("pe", t)
                tgs = act(lambda e, nx=1 - cur: e.activation(out=G_sb[nx][:, :],
                                                             in_=p4[:, :], func=AF.Copy),
                          waits=BK["p4"].for_read())
                BK["p4"].read("ac", tgs)
                if s < 4:
                    first = True
                    for ci in range(2):
                        for h in range(NH):
                            w = ()
                            if first:
                                w = BK["p5"].for_write()
                                first = False
                            t = mm(G_sb[cur][ci * 64:ci * 64 + 64, h * 64:h * 64 + 64],
                                   H_sb[cur][ci * 64:ci * 64 + 64, h * 64:h * 64 + 64],
                                   p5[ci * 64:ci * 64 + 64, h * 64:h * 64 + 64],
                                   True, True, tp=(ci * 64, ci * 64), waits=w)
                    BK["p5"].wrote("pe", t)
                    ths = act(lambda e, nx=1 - cur: e.activation(out=H_sb[nx][:, :],
                                                                 in_=p5[:, :],
                                                                 func=AF.Copy),
                              waits=BK["p5"].for_read())
                    BK["p5"].read("ac", ths)
                cur = 1 - cur
        u_done = u_dep[0][1]
        tap("t_U", U_s[:, :], [("dv", u_done)])

        if PHASE == "solve" and pr == NPAIR_RUN - 1:
            E.ops["sync"].append(lambda e, n=E.count["spd"], sem=E.sems["spd"]: e.wait_ge(sem, n))
            E.ops["gpsimd"].append(lambda e, n=E.count["gpd"], sem=E.sems["gpd"]: e.wait_ge(sem, n))
            E.finalize()
            return nc


        def bak(lsl, rsl, pm, pname, extra):
            first = True
            for ci in range(2):
                for h in range(NH):
                    w = (extra + BK[pname].for_write()) if first else ()
                    first = False
                    t = mm(lsl(ci, h), rsl(ci, h),
                           pm[rg(h):rg(h) + 64,
                              hp(h) * 128 + ci * 64: hp(h) * 128 + ci * 64 + 64],
                           True, True, tp=(ci * 64, rg(h)), waits=w)
            BK[pname].wrote("pe", t)
            return t

        bak(lambda ci, h: ktil_tm[ci * 64:ci * 64 + 64, h * 64:h * 64 + 64],
            lambda ci, h: U_s[ci * 64:ci * 64 + 64, h * 128:h * 128 + 64],
            p4, "p4", [("dv", u_done)])
        tbe = act(lambda e: e.activation(out=B_sb[:, :], in_=p4[:, :], func=AF.Copy),
                  waits=BK["p4"].for_read())
        BK["p4"].read("ac", tbe)
        bak(lambda ci, h: U_s[ci * 64:ci * 64 + 64, h * 128 + 64:h * 128 + 128],
            lambda ci, h: ktil_tm[ci * 64:ci * 64 + 64, h * 64:h * 64 + 64],
            p5, "p5", [])
        tae = act(lambda e: e.activation(out=AT_sb[:, :], in_=p5[:, :], func=AF.Copy),
                  waits=BK["p5"].for_read() + [("spd", E.tok("spd"))])
        BK["p5"].read("ac", tae)
        for ci in range(2):
            dma("spd", d_A[2 * pr + ci, :, :].rearrange("p (g i) -> p g i", g=8),
                AT_sb[:].rearrange("p (g c i) -> p g c i", g=8, c=2)[:, :, ci, :],
                waits=[("ac", tae)])
        bak(lambda ci, h: U_s[ci * 64:ci * 64 + 64, h * 128 + 64:h * 128 + 128],
            lambda ci, h: MT_sb[ci * 64:ci * 64 + 64, h * 64:h * 64 + 64],
            p4, "p4", [("dv", tme)])
        tqe = dve(lambda e: e.tensor_tensor(out=Qt_sb[:, :], in0=p4[:, :],
                                            in1=qf_f[:, :], op=ALU.add),
                  waits=BK["p4"].for_read() + [("ac", t_fm), ("spd", E.tok("spd"))])
        BK["p4"].read("dv", tqe)
        for ci in range(2):
            dma("spd", d_Qt[2 * pr + ci, :, :].rearrange("p (g i) -> p g i", g=8),
                Qt_sb[:].rearrange("p (g c i) -> p g c i", g=8, c=2)[:, :, ci, :],
                waits=[("dv", tqe)])
        tap("t_AT", AT_sb[:, :], [("ac", tae)])
        tap("t_B", B_sb[:, :], [("ac", tbe)])
        tap("t_Qt", Qt_sb[:, :], [("dv", tqe)])

        if PHASE == "bak" and pr == NPAIR_RUN - 1:
            E.ops["sync"].append(lambda e, n=E.count["spd"], sem=E.sems["spd"]: e.wait_ge(sem, n))
            E.ops["gpsimd"].append(lambda e, n=E.count["gpd"], sem=E.sems["gpd"]: e.wait_ge(sem, n))
            E.finalize()
            return nc


        for ci in range(2):
            c = 2 * pr + ci
            Scur, Snxt = S_ab[c % 2], S_ab[(c + 1) % 2]
            Pcur, Pnxt = P_ab[c % 2], P_ab[(c + 1) % 2]
            wS = [("dv", s_evac), ("ac", p_evac), ("dv", tfc)]
            fsl = fcx_p[:, c * 8:(c + 1) * 8]
            tSF = dve(lambda e, S=Scur, f=fsl: e.tensor_tensor(
                out=g8(SF_t), in0=g8(S), in1=bcast(f, 64), op=ALU.mult), waits=wS)
            tPF = dve(lambda e, P=Pcur, f=fsl: e.tensor_tensor(
                out=g8(PF_t), in0=g8(P), in1=bcast(f, 64), op=ALU.mult))
            first = True
            for h in range(NH):
                a_sl = AT_sb[rg(h):rg(h) + 64,
                             hp(h) * 128 + ci * 64: hp(h) * 128 + ci * 64 + 64]
                w = ([("dv", tPF), ("ac", tae)] + BK["p2"].for_write() +
                     BK["p0"].for_write()) if first else ()
                first = False
                mm(a_sl, Scur[rg(h):rg(h) + 64, hp(h) * 64: hp(h) * 64 + 64],
                   p2[rg(h):rg(h) + 64, hp(h) * 64: hp(h) * 64 + 64],
                   True, False, tp=(rg(h), rg(h)), waits=w)
                mm(id_s[rg(h):rg(h) + 64, rg(h):rg(h) + 64],
                   SF_t[rg(h):rg(h) + 64, hp(h) * 64: hp(h) * 64 + 64],
                   p2[rg(h):rg(h) + 64, hp(h) * 64: hp(h) * 64 + 64],
                   False, True, tp=(rg(h), rg(h)))
                mm(a_sl, Pcur[rg(h):rg(h) + 64, hp(h) * 64: hp(h) * 64 + 64],
                   p0[rg(h):rg(h) + 64, hp(h) * 64: hp(h) * 64 + 64],
                   True, False, tp=(rg(h), rg(h)))
                tpl = mm(id_s[rg(h):rg(h) + 64, rg(h):rg(h) + 64],
                         PF_t[rg(h):rg(h) + 64, hp(h) * 64: hp(h) * 64 + 64],
                         p0[rg(h):rg(h) + 64, hp(h) * 64: hp(h) * 64 + 64],
                         False, True, tp=(rg(h), rg(h)))
            BK["p2"].wrote("pe", tpl)
            BK["p0"].wrote("pe", tpl)
            s_evac = dve(lambda e, S=Snxt, ci=ci: e.tensor_tensor(
                out=g8(S), in0=g8(p2),
                in1=B_sb[:].rearrange("p (g c i) -> p g c i", g=8, c=2)[:, :, ci, :],
                op=ALU.add), waits=BK["p2"].for_read() + [("ac", tbe)])
            BK["p2"].read("dv", s_evac)
            p_evac = act(lambda e, P=Pnxt: e.activation(out=P[:, :], in_=p0[:, :],
                                                        func=AF.Copy),
                         waits=BK["p0"].for_read())
            BK["p0"].read("ac", p_evac)
            first = True
            for h in range(NH):
                w = ([("dv", tqe)] + BK["p5"].for_write()) if first else ()
                first = False
                mm(Scur[rg(h):rg(h) + 64, hp(h) * 64: hp(h) * 64 + 64],
                   Qt_sb[rg(h):rg(h) + 64,
                         hp(h) * 128 + ci * 64: hp(h) * 128 + ci * 64 + 64],
                   p5[rg(h):rg(h) + 64, hp(h) * 64: hp(h) * 64 + 64],
                   True, True, tp=(rg(h), rg(h)), waits=w)
            for h in range(NH):
                to2 = mm(U_s[ci * 64:ci * 64 + 64, h * 128:h * 128 + 64],
                         MT_sb[ci * 64:ci * 64 + 64, h * 64:h * 64 + 64],
                         p5[rg(h):rg(h) + 64,
                            512 + hp(h) * 64: 512 + hp(h) * 64 + 64],
                         True, True, tp=(ci * 64, rg(h)))
            BK["p5"].wrote("pe", to2)
            to2e = act(lambda e: e.activation(out=o2_t[:, :], in_=p5[:, 512:1024],
                                              func=AF.Copy),
                       waits=BK["p5"].for_read())
            toe = dve(lambda e: e.tensor_tensor(out=ostg[:, :], in0=p5[:, 0:512],
                                                in1=o2_t[:, :], op=ALU.add),
                      waits=BK["p5"].for_read() + [("ac", to2e), ("spd", E.tok("spd"))])
            BK["p5"].read("dv", toe)
            tos = dma("spd", d_out[c, :, :], ostg[:, :], waits=[("dv", toe)])
            g_spd = tos
        tap("t_S", S_ab[NCH % 2][:, :] if pr == NPAIR - 1 else S_ab[0][:, :],
            [("dv", s_evac)] if pr == NPAIR - 1 else [("dv", s_evac)])

        if pr + 2 < NPAIR:
            xw_tok[pr % 2] = dma(
                f"gpd{pr % 2}", xwb_[pr % 2][:].rearrange("p (kc t) -> p kc t", kc=NKC),
                xT.rearrange("(kc p) t -> p kc t", p=128)
                [:, :, (pr + 2) * 128:(pr + 3) * 128],
                waits=[("pe", E.tok("pe"))])
        g_pe, g_dv, g_ac = E.tok("pe"), E.tok("dv"), E.tok("ac")

    tap("t_fcx", fcx_p[:, :], [("ac", g_ac)])

    if PHASE == "pairs":
        E.ops["sync"].append(lambda e, n=E.count["spd"], sem=E.sems["spd"]: e.wait_ge(sem, n))
        E.ops["gpsimd"].append(lambda e, n=E.count["gpd"], sem=E.sems["gpd"]: e.wait_ge(sem, n))
        E.finalize()
        return nc

    Pfin = P_ab[NCH % 2]
    first = True
    for h in range(NH):
        w = ([("ac", p_evac)] + BK["p3"].for_write()) if first else ()
        first = False
        tl = trans(Pfin[rg(h):rg(h) + 64, hp(h) * 64: hp(h) * 64 + 64],
                   p3[rg(h):rg(h) + 64, hp(h) * 64: hp(h) * 64 + 64],
                   id_s[rg(h):rg(h) + 64, rg(h):rg(h) + 64],
                   tp=(rg(h), rg(h)), waits=w)
    BK["p3"].wrote("pe", tl)
    tpt = act(lambda e: e.activation(out=ptseg[:, :], in_=p3[:, 0:512], func=AF.Copy),
              waits=BK["p3"].for_read())
    BK["p3"].read("ac", tpt)
    dma("spd", ccin[:, 0:512], ptseg[:, :], waits=[("ac", tpt)])
    dma("spd", ccin[:, 512:1024], S_ab[NCH % 2][:, :], waits=[("dv", s_evac)])
    cc_w = E.tok("spd")
    tcc = gpe(lambda e: e.collective_compute(
        "AllGather", ALU.bypass, replica_groups=[[0, 1, 2, 3], [4, 5, 6, 7]],
        ins=[ccin[:, :]], outs=[ccout[:, :]]), waits=[("spd", cc_w)])
    tgl = dma("spd", gat_s[:].rearrange("p (r c) -> p r c", r=4),
              ccout.rearrange("(r p) c -> p r c", p=128),
              waits=[("gp", tcc), ("cst", cast_done[0]), ("cst2", cast_done[1])])
    tz = dve(lambda e: e.memset(rpj[0][:, :], 0.0))
    last_dv = tz
    for j in range(3):
        first = True
        for h in range(NH):
            w = ([("spd", tgl), ("dv", last_dv)] + BK["p2"].for_write()) if first else ()
            first = False
            tcj = mm(gat_s[rg(h):rg(h) + 64,
                           j * 1024 + hp(h) * 64: j * 1024 + hp(h) * 64 + 64],
                     rpj[j][rg(h):rg(h) + 64, hp(h) * 64: hp(h) * 64 + 64],
                     p2[rg(h):rg(h) + 64, hp(h) * 64: hp(h) * 64 + 64],
                     True, True, tp=(rg(h), rg(h)), waits=w)
        BK["p2"].wrote("pe", tcj)
        last_dv = dve(lambda e, j=j: e.tensor_tensor(
            out=rpj[j + 1][:, :], in0=p2[:, :],
            in1=gat_s[:, j * 1024 + 512: j * 1024 + 1024], op=ALU.add),
            waits=BK["p2"].for_read() + [("spd", tgl)])
        BK["p2"].read("dv", last_dv)
    dve(lambda e: e.tensor_scalar(out=sin_s[:, :], in0=rpj[1][:, :],
                                  scalar1=sel_s[:, 1:2], scalar2=None, op0=ALU.mult),
        waits=[("dv", last_dv), ("spd", ld_c)])
    dve(lambda e: e.scalar_tensor_tensor(out=sin_s[:, :], in0=rpj[2][:, :],
                                         scalar=sel_s[:, 2:3], in1=sin_s[:, :],
                                         op0=ALU.mult, op1=ALU.add),
        waits=[("dv", E.tok("dv"))])
    dve(lambda e: e.scalar_tensor_tensor(out=sin_s[:, :], in0=rpj[3][:, :],
                                         scalar=sel_s[:, 3:4], in1=sin_s[:, :],
                                         op0=ALU.mult, op1=ALU.add),
        waits=[("dv", E.tok("dv"))])
    tsin = dve(lambda e: e.tensor_copy(out=dlt[0][:, :], in_=sin_s[:, :]),
               waits=[("dv", E.tok("dv"))])
    tap("t_sin", sin_s[:, :], [("dv", tsin)])

    if PHASE == "gather":
        E.ops["sync"].append(lambda e, n=E.count["spd"], sem=E.sems["spd"]: e.wait_ge(sem, n))
        E.ops["gpsimd"].append(lambda e, n=E.count["gpd"], sem=E.sems["gpd"]: e.wait_ge(sem, n))
        E.finalize()
        return nc

    qa_tok = [0, 0]
    dl_ev = tsin
    for c in range(2):
        dma(f"spd{c}", qtc[c][:, :], d_Qt[c, :, :], waits=[("spd", E.tok("spd"))])
        dma(f"spd{c}", atc[c][:, :], d_A[c, :, :])
        qa_tok[c] = dma(f"spd{c}", och[c][:, :], d_out[c, :, :])
    for c in range(NCH):
        ri = c % 2
        first = True
        for h in range(NH):
            w = ([(f"spd{ri}", qa_tok[ri]), ("dv", dl_ev if c == 0 else 0),
                  ("ac", 0 if c == 0 else dl_ev)] + BK["p5"].for_write()) if first else ()
            first = False
            tocm = mm(dlt[c % 2][rg(h):rg(h) + 64, hp(h) * 64: hp(h) * 64 + 64],
                      qtc[ri][rg(h):rg(h) + 64, hp(h) * 64: hp(h) * 64 + 64],
                      p5[rg(h):rg(h) + 64, hp(h) * 64: hp(h) * 64 + 64],
                      True, True, tp=(rg(h), rg(h)), waits=w)
        BK["p5"].wrote("pe", tocm)
        tca = dve(lambda e, ri=ri: e.tensor_tensor(out=och[ri][:, :], in0=p5[:, 0:512],
                                                   in1=och[ri][:, :], op=ALU.add),
                  waits=BK["p5"].for_read())
        BK["p5"].read("dv", tca)
        tcs = dma(f"spd{ri}", d_out[c, :, :], och[ri][:, :], waits=[("dv", tca)])
        tdf = dve(lambda e, c=c: e.tensor_tensor(
            out=g8(PF_t), in0=g8(dlt[c % 2]),
            in1=bcast(fcx_p[:, c * 8:(c + 1) * 8], 64), op=ALU.mult),
            waits=[("dv", tsin)] if c == 0 else [("ac", dl_ev), ("dv", tca)])
        first = True
        for h in range(NH):
            w = ([("dv", tdf)] + BK["p2"].for_write()) if first else ()
            first = False
            mm(atc[ri][rg(h):rg(h) + 64, hp(h) * 64: hp(h) * 64 + 64],
               dlt[c % 2][rg(h):rg(h) + 64, hp(h) * 64: hp(h) * 64 + 64],
               p2[rg(h):rg(h) + 64, hp(h) * 64: hp(h) * 64 + 64],
               True, False, tp=(rg(h), rg(h)), waits=w)
            tdm = mm(id_s[rg(h):rg(h) + 64, rg(h):rg(h) + 64],
                     PF_t[rg(h):rg(h) + 64, hp(h) * 64: hp(h) * 64 + 64],
                     p2[rg(h):rg(h) + 64, hp(h) * 64: hp(h) * 64 + 64],
                     False, True, tp=(rg(h), rg(h)))
        BK["p2"].wrote("pe", tdm)
        dl_ev = act(lambda e, nx=(c + 1) % 2: e.activation(out=dlt[nx][:, :],
                                                           in_=p2[:, :], func=AF.Copy),
                    waits=BK["p2"].for_read())
        BK["p2"].read("ac", dl_ev)
        if c + 2 < NCH:
            dma(f"spd{ri}", qtc[ri][:, :], d_Qt[c + 2, :, :],
                waits=[("pe", tocm), ("dv", tca), ("ac", dl_ev),
                       (f"spd{ri}", tcs)])
            dma(f"spd{ri}", atc[ri][:, :], d_A[c + 2, :, :])
            qa_tok[ri] = dma(f"spd{ri}", och[ri][:, :], d_out[c + 2, :, :])
    corr_dma = E.tok("spd")
    corr_dma01 = (E.tok("spd0"), E.tok("spd1"))
    if PHASE == "corr":
        E.ops["sync"].append(lambda e, n=E.count["spd"], sem=E.sems["spd"]: e.wait_ge(sem, n))
        E.ops["gpsimd"].append(lambda e, n=E.count["gpd"], sem=E.sems["gpd"]: e.wait_ge(sem, n))
        E.finalize()
        return nc
    ph1 = [("pe", E.tok("pe")), ("dv", E.tok("dv")), ("ac", E.tok("ac")),
           ("spd", E.tok("spd")), ("gpd", E.tok("gpd")), ("gp", E.tok("gp"))]

    def ln_stats(buf, n, sq_ap, apply_list, wtok_pe, prev_reads):
        first = True
        for kc in range(n):
            w = (prev_reads + BK["p0"].for_write()) if first else ()
            first = False
            t1 = mm(ones_s[:, 0:1], buf[:, kc * 512:(kc + 1) * 512],
                    p0[0:1, 0:512], kc == 0, kc == n - 1, waits=w)
        BK["p0"].wrote("pe", t1)
        first = True
        for kc in range(n):
            w = (BK["p1"].for_write() + [("ac", sq_ap)]) if first else ()
            first = False
            t2 = mm(ones_s[:, 0:1], sq_w[:, kc * 512:(kc + 1) * 512],
                    p1[0:1, 0:512], kc == 0, kc == n - 1, waits=w)
        BK["p1"].wrote("pe", t2)
        tmu = dve(lambda e: e.tensor_scalar(out=row_t[0:1, 0:512], in0=p0[0:1, :],
                                            scalar1=1.0 / DM, scalar2=None,
                                            op0=ALU.mult), waits=BK["p0"].for_read())
        BK["p0"].read("dv", tmu)
        tv_ = dve(lambda e: e.tensor_scalar(out=row_t[0:1, 512:1024], in0=p1[0:1, :],
                                            scalar1=1.0 / DM, scalar2=None,
                                            op0=ALU.mult), waits=BK["p1"].for_read())
        BK["p1"].read("dv", tv_)
        t_r = dve(lambda e: e.tensor_tensor(out=row_t[0:1, 1024:1536],
                                            in0=row_t[0:1, 0:512], in1=row_t[0:1, 0:512],
                                            op=ALU.mult), waits=[("dv", tmu)])
        t_r = dve(lambda e: e.tensor_tensor(out=row_t[0:1, 512:1024],
                                            in0=row_t[0:1, 512:1024],
                                            in1=row_t[0:1, 1024:1536], op=ALU.subtract),
                  waits=[("dv", t_r)])
        tva = dve(lambda e: e.tensor_scalar(out=row_t[0:1, 512:1024],
                                            in0=row_t[0:1, 512:1024], scalar1=1e-5,
                                            scalar2=None, op0=ALU.add),
                  waits=[("dv", t_r)])
        tsd = act(lambda e: e.activation(out=row_t[0:1, 1024:1536],
                                         in_=row_t[0:1, 512:1024], func=AF.Sqrt),
                  waits=[("dv", tva)])
        tiv = dve(lambda e: e.reciprocal(out=row_t[0:1, 512:1024],
                                         in_=row_t[0:1, 1024:1536]),
                  waits=[("ac", tsd)])
        wb = BK["p4"].for_write() + [("dv", tiv)]
        mm(ones_s[0:1, :], row_t[0:1, 0:512], p4[:, 0:512], True, True, waits=wb)
        tb2 = mm(ones_s[0:1, :], row_t[0:1, 512:1024], p4[:, 512:1024], True, True)
        BK["p4"].wrote("pe", tb2)
        return tb2

    x1_dma = []
    for w in range(NWIN):
        t0 = w * 512
        tx = dma("spd", xf_w[:].rearrange("p (kc t) -> p kc t", kc=NKC),
                 xT.rearrange("(kc p) t -> p kc t", p=128)[:, :, t0:t0 + 512],
                 waits=(ph1 + [("spd0", corr_dma01[0]), ("spd1", corr_dma01[1])])
                 if w == 0 else [("spd", corr_dma)])
        tow = dma("spd", outw[:].rearrange("p (kc c i) -> p kc c i", kc=NKC, c=8),
                  d_out[w * 8:(w + 1) * 8, :, :]
                  .rearrange("c p (kc i) -> p kc c i", kc=8),
                  waits=[("spd", corr_dma)])
        first = True
        for i in range(4):
            wt = ((ph1 if w == 0 else []) + BK["p3"].for_write()) if first else ()
            first = False
            tg = trans(g1_tm[:, (w * 4 + i) * 64:(w * 4 + i + 1) * 64],
                       p3[0:64, i * 128:(i + 1) * 128], id_s[:, :], waits=wt)
        BK["p3"].wrote("pe", tg)
        tgf = act(lambda e: e.activation(out=g1f_w[:, :], in_=p3[0:64, 0:512],
                                         func=AF.Copy), waits=BK["p3"].for_read())
        BK["p3"].read("ac", tgf)
        for kc in range(NKC):
            pm, bk = (p0, "p0") if kc % 2 == 0 else (p1, "p1")
            wt = [("ac", tgf)] + BK[bk].for_write()
            tgm = mm(wg2_s[:, kc * 128:(kc + 1) * 128], g1f_w[:, :], pm[:, :],
                     True, True, waits=wt)
            BK[bk].wrote("pe", tgm)
            tsg = act(lambda e, kc=kc, pm=pm: e.activation(
                out=gate_w[:, kc * 512:(kc + 1) * 512], in_=pm[:, :],
                func=AF.Sigmoid), waits=BK[bk].for_read())
            BK[bk].read("ac", tsg)
        tgate = E.tok("ac")
        tap("t_gate", gate_w[:, :], [("ac", tgate)]) if w == 0 else None
        tog = dve(lambda e: e.tensor_tensor(out=og_w[:, :], in0=outw[:, :],
                                            in1=gate_w[:, :], op=ALU.mult),
                  waits=[("ac", tgate), ("spd", tow)])
        tsq = act(lambda e: e.activation(out=sq_w[:, :], in_=og_w[:, :],
                                         func=AF.Square), waits=[("dv", tog)])
        tb2 = ln_stats(og_w, 8, tsq, None, None, [("dv", tog)])
        for kc in range(NKC):
            t_ = dve(lambda e, kc=kc: e.tensor_tensor(
                out=og_w[:, kc * 512:(kc + 1) * 512],
                in0=og_w[:, kc * 512:(kc + 1) * 512], in1=p4[:, 0:512],
                op=ALU.subtract),
                waits=(BK["p4"].for_read() + [("ac", tsq)]) if kc == 0 else ())
            t_ = dve(lambda e, kc=kc: e.scalar_tensor_tensor(
                out=on_w[:, kc * 512:(kc + 1) * 512],
                in0=og_w[:, kc * 512:(kc + 1) * 512],
                scalar=normw_s[:, kc:kc + 1], in1=p4[:, 512:1024],
                op0=ALU.mult, op1=ALU.mult), waits=[("dv", t_)])
        BK["p4"].read("dv", t_)
        ton = t_
        for m in range(NKC):
            pm, bk = (p2, "p2") if m % 2 == 0 else (p5, "p5")
            pslice = pm[:, 0:512]
            first = True
            for kc in range(NKC):
                wt = ([("dv", ton), ("spd", tow)] + BK[bk].for_write()) if first else ()
                first = False
                twm = mm(wo_s[:, kc * DM + m * 128: kc * DM + m * 128 + 128],
                         on_w[:, kc * 512:(kc + 1) * 512], pslice,
                         kc == 0, kc == NKC - 1, waits=wt)
            BK[bk].wrote("pe", twm)
            tr1 = dve(lambda e, m=m, pslice=pslice: e.tensor_tensor(
                out=x1f_w[:, m * 512:(m + 1) * 512], in0=pslice,
                in1=xf_w[:, m * 512:(m + 1) * 512], op=ALU.add),
                waits=BK[bk].for_read())
            BK[bk].read("dv", tr1)
        tsq2 = act(lambda e: e.activation(out=sq_w[:, :], in_=x1f_w[:, :],
                                          func=AF.Square), waits=[("dv", tr1)])
        tb2 = ln_stats(x1f_w, 8, tsq2, None, None, [("dv", tr1)])
        for kc in range(NKC):
            t_ = dve(lambda e, kc=kc: e.tensor_tensor(
                out=x1f_w[:, kc * 512:(kc + 1) * 512],
                in0=x1f_w[:, kc * 512:(kc + 1) * 512], in1=p4[:, 0:512],
                op=ALU.subtract),
                waits=(BK["p4"].for_read() + [("ac", tsq2)]) if kc == 0 else ())
            t_ = dve(lambda e, kc=kc: e.tensor_tensor(
                out=x1f_w[:, kc * 512:(kc + 1) * 512],
                in0=x1f_w[:, kc * 512:(kc + 1) * 512], in1=p4[:, 512:1024],
                op=ALU.mult), waits=[("dv", t_)])
        BK["p4"].read("dv", t_)
        tx1 = t_
        tx1b = act(lambda e: e.activation(out=x1b_w[:, :], in_=x1f_w[:, :],
                                          func=AF.Copy), waits=[("dv", tx1)])
        tap("t_x1", x1f_w[:, 0:4096], [("dv", tx1)]) if w == 0 else None
        d1 = dma("spd", d_x1f[:, :, t0:t0 + 512].rearrange("kc p t -> p kc t"),
                 x1f_w[:].rearrange("p (kc t) -> p kc t", kc=NKC),
                 waits=[("dv", tx1)])
        d2 = dma("spd", d_x1b[:, :, t0:t0 + 512].rearrange("kc p t -> p kc t"),
                 x1b_w[:].rearrange("p (kc t) -> p kc t", kc=NKC),
                 waits=[("ac", tx1b)])
        x1_dma.append(d2)
        g_ac = E.tok("ac")
    tail_dma = E.tok("spd")
    if PHASE == "tail":
        E.ops["sync"].append(lambda e, n=E.count["spd"], sem=E.sems["spd"]: e.wait_ge(sem, n))
        E.ops["gpsimd"].append(lambda e, n=E.count["gpd"], sem=E.sems["gpd"]: e.wait_ge(sem, n))
        E.finalize()
        return nc

    ph2 = [("pe", E.tok("pe")), ("dv", E.tok("dv")), ("ac", E.tok("ac")),
           ("spd", E.tok("spd")), ("gpd", E.tok("gpd"))]
    y_dma = []
    y_w = sq2_w
    g_pe2, g_dv2, g_ac2 = E.tok("pe"), E.tok("dv"), E.tok("ac")
    for wi in range(4):
        woff = wi * 512
        txb = dma("spd", x1w_b[:].rearrange("p (kc t) -> p kc t", kc=NKC),
                  d_x1b[:, :, woff:woff + 512].rearrange("kc p t -> p kc t"),
                  waits=(ph2 if wi == 0 else [("pe", g_pe2), ("dv", g_dv2),
                                              ("ac", g_ac2)]))
        dmw = [0, 0]

        def l1_load(g, mg2, w0):
            dma(f"gpd{g}", hstr[g][:, 0:2048].rearrange("p (kc c) -> p kc c", kc=8),
                d_w1b.rearrange("(kc p) c -> p kc c", p=128)
                [:, :, mg2 * 256:(mg2 + 1) * 256], waits=w0)
            tok = dma(f"gpd{g}", hstr[g][:, 2048:4096].rearrange("p (kc c) -> p kc c", kc=8),
                      d_w2b.rearrange("(kc p) c -> p kc c", p=128)
                      [:, :, mg2 * 256:(mg2 + 1) * 256])
            return tok

        for g in range(2):
            dmw[g] = l1_load(g, g, ((ph2 if wi == 0 and g == 0 else [("gpd", ld_w)]) +
                                    [("cst", cast_done[0]), ("cst2", cast_done[1]),
                                     (f"gpd{g}", E.tok(f"gpd{g}")), ("pe", E.tok("pe"))]))
        prev_tac = 0
        for mg in range(16):
            ri = mg % 2
            for mi in range(2):
                m = mg * 2 + mi
                pm1, bk1 = (p4, "p4") if m % 2 == 0 else (p5, "p5")
                first = True
                for kc in range(NKC):
                    wt = ([(f"gpd{ri}", dmw[ri]), ("spd", txb)] +
                          BK[bk1].for_write()) if first else ()
                    first = False
                    mm(hstr[ri][:, kc * 256 + mi * 128: kc * 256 + mi * 128 + 128],
                       x1w_b[:, kc * 512:(kc + 1) * 512],
                       pm1[:, 0:512], kc == 0, kc == NKC - 1, waits=wt)
                    t1 = mm(hstr[ri][:, 2048 + kc * 256 + mi * 128:
                                     2048 + kc * 256 + mi * 128 + 128],
                            x1w_b[:, kc * 512:(kc + 1) * 512],
                            pm1[:, 512:1024], kc == 0, kc == NKC - 1)
                BK[bk1].wrote("pe", t1)
                tsl = act(lambda e, m=m, pm1=pm1: e.activation(
                    out=silu_t[:, :], in_=pm1[:, 0:512], func=AF.Silu,
                    scale=scl_s[:, m:m + 1]),
                    waits=BK[bk1].for_read() + [("dv", prev_tac)])
                tac = dve(lambda e, m=m, pm1=pm1: e.scalar_tensor_tensor(
                    out=a_sb[:, m * 512:(m + 1) * 512], in0=pm1[:, 512:1024],
                    scalar=scl_s[:, 32 + m:33 + m], in1=silu_t[:, :],
                    op0=ALU.mult, op1=ALU.mult), waits=[("ac", tsl)])
                BK[bk1].read("dv", tac)
                prev_tac = tac
            if mg + 2 < 16:
                dmw[ri] = l1_load(ri, mg + 2,
                                  [("pe", E.tok("pe")), ("dv", E.tok("dv")),
                                   (f"gpd{ri}", E.tok(f"gpd{ri}"))])
        a_done = E.tok("dv")

        def hload(hT, hB, g, mg2, w0):
            return dma(f"gpd{g}", hstr[g][:].rearrange("p (kc c) -> p kc c", kc=32),
                       hB.rearrange("(kc p) c -> p kc c", p=128)
                       [:, :, mg2 * 256:(mg2 + 1) * 256], waits=w0)

        def hlayer(hT, hB, src, dst, scol, dep):
            dmh = [0, 0]
            for g in range(2):
                dmh[g] = hload(hT, hB, g, g,
                               [("pe", E.tok("pe")), ("dv", E.tok("dv")),
                                ("ac", E.tok("ac")),
                                (f"gpd{g}", E.tok(f"gpd{g}"))])
            for mg in range(16):
                ri = mg % 2
                for mi in range(2):
                    m = mg * 2 + mi
                    pm1, bk1 = (p4, "p4") if mi == 0 else (p5, "p5")
                    first = True
                    for kc in range(32):
                        wt = ([(f"gpd{ri}", dmh[ri])] + dep +
                              BK[bk1].for_write()) if first else ()
                        first = False
                        t1 = mm(hstr[ri][:, kc * 256 + mi * 128:
                                         kc * 256 + mi * 128 + 128],
                                src[:, kc * 512:(kc + 1) * 512],
                                pm1[:, 0:512], kc == 0, kc == 31, waits=wt)
                    BK[bk1].wrote("pe", t1)
                    tgl = act(lambda e, pm1=pm1, m=m, scol=scol: e.activation(
                        out=dst[:, m * 512:(m + 1) * 512], in_=pm1[:, 0:512],
                        func=AF.Gelu, scale=scl_s[:, scol + m: scol + m + 1]),
                        waits=BK[bk1].for_read())
                    BK[bk1].read("ac", tgl)
                if mg + 2 < 16:
                    dmh[ri] = hload(hT, hB, ri, mg + 2,
                                    [("pe", E.tok("pe")),
                                     (f"gpd{ri}", E.tok(f"gpd{ri}"))])
            return E.tok("ac")

        h1_done = hlayer(h1T, d_h1b, a_sb, b_sb, 64, [("dv", a_done)])
        h2_done = hlayer(h2T, d_h2b, b_sb, a_sb, 96, [("ac", h1_done)])

        dmh = [0, 0]
        for g in range(2):
            dmh[g] = hload(w3T, d_w3b, g, g,
                           [("pe", E.tok("pe")), ("ac", E.tok("ac")),
                            (f"gpd{g}", E.tok(f"gpd{g}"))])
        x1t = [0, 0]
        for g in range(2):
            x1t[g] = dma(f"spd{g}", x1fr[g][:, :], d_x1f[g, :, woff:woff + 512],
                         waits=[("dv", E.tok("dv")), ("spd", E.tok("spd"))])
        for mg in range(4):
            ri = mg % 2
            for mi in range(2):
                m = mg * 2 + mi
                pm1, bk1 = (p4, "p4") if mi == 0 else (p5, "p5")
                first = True
                for kc in range(32):
                    wt = ([(f"gpd{ri}", dmh[ri]), ("ac", h2_done),
                           (f"spd{m % 2}", x1t[m % 2])] + BK[bk1].for_write()) if first else ()
                    first = False
                    t1 = mm(hstr[ri][:, kc * 256 + mi * 128: kc * 256 + mi * 128 + 128],
                            a_sb[:, kc * 512:(kc + 1) * 512],
                            pm1[:, 0:512], kc == 0, kc == 31, waits=wt)
                BK[bk1].wrote("pe", t1)
                tr2 = dve(lambda e, pm1=pm1, m=m: e.scalar_tensor_tensor(
                    out=r2_w[:, m * 512:(m + 1) * 512], in0=pm1[:, 0:512],
                    scalar=scl_s[:, 128 + m:129 + m], in1=x1fr[m % 2][:, :],
                    op0=ALU.mult, op1=ALU.add), waits=BK[bk1].for_read())
                BK[bk1].read("dv", tr2)
                if m + 2 < 8:
                    x1t[m % 2] = dma(f"spd{m % 2}", x1fr[m % 2][:, :],
                                     d_x1f[m + 2, :, woff:woff + 512],
                                     waits=[("dv", tr2)])
            if mg + 2 < 4:
                dmh[ri] = hload(w3T, d_w3b, ri, mg + 2,
                                [("pe", E.tok("pe")),
                                 (f"gpd{ri}", E.tok(f"gpd{ri}"))])
        r2_done = E.tok("dv")

        tq3 = act(lambda e: e.activation(out=sq2_w[:, :], in_=r2_w[:, :],
                                         func=AF.Square), waits=[("dv", r2_done)])
        first = True
        for kc in range(NKC):
            wt = ([("dv", r2_done)] + BK["p0"].for_write()) if first else ()
            first = False
            t1 = mm(ones_s[:, 0:1], r2_w[:, kc * 512:(kc + 1) * 512],
                    p0[0:1, 0:512], kc == 0, kc == NKC - 1, waits=wt)
        BK["p0"].wrote("pe", t1)
        first = True
        for kc in range(NKC):
            wt = ([("ac", tq3)] + BK["p1"].for_write()) if first else ()
            first = False
            t2 = mm(ones_s[:, 0:1], sq2_w[:, kc * 512:(kc + 1) * 512],
                    p1[0:1, 0:512], kc == 0, kc == NKC - 1, waits=wt)
        BK["p1"].wrote("pe", t2)
        tmu = dve(lambda e: e.tensor_scalar(out=row_t[0:1, 0:512], in0=p0[0:1, :],
                                            scalar1=1.0 / DM, scalar2=None,
                                            op0=ALU.mult), waits=BK["p0"].for_read())
        BK["p0"].read("dv", tmu)
        tvv = dve(lambda e: e.tensor_scalar(out=row_t[0:1, 512:1024], in0=p1[0:1, :],
                                            scalar1=1.0 / DM, scalar2=None,
                                            op0=ALU.mult), waits=BK["p1"].for_read())
        BK["p1"].read("dv", tvv)
        t_r = dve(lambda e: e.tensor_tensor(out=row_t[0:1, 1024:1536], in0=row_t[0:1, 0:512],
                                            in1=row_t[0:1, 0:512], op=ALU.mult),
                  waits=[("dv", tmu)])
        t_r = dve(lambda e: e.tensor_tensor(out=row_t[0:1, 512:1024],
                                            in0=row_t[0:1, 512:1024],
                                            in1=row_t[0:1, 1024:1536], op=ALU.subtract),
                  waits=[("dv", t_r)])
        tva = dve(lambda e: e.tensor_scalar(out=row_t[0:1, 512:1024],
                                            in0=row_t[0:1, 512:1024], scalar1=1e-5,
                                            scalar2=None, op0=ALU.add),
                  waits=[("dv", t_r)])
        tsd = act(lambda e: e.activation(out=row_t[0:1, 1024:1536],
                                         in_=row_t[0:1, 512:1024], func=AF.Sqrt),
                  waits=[("dv", tva)])
        tiv = dve(lambda e: e.reciprocal(out=row_t[0:1, 512:1024],
                                         in_=row_t[0:1, 1024:1536]),
                  waits=[("ac", tsd)])
        wb = BK["p4"].for_write() + [("dv", tiv)]
        mm(ones_s[0:1, :], row_t[0:1, 0:512], p4[:, 0:512], True, True, waits=wb)
        tb2 = mm(ones_s[0:1, :], row_t[0:1, 512:1024], p4[:, 512:1024], True, True)
        BK["p4"].wrote("pe", tb2)
        for kc in range(NKC):
            t_ = dve(lambda e, kc=kc: e.tensor_tensor(
                out=y_w[:, kc * 512:(kc + 1) * 512],
                in0=r2_w[:, kc * 512:(kc + 1) * 512], in1=p4[:, 0:512],
                op=ALU.subtract),
                waits=(BK["p4"].for_read() + [("pe", t2)]) if kc == 0 else ())
            t_ = dve(lambda e, kc=kc: e.tensor_tensor(
                out=y_w[:, kc * 512:(kc + 1) * 512],
                in0=y_w[:, kc * 512:(kc + 1) * 512], in1=p4[:, 512:1024],
                op=ALU.mult), waits=[("dv", t_)])
        BK["p4"].read("dv", t_)
        ty = E.tok("dv")
        y_dma.append(dma("spd",
                         yT.rearrange("(kc p) t -> p kc t", p=128)[:, :, woff:woff + 512],
                         y_w[:].rearrange("p (kc t) -> p kc t", kc=NKC),
                         waits=[("dv", ty)]))
        g_pe2, g_dv2, g_ac2 = E.tok("pe"), E.tok("dv"), E.tok("ac")

    final_spd = E.count["spd"]
    def _final_wait(e, n=final_spd, sem=E.sems["spd"]):
        e.wait_ge(sem, n)
    E.ops["sync"].append(lambda e: _final_wait(e))
    E.finalize()
    return nc


_NC_CACHE = {}


def _get_nc():
    key = ("prog", tuple(DEBUG), PHASE, NPAIR_RUN)
    if key not in _NC_CACHE:
        _NC_CACHE[key] = build_program()
    return _NC_CACHE[key]


def kernel(L0_attn_x, L0_attn_Wq, L0_attn_Wk, L0_attn_Wv, L0_attn_Wgamma,
           L0_attn_Wf, L0_attn_Wg1, L0_attn_Wg2, L0_attn_Wo, L0_attn_norm_w,
           L0_ffn_w1_w, L0_ffn_w1_s, L0_ffn_w2_w, L0_ffn_w2_s, L0_ffn_h1_w,
           L0_ffn_h1_s, L0_ffn_h2_w, L0_ffn_h2_s, L0_ffn_w3_w, L0_ffn_w3_s):
    nc = _get_nc()
    x = np.asarray(L0_attn_x)
    cc = np.ascontiguousarray
    wsm_np = cc(np.concatenate([np.asarray(L0_attn_Wf), np.asarray(L0_attn_Wgamma),
                                np.asarray(L0_attn_Wg1)], axis=1))
    scl_np = np.zeros((128, 136), np.float32)
    scl_np[:, 0:32] = np.asarray(L0_ffn_w1_s).reshape(32, 128).T
    scl_np[:, 32:64] = np.asarray(L0_ffn_w2_s).reshape(32, 128).T
    scl_np[:, 64:96] = np.asarray(L0_ffn_h1_s).reshape(32, 128).T
    scl_np[:, 96:128] = np.asarray(L0_ffn_h2_s).reshape(32, 128).T
    scl_np[:, 128:136] = np.asarray(L0_ffn_w3_s).reshape(8, 128).T
    shared = {
        "wq": cc(np.asarray(L0_attn_Wq)), "wk": cc(np.asarray(L0_attn_Wk)),
        "wv": cc(np.asarray(L0_attn_Wv)), "wo": cc(np.asarray(L0_attn_Wo)),
        "wsm": wsm_np, "wg2": cc(np.asarray(L0_attn_Wg2)),
        "normw": cc(np.asarray(L0_attn_norm_w).reshape(8, 128).T),
        "w1T": cc(np.asarray(L0_ffn_w1_w).T), "w2T": cc(np.asarray(L0_ffn_w2_w).T),
        "h1T": cc(np.asarray(L0_ffn_h1_w).T), "h2T": cc(np.asarray(L0_ffn_h2_w).T),
        "w3T": cc(np.asarray(L0_ffn_w3_w).T), "scl": scl_np,
    }
    in_maps = []
    xTb = [cc(x[b].T) for b in range(2)]
    for core in range(8):
        b, s = core // 4, core % 4
        selv = np.zeros((128, 4), np.float32)
        selv[:, s] = 1.0
        m = dict(shared)
        m["xT"] = cc(xTb[b][:, s * SEG:(s + 1) * SEG])
        m["sel"] = selv
        in_maps.append(m)
    res = run_bass_kernel_spmd(nc, in_maps, core_ids=list(range(8)))
    out = np.zeros((2, 8192, DM), np.float32)
    for core in range(8):
        b, s = core // 4, core % 4
        out[b, s * SEG:(s + 1) * SEG, :] = np.asarray(res.results[core]["yT"]).T
    if DEBUG:
        global LAST_TAPS
        LAST_TAPS = [{k: np.asarray(v) for k, v in res.results[c].items()
                      if k.startswith("t_")} for c in range(8)]
    return out


# version 05
# speedup vs baseline: 1.1225x; 1.0940x over previous
import numpy as np
import ml_dtypes
import concourse.bass as bass
import concourse.mybir as mybir
from concourse.bass_utils import run_bass_kernel_spmd
from concourse.ap import AP

F32 = mybir.dt.float32
BF16 = mybir.dt.bfloat16
BFnp = ml_dtypes.bfloat16
AF = mybir.ActivationFunctionType
ALU = mybir.AluOpType

SEG, C, NH, HD, DM, DH = 2048, 64, 16, 64, 1024, 4096
NCH, NPAIR, NKC, NWIN = 32, 16, 8, 4
FFN_W = (768, 640, 640)
DEBUG = ()
PHASE = "full"
NPAIR_RUN = NPAIR


def bcast(ap, n):
    return AP(ap.tensor, ap.offset, [list(x) for x in ap.ap] + [[0, n]])


def rep(ap, n, m):
    return AP(ap.tensor, ap.offset, [list(ap.ap[0])] + [[0, n]] + [[1, m]])


class Emitter:
    ENGINES = ("sync", "tensor", "vector", "scalar", "gpsimd")

    def __init__(self, nc):
        self.nc = nc
        self.ops = {e: [] for e in self.ENGINES}
        self.sems, self.count = {}, {}
        for name in ("pe", "dv", "ac", "gp", "spd", "gpd", "spd0", "spd1",
                     "gpd0", "gpd1", "cst", "cst2"):
            self.sems[name] = nc.alloc_semaphore("sem_" + name)
            self.count[name] = 0

    def emit(self, eng, fn, waits=(), inc=None, inc_by=1):
        for (s, v) in waits:
            assert v <= self.count[s], f"{eng}: wait {s}>={v} > {self.count[s]}"
        tok = None
        if inc is not None:
            self.count[inc] += inc_by
            tok = self.count[inc]
        wl = [(self.sems[s], v) for (s, v) in waits if v > 0]
        sem = self.sems[inc] if inc is not None else None

        def closure(e, wl=wl, sem=sem, inc_by=inc_by, fn=fn):
            for (sh, v) in wl:
                e.wait_ge(sh, v)
            ins = fn(e)
            if sem is not None:
                ins.then_inc(sem, inc_by)
        self.ops[eng].append(closure)
        return tok

    def tok(self, s):
        return self.count[s]

    def finalize(self):
        with self.nc.Block() as block:
            for eng in self.ENGINES:
                ops = self.ops[eng]
                getattr(block, eng)(lambda e, ops=ops: [f(e) for f in ops])


class Bank:
    def __init__(self):
        self.w, self.r = [], []

    def for_write(self):
        return list(self.r) + list(self.w)

    def for_read(self):
        return list(self.w)

    def wrote(self, sem, val):
        self.w, self.r = [(sem, val)], []

    def read(self, sem, val):
        self.r = [(sem, val)]


class Arena:
    def __init__(self, ap):
        self.ap = ap
        self.off = 0

    def al(self, shape, dt):
        esz = {F32: 4, BF16: 2}[dt]
        size = shape[1] * esz
        sl = self.ap[0:shape[0], self.off:self.off + size].bitcast(dt)
        self.off += (size + 63) // 64 * 64
        return sl


def build_program():
    nc = bass.Bass("TRN2", target_bir_lowering=False, num_devices=8)
    E = Emitter(nc)

    din = lambda n, shp, dt=F32: nc.dram_tensor(n, shp, dt, kind="ExternalInput").ap()
    xT = din("xT", [DM, SEG])
    sel = din("sel", [128, 4])
    wq, wk, wv, wo = din("wq", [DM, DM]), din("wk", [DM, DM]), din("wv", [DM, DM]), din("wo", [DM, DM])
    wsm = din("wsm", [DM, 96])
    wg2 = din("wg2", [64, DM])
    normw = din("normw", [128, 8])
    w1T, w2T = din("w1T", [DM, DH]), din("w2T", [DM, DH])
    h1T, h2T = din("h1T", [DH, DH]), din("h2T", [DH, DH])
    w3T = din("w3T", [DH, DM])
    scl = din("scl", [128, 136])
    yT = nc.dram_tensor("yT", [DM, SEG], F32, kind="ExternalOutput").ap()

    d_A = nc.dram_tensor("d_A", [NCH, 128, 512], BF16, kind="Internal").ap()
    d_Qt = nc.dram_tensor("d_Qt", [NCH, 128, 512], BF16, kind="Internal").ap()
    d_out = nc.dram_tensor("d_out", [NCH, 128, 512], BF16, kind="Internal").ap()
    d_x1f = nc.dram_tensor("d_x1f", [NKC, 128, SEG], F32, kind="Internal").ap()
    d_x1b = nc.dram_tensor("d_x1b", [NKC, 128, SEG], BF16, kind="Internal").ap()
    ccin = nc.dram_tensor("ccin", [128, 1024], BF16, kind="Internal").ap()
    d_w1b = nc.dram_tensor("d_w1b", [DM, DH], BF16, kind="Internal").ap()
    d_w2b = nc.dram_tensor("d_w2b", [DM, DH], BF16, kind="Internal").ap()
    d_h1b = nc.dram_tensor("d_h1b", [DH, DH], BF16, kind="Internal").ap()
    d_h2b = nc.dram_tensor("d_h2b", [DH, DH], BF16, kind="Internal").ap()
    d_w3b = nc.dram_tensor("d_w3b", [DH, DM], BF16, kind="Internal").ap()
    ccout = nc.dram_tensor("ccout", [512, 1024], BF16, kind="Internal").ap()

    dbg = {}
    for name in DEBUG:
        shp = {"t_q": [128, 1024], "t_kn": [128, 1024], "t_nlc": [128, 16],
               "t_G": [128, 1024], "t_MT": [128, 1024], "t_U": [128, 2048],
               "t_AT": [128, 1024], "t_B": [128, 1024], "t_Qt": [128, 1024],
               "t_S": [128, 512], "t_sin": [128, 512], "t_gate": [128, 4096],
               "t_x1": [128, 4096], "t_a": [128, 4096], "t_fcx": [128, 256],
               }[name]
        dt = F32
        dbg[name] = nc.dram_tensor(name, shp, dt, kind="ExternalOutput").ap()

    def inline(name, arr):
        if arr.dtype == BFnp:
            return nc.inline_tensor(np.ascontiguousarray(arr).view(np.uint16),
                                    name=name).ap().bitcast(BF16)
        return nc.inline_tensor(np.ascontiguousarray(arr), name=name).ap()

    tri_np = np.zeros((128, 64), np.float32)
    for p in range(128):
        tri_np[p, p % 64:] = 1.0
    mG = np.zeros((128, 1024), np.float32)
    mH = np.zeros((128, 1024), np.float32)
    mM = np.zeros((128, 1024), np.float32)
    ii = np.arange(64)
    for p in range(128):
        j = p % 64
        for blk in range(16):
            mG[p, blk * 64:(blk + 1) * 64] = np.where(j < ii, -1.0, 0.0)
            mH[p, blk * 64:(blk + 1) * 64] = np.where(ii < j, -1.0, 0.0)
            mM[p, blk * 64:(blk + 1) * 64] = np.where(j <= ii, 1.0, 0.0)
    c_tri = inline("c_tri", tri_np)
    c_id = inline("c_id", np.eye(128, dtype=np.float32).astype(BFnp))
    c_irep = inline("c_irep", np.tile(np.eye(64, dtype=np.float32), (2, 8)).astype(BFnp))
    c_ones = inline("c_ones", np.ones((128, 128), np.float32))
    c_mG = inline("c_mG", mG.astype(BFnp))
    c_mH = inline("c_mH", mH.astype(BFnp))
    c_mM = inline("c_mM", mM.astype(BFnp))

    sb = lambda n, shp, dt=F32: nc.alloc_sbuf_tensor(n, shp, dt).ap()
    tri_s = sb("tri_s", [128, 64])
    id_s = sb("id_s", [128, 128], BF16)
    irep_s = sb("irep_s", [128, 512], BF16)
    ones_s = sb("ones_s", [128, 128])
    mG_s = sb("mG_s", [128, 1024], BF16)
    mH_s = sb("mH_s", [128, 1024], BF16)
    mM_s = sb("mM_s", [128, 1024], BF16)
    sel_s = sb("sel_s", [128, 4])
    normw_s = sb("normw_s", [128, 8])
    scl_s = sb("scl_s", [128, 136])
    wsm_s = sb("wsm_s", [128, 8 * 96], BF16)
    wg2_s = sb("wg2_s", [64, DM], BF16)
    fcx_p = sb("fcx_p", [128, NCH * 8])
    g1_tm = sb("g1_tm", [128, NPAIR * 64], BF16)
    S_ab = [sb(f"S_ab{i}", [128, 512], BF16) for i in range(2)]
    P_ab = [sb(f"P_ab{i}", [128, 512], BF16) for i in range(2)]
    SF_t = sb("SF_t", [128, 512], BF16)
    PF_t = sb("PF_t", [128, 512], BF16)
    o2_t = sb("o2_t", [128, 512], BF16)
    ostg = sb("ostg", [128, 512], BF16)
    ptseg = sb("ptseg", [128, 512], BF16)
    gat_s = sb("gat_s", [128, 4096], BF16)
    rpj = [sb(f"rpj{i}", [128, 512], BF16) for i in range(4)]
    sin_s = sb("sin_s", [128, 512], BF16)
    dlt = [sb(f"dlt{i}", [128, 512], BF16) for i in range(2)]
    qtc = [sb(f"qtc{i}", [128, 512], BF16) for i in range(2)]
    atc = [sb(f"atc{i}", [128, 512], BF16) for i in range(2)]
    och = [sb(f"och{i}", [128, 512], BF16) for i in range(2)]
    row_t = sb("row_t", [1, 3 * 512])
    rn16 = sb("rn16", [128, 16])
    sg_f = sb("sg_f", [128, 16])
    nlf_s = sb("nlf_s", [128, 16])
    nspg_s = sb("nspg_s", [128, 16])
    nlc_s = sb("nlc_s", [128, 16])
    nn_s = sb("nn_s", [128, 16])
    dd_s = sb("dd_s", [128, 16])
    edec_s = sb("edec_s", [128, 16])
    rowm_s = sb("rowm_s", [128, 16])
    rowg_s = sb("rowg_s", [128, 16])
    el_s = sb("el_s", [128, 16])
    bcol_s = sb("bcol_s", [128, 16])
    nlcc_s = sb("nlcc_s", [128, 32])
    fcrow_s = sb("fcrow_s", [128, 32])

    dbg_stg = sb("dbg_stg", [128, 1024]) if DEBUG else None

    AR_BYTES = 150 * 1024 - (4608 if DEBUG else 0)
    ar_t = nc.alloc_sbuf_tensor("arena", [128, AR_BYTES], mybir.dt.uint8).ap()

    A = Arena(ar_t)
    wq_s = A.al([128, 8 * DM], BF16)
    wk_s = A.al([128, 8 * DM], BF16)
    wv_s = A.al([128, 8 * DM], BF16)
    xwb_ = [A.al([128, NKC * 128], BF16) for i in range(2)]
    q_f = A.al([128, 1024], F32)
    k_f = A.al([128, 1024], F32)
    ksq = A.al([128, 1024], F32)
    kn_f = A.al([128, 1024], F32)
    ktil_tm = A.al([128, 1024], BF16)
    kg_tm = A.al([128, 1024], BF16)
    qm_tm = A.al([128, 1024], BF16)
    qf_tm = A.al([128, 1024], BF16)
    ktil_f = A.al([128, 1024], BF16)
    kg_f = A.al([128, 1024], BF16)
    qm_f = A.al([128, 1024], BF16)
    qf_f = A.al([128, 1024], BF16)
    U_s = A.al([128, 2048], BF16)
    G_sb = [A.al([128, 1024], BF16) for i in range(2)]
    H_sb = [A.al([128, 1024], BF16) for i in range(2)]
    MT_sb = A.al([128, 1024], BF16)
    B_sb = A.al([128, 1024], F32)
    AT_sb = A.al([128, 1024], BF16)
    Qt_sb = A.al([128, 1024], BF16)
    wo_s = A.al([128, 8 * DM], BF16)
    assert A.off <= AR_BYTES, A.off
    wo_off = A.off - 16 * 1024

    T = Arena(ar_t)
    gate_w = T.al([128, 8 * 512], F32)
    og_w = T.al([128, 8 * 512], F32)
    xf_w = T.al([128, 8 * 512], F32)
    x1f_w = T.al([128, 8 * 512], F32)
    sq_w = T.al([128, 8 * 512], F32)
    outw = T.al([128, 8 * 512], BF16)
    on_w = T.al([128, 8 * 512], BF16)
    x1b_w = T.al([128, 8 * 512], BF16)
    g1f_w = T.al([64, 512], BF16)
    tail_end = T.off
    assert tail_end <= wo_off, (tail_end, wo_off)
    F = Arena(ar_t)
    FW = 512
    hstr = [F.al([128, 32 * 256], BF16) for i in range(2)]
    a_sb = F.al([128, 32 * FW], BF16)
    b_sb = F.al([128, 32 * FW], BF16)
    x1w_b = F.al([128, 8 * FW], BF16)
    r2_w = F.al([128, 8 * FW], F32)
    sq2_w = F.al([128, 8 * FW], F32)
    x1fr = [F.al([128, 512], F32) for i in range(2)]
    y_stg = F.al([128, 512], F32)
    silu_t = F.al([128, 512], F32)
    assert max(F.off, tail_end) <= AR_BYTES, (F.off, tail_end)

    p0 = nc.alloc_psum_tensor("p0", [128, 512], F32).ap()
    p1 = nc.alloc_psum_tensor("p1", [128, 512], F32).ap()
    p2 = nc.alloc_psum_tensor("p2", [128, 512], F32).ap()
    p3 = nc.alloc_psum_tensor("p3", [128, 1024], BF16).ap()
    p4 = nc.alloc_psum_tensor("p4", [128, 1024], F32).ap()
    p5 = nc.alloc_psum_tensor("p5", [128, 1024], F32).ap()
    BK = {n: Bank() for n in ("p0", "p1", "p2", "p3", "p4", "p5")}

    def dma(q, dst, src, waits=()):
        eng = "sync" if q.startswith("spd") else "gpsimd"
        return E.emit(eng, lambda e, d=dst, s=src: e.dma_start(out=d, in_=s),
                      waits=waits, inc=q, inc_by=16)

    def mm(lhsT, rhs, out, start, stop, tp=None, waits=()):
        return E.emit("tensor",
                      lambda e, l=lhsT, r=rhs, o=out, s1=start, s2=stop, t=tp:
                      e.matmul(out=o, lhsT=l, rhs=r, start=s1, stop=s2,
                               tile_position=t),
                      waits=waits, inc="pe")

    def trans(in_, out, ident, tp=None, waits=()):
        return E.emit("tensor",
                      lambda e, i=in_, o=out, d=ident, t=tp:
                      e.transpose(out=o, in_=i, identity=d, tile_position=t),
                      waits=waits, inc="pe")

    act = lambda fn, waits=(): E.emit("scalar", fn, waits=waits, inc="ac")
    dve = lambda fn, waits=(): E.emit("vector", fn, waits=waits, inc="dv")
    gpe = lambda fn, waits=(): E.emit("gpsimd", fn, waits=waits, inc="gp")

    def tap(name, src, waits):
        if name not in dbg:
            return
        if src.dtype == BF16:
            return
            n = src.shape[1]
            for c0 in range(0, n, 1024):
                cw = min(1024, n - c0)
                t = act(lambda e, src=src, c0=c0, cw=cw: e.activation(
                    out=dbg_stg[:, 0:cw], in_=src[:, c0:c0 + cw], func=AF.Copy),
                    waits=waits)
                td = dma("gpd", dbg[name][:, c0:c0 + cw],
                         dbg_stg[0:src.shape[0], 0:cw], waits=[("ac", t)])
                waits = [("gpd", td)]
        else:
            dma("gpd", dbg[name][:, :], src, waits=waits)

    rg = lambda h: (h % 2) * 64
    hp = lambda h: h // 2
    g8 = lambda t: t[:].rearrange("p (g i) -> p g i", g=8)
    g16 = lambda t: t[:].rearrange("p (g i) -> p g i", g=16)

    for dst, src in [(tri_s, c_tri), (id_s, c_id), (irep_s, c_irep),
                     (ones_s, c_ones), (mG_s, c_mG), (mH_s, c_mH), (mM_s, c_mM),
                     (sel_s, sel), (normw_s, normw), (scl_s, scl)]:
        dma("spd", dst[:, :], src[:, :])
    ld_c = E.tok("spd")
    for dst, src in [(wq_s, wq), (wk_s, wk), (wv_s, wv), (wo_s, wo)]:
        dma("gpd", dst[:].rearrange("p (kc c) -> p kc c", kc=8),
            src.rearrange("(kc p) c -> p kc c", p=128))
    dma("gpd", wsm_s[:].rearrange("p (kc c) -> p kc c", kc=8),
        wsm.rearrange("(kc p) c -> p kc c", p=128))
    dma("gpd", wg2_s[:, :], wg2[:, :])
    ld_w = E.tok("gpd")

    xw_tok = [None, None]
    for i in range(2):
        xw_tok[i] = dma(f"gpd{i}", xwb_[i][:].rearrange("p (kc t) -> p kc t", kc=NKC),
                        xT.rearrange("(kc p) t -> p kc t", p=128)
                        [:, :, i * 128:(i + 1) * 128], waits=[("gpd", ld_w)])

    def cast_chain():
        for (wsrc, wdst) in [(w1T, d_w1b), (w2T, d_w2b), (w3T, d_w3b),
                             (h1T, d_h1b), (h2T, d_h2b)]:
            tok = dma("cst", wdst[:, :], wsrc[:, :], waits=[("gpd", ld_w)])
        return tok

    cast_done = cast_chain()

    dve(lambda e: e.memset(S_ab[0][:, :], 0.0))
    dve(lambda e: e.tensor_copy(out=P_ab[0][:, :], in_=irep_s[:, :]),
        waits=[("spd", ld_c)])
    sp_tok = E.tok("dv")

    g_pe, g_dv, g_ac, g_spd = 0, 0, 0, 0
    s_evac = sp_tok
    p_evac = 0

    for pr in range(NPAIR_RUN):
        xwb = xwb_[pr % 2]
        Wg = [("pe", g_pe), ("dv", g_dv), ("ac", g_ac),
              (f"gpd{pr % 2}", xw_tok[pr % 2]),
              ("spd", E.tok("spd")), ("gpd", ld_w)]

        def proj(wt):
            for half, pm, bk in ((0, p0, "p0"), (1, p1, "p1")):
                for kc in range(NKC):
                    w = (Wg + BK[bk].for_write()) if kc == 0 else ()
                    t = mm(xwb[:, kc * 128:(kc + 1) * 128],
                           wt[:, kc * DM + half * 512: kc * DM + half * 512 + 512],
                           pm[:, :], kc == 0, kc == NKC - 1, waits=w)
                BK[bk].wrote("pe", t)

        proj(wq_s)
        tq1 = act(lambda e: e.activation(out=q_f[:, 0:512], in_=p0[:, :], func=AF.Silu),
                  waits=BK["p0"].for_read() + [("dv", g_dv)])
        BK["p0"].read("ac", tq1)
        tq2 = act(lambda e: e.activation(out=q_f[:, 512:1024], in_=p1[:, :], func=AF.Silu),
                  waits=BK["p1"].for_read())
        BK["p1"].read("ac", tq2)

        if PHASE == "projq" and pr == NPAIR_RUN - 1:
            E.ops["sync"].append(lambda e, n=E.count["spd"], sem=E.sems["spd"]: e.wait_ge(sem, n))
            E.ops["gpsimd"].append(lambda e, n=E.count["gpd"], sem=E.sems["gpd"]: e.wait_ge(sem, n))
            E.finalize()
            return nc
        proj(wk_s)
        tk1 = act(lambda e: e.activation(out=k_f[:, 0:512], in_=p0[:, :], func=AF.Silu),
                  waits=BK["p0"].for_read())
        BK["p0"].read("ac", tk1)
        tk2 = act(lambda e: e.activation(out=k_f[:, 512:1024], in_=p1[:, :], func=AF.Silu),
                  waits=BK["p1"].for_read())
        BK["p1"].read("ac", tk2)
        tap("t_q", q_f[:, :], [("ac", tq2)])
        tsq = act(lambda e: e.activation(out=ksq[:, :], in_=k_f[:, :], func=AF.Square), waits=[("ac", tk2)])
        trd = dve(lambda e: e.tensor_reduce(
            out=rn16[:].rearrange("p (g o) -> p g o", o=1),
            in_=g16(ksq), axis=mybir.AxisListType.X, op=ALU.add),
            waits=[("ac", tsq)])
        tmx = dve(lambda e: e.tensor_scalar(out=rn16[:, :], in0=rn16[:, :],
                                            scalar1=1e-24, scalar2=None, op0=ALU.max),
                  waits=[("dv", trd)])
        proj(wv_s)
        Uv3 = U_s[:].rearrange("p (h c) -> p h c", h=NH)
        tv1 = act(lambda e: e.activation(
            out=Uv3[:, 0:8, 0:64], in_=p0[:].rearrange("p (h c) -> p h c", c=64),
            func=AF.Silu), waits=BK["p0"].for_read())
        BK["p0"].read("ac", tv1)
        tv2 = act(lambda e: e.activation(
            out=Uv3[:, 8:16, 0:64], in_=p1[:].rearrange("p (h c) -> p h c", c=64),
            func=AF.Silu), waits=BK["p1"].for_read())
        BK["p1"].read("ac", tv2)

        if PHASE == "projv" and pr == NPAIR_RUN - 1:
            E.ops["sync"].append(lambda e, n=E.count["spd"], sem=E.sems["spd"]: e.wait_ge(sem, n))
            E.ops["gpsimd"].append(lambda e, n=E.count["gpd"], sem=E.sems["gpd"]: e.wait_ge(sem, n))
            E.finalize()
            return nc
        for kc in range(NKC):
            w = (Wg + BK["p2"].for_write()) if kc == 0 else ()
            tsm = mm(xwb[:, kc * 128:(kc + 1) * 128],
                     wsm_s[:, kc * 96:(kc + 1) * 96], p2[:, 0:96],
                     kc == 0, kc == NKC - 1, waits=w)
        BK["p2"].wrote("pe", tsm)
        tsg1 = act(lambda e: e.activation(out=sg_f[:, :], in_=p2[:, 0:16],
                                          func=AF.Sigmoid), waits=BK["p2"].for_read())
        tsg2 = act(lambda e: e.activation(out=nspg_s[:, :], in_=p2[:, 16:32],
                                          func=AF.Sigmoid))
        tg1 = act(lambda e, pr=pr: e.activation(out=g1_tm[:, pr * 64:(pr + 1) * 64],
                                                in_=p2[:, 32:96], func=AF.Copy))
        BK["p2"].read("ac", tg1)

        if PHASE == "projsm" and pr == NPAIR_RUN - 1:
            E.ops["sync"].append(lambda e, n=E.count["spd"], sem=E.sems["spd"]: e.wait_ge(sem, n))
            E.ops["gpsimd"].append(lambda e, n=E.count["gpd"], sem=E.sems["gpd"]: e.wait_ge(sem, n))
            E.finalize()
            return nc
        tln1 = act(lambda e: e.activation(out=nlf_s[:, :], in_=sg_f[:, :], func=AF.Ln),
                   waits=[("ac", tsg1)])
        tln2 = act(lambda e: e.activation(out=nspg_s[:, :], in_=nspg_s[:, :], func=AF.Ln),
                   waits=[("ac", tsg2)])
        tln3 = act(lambda e: e.activation(out=rn16[:, :], in_=rn16[:, :], func=AF.Ln),
                   waits=[("dv", tmx)])
        tneg1 = dve(lambda e: e.tensor_scalar(out=nlf_s[:, :], in0=nlf_s[:, :],
                                              scalar1=-1.0, scalar2=None, op0=ALU.mult),
                    waits=[("ac", tln1)])
        tneg2 = dve(lambda e: e.tensor_scalar(out=nspg_s[:, :], in0=nspg_s[:, :],
                                              scalar1=-1.0, scalar2=None, op0=ALU.mult),
                    waits=[("ac", tln2)])
        w0 = [("dv", tneg1)] + BK["p2"].for_write()
        mm(tri_s[0:64, :], nlf_s[0:64, :], p2[0:64, 96:112], True, True, waits=w0)
        mm(tri_s[64:128, :], nlf_s[64:128, :], p2[64:128, 96:112], True, True,
           tp=(64, 64))
        mm(ones_s[0:64, 0:1], nlf_s[0:64, :], p2[0:1, 112:128], True, True)
        cm4 = mm(ones_s[64:128, 0:1], nlf_s[64:128, :], p2[64:65, 128:144], True, True,
                 tp=(64, 64))
        BK["p2"].wrote("pe", cm4)
        tnlc = act(lambda e: e.activation(out=nlc_s[:, :], in_=p2[:, 96:112],
                                          func=AF.Copy), waits=BK["p2"].for_read())
        tncc = act(lambda e: e.activation(out=nlcc_s[0:1, 0:16], in_=p2[0:1, 112:128],
                                          func=AF.Copy))
        tncc = act(lambda e: e.activation(out=nlcc_s[64:65, 16:32],
                                          in_=p2[64:65, 128:144], func=AF.Copy))
        tnmv = dma("gpd", nlcc_s[0:1, 16:32], nlcc_s[64:65, 16:32],
                   waits=[("ac", tncc)])
        BK["p2"].read("ac", tncc)
        tap("t_nlc", nlc_s[:, :], [("ac", tnlc)])

        if PHASE == "cums" and pr == NPAIR_RUN - 1:
            E.ops["sync"].append(lambda e, n=E.count["spd"], sem=E.sems["spd"]: e.wait_ge(sem, n))
            E.ops["gpsimd"].append(lambda e, n=E.count["gpd"], sem=E.sems["gpd"]: e.wait_ge(sem, n))
            E.finalize()
            return nc

        wb = BK["p2"].for_write() + [("ac", tncc), ("gpd", tnmv)]
        mm(ones_s[0:1, 0:64], nlcc_s[0:1, 0:16], p2[0:64, 112:128], True, True, waits=wb)
        bc2 = mm(ones_s[0:1, 0:64], nlcc_s[0:1, 16:32], p2[64:128, 112:128], True,
                 True, tp=(0, 64))
        BK["p2"].wrote("pe", bc2)
        tdd = dve(lambda e: e.tensor_tensor(out=dd_s[:, :], in0=p2[:, 112:128],
                                            in1=nlc_s[:, :], op=ALU.subtract),
                  waits=BK["p2"].for_read() + [("ac", tncc)])
        tnn1 = dve(lambda e: e.tensor_tensor(out=nn_s[:, :], in0=nlc_s[:, :],
                                             in1=nlf_s[:, :], op=ALU.add),
                   waits=[("dv", tneg2)])
        tnn2 = dve(lambda e: e.tensor_tensor(out=nn_s[:, :], in0=nn_s[:, :],
                                             in1=nspg_s[:, :], op=ALU.add),
                   waits=[("dv", tnn1)])
        trg = dve(lambda e: e.tensor_tensor(out=rowg_s[:, :], in0=p2[:, 112:128],
                                            in1=nn_s[:, :], op=ALU.subtract),
                  waits=[("dv", tnn2)])
        BK["p2"].read("dv", trg)

        if PHASE == "rows" and pr == NPAIR_RUN - 1:
            E.ops["sync"].append(lambda e, n=E.count["spd"], sem=E.sems["spd"]: e.wait_ge(sem, n))
            E.ops["gpsimd"].append(lambda e, n=E.count["gpd"], sem=E.sems["gpd"]: e.wait_ge(sem, n))
            E.finalize()
            return nc
        te1 = act(lambda e: e.activation(out=edec_s[:, :], in_=dd_s[:, :], func=AF.Exp,
                                         scale=-1.0), waits=[("dv", trg), ("ac", tln3)])
        te2 = act(lambda e: e.activation(out=rowm_s[:, :], in_=dd_s[:, :], func=AF.Exp))
        te3 = act(lambda e: e.activation(out=el_s[:, :], in_=nlc_s[:, :], func=AF.Exp,
                                         scale=-1.0))
        te4 = act(lambda e: e.activation(out=bcol_s[:, :], in_=nn_s[:, :], func=AF.Exp,
                                         scale=-1.0))
        te5 = act(lambda e: e.activation(out=rowg_s[:, :], in_=rowg_s[:, :], func=AF.Exp))
        te6 = act(lambda e: e.activation(out=rn16[:, :], in_=rn16[:, :], func=AF.Exp,
                                         scale=-0.5))
        tfce = act(lambda e: e.activation(out=nlcc_s[0:1, 0:32], in_=nlcc_s[0:1, 0:32],
                                          func=AF.Exp, scale=-1.0),
                   waits=[("gpd", tnmv)])
        tbneg = dve(lambda e: e.tensor_scalar(out=bcol_s[:, :], in0=bcol_s[:, :],
                                              scalar1=-1.0, scalar2=None, op0=ALU.mult),
                    waits=[("ac", te4)])

        if PHASE == "exp" and pr == NPAIR_RUN - 1:
            E.ops["sync"].append(lambda e, n=E.count["spd"], sem=E.sems["spd"]: e.wait_ge(sem, n))
            E.ops["gpsimd"].append(lambda e, n=E.count["gpd"], sem=E.sems["gpd"]: e.wait_ge(sem, n))
            E.finalize()
            return nc
        tfr = dve(lambda e: e.tensor_copy(
            out=fcrow_s[0:1, 0:32].rearrange("p (c u e) -> p c u e", c=2, u=2),
            in_=nlcc_s[0:1, 0:32].rearrange("p (c e u) -> p c u e", c=2, e=8)),
            waits=[("ac", tfce)])

        if PHASE == "fcx1" and pr == NPAIR_RUN - 1:
            E.ops["sync"].append(lambda e, n=E.count["spd"], sem=E.sems["spd"]: e.wait_ge(sem, n))
            E.ops["gpsimd"].append(lambda e, n=E.count["gpd"], sem=E.sems["gpd"]: e.wait_ge(sem, n))
            E.finalize()
            return nc
        wf = BK["p2"].for_write() + [("dv", tfr)]
        mm(ones_s[0:1, 0:64], fcrow_s[0:1, 0:8], p2[0:64, 96:104], True, True,
           waits=wf)
        mm(ones_s[0:1, 0:64], fcrow_s[0:1, 8:16], p2[64:128, 96:104], True, True,
           tp=(0, 64))

        if PHASE == "fcx2" and pr == NPAIR_RUN - 1:
            E.ops["sync"].append(lambda e, n=E.count["spd"], sem=E.sems["spd"]: e.wait_ge(sem, n))
            E.ops["gpsimd"].append(lambda e, n=E.count["gpd"], sem=E.sems["gpd"]: e.wait_ge(sem, n))
            E.finalize()
            return nc
        mm(ones_s[0:1, 0:64], fcrow_s[0:1, 16:24], p2[0:64, 104:112], True, True)
        fb4 = mm(ones_s[0:1, 0:64], fcrow_s[0:1, 24:32], p2[64:128, 104:112],
                 True, True, tp=(0, 64))
        BK["p2"].wrote("pe", fb4)

        if PHASE == "fcx3" and pr == NPAIR_RUN - 1:
            E.ops["sync"].append(lambda e, n=E.count["spd"], sem=E.sems["spd"]: e.wait_ge(sem, n))
            E.ops["gpsimd"].append(lambda e, n=E.count["gpd"], sem=E.sems["gpd"]: e.wait_ge(sem, n))
            E.finalize()
            return nc
        tfc = dve(lambda e, pr=pr: e.tensor_copy(out=fcx_p[:, pr * 16:pr * 16 + 16],
                                                 in_=p2[:, 96:112]),
                  waits=BK["p2"].for_read())
        BK["p2"].read("dv", tfc)

        if PHASE == "proj" and pr == NPAIR_RUN - 1:
            E.ops["sync"].append(lambda e, n=E.count["spd"], sem=E.sems["spd"]: e.wait_ge(sem, n))
            E.ops["gpsimd"].append(lambda e, n=E.count["gpd"], sem=E.sems["gpd"]: e.wait_ge(sem, n))
            E.finalize()
            return nc
        tkn = dve(lambda e: e.tensor_tensor(out=g16(kn_f), in0=g16(k_f),
                                            in1=bcast(rn16[:], HD), op=ALU.mult),
                  waits=[("ac", te6)])
        dve(lambda e: e.tensor_tensor(out=g16(ktil_tm), in0=g16(kn_f),
                                      in1=bcast(edec_s[:], HD), op=ALU.mult),
            waits=[("ac", te1), ("dv", tkn)])
        dve(lambda e: e.tensor_tensor(out=g16(kg_tm), in0=g16(kn_f),
                                      in1=bcast(rowg_s[:], HD), op=ALU.mult),
            waits=[("ac", te5), ("dv", tkn)])
        tbk = dve(lambda e: e.tensor_tensor(out=Uv3[:, :, 64:128], in0=g16(kn_f),
                                            in1=bcast(bcol_s[:], HD), op=ALU.mult),
                  waits=[("dv", tbneg), ("dv", tkn), ("ac", tv2)])
        dve(lambda e: e.tensor_tensor(out=g16(qm_tm), in0=g16(q_f),
                                      in1=bcast(rowm_s[:], HD), op=ALU.mult),
            waits=[("ac", te2)])
        dve(lambda e: e.tensor_tensor(out=g16(qf_tm), in0=g16(q_f),
                                      in1=bcast(el_s[:], HD), op=ALU.mult),
            waits=[("ac", te3)])
        tscal = E.tok("dv")
        tap("t_kn", kn_f[:, :], [("dv", tkn)])

        if PHASE == "scal" and pr == NPAIR_RUN - 1:
            E.ops["sync"].append(lambda e, n=E.count["spd"], sem=E.sems["spd"]: e.wait_ge(sem, n))
            E.ops["gpsimd"].append(lambda e, n=E.count["gpd"], sem=E.sems["gpd"]: e.wait_ge(sem, n))
            E.finalize()
            return nc

        for src, dstf in [(ktil_tm, ktil_f), (kg_tm, kg_f), (qm_tm, qm_f),
                          (qf_tm, qf_f)]:
            for hh in range(8):
                w = ([("dv", tscal)] + BK["p3"].for_write()) if hh == 0 else ()
                tt = trans(src[:, hh * 128:(hh + 1) * 128],
                           p3[:, hh * 128:(hh + 1) * 128], id_s[:, :], waits=w)
            BK["p3"].wrote("pe", tt)
            tev = act(lambda e, d=dstf: e.activation(out=d[:, :], in_=p3[:, :],
                                                     func=AF.Copy),
                      waits=BK["p3"].for_read())
            BK["p3"].read("ac", tev)
        t_fm = E.tok("ac")

        if PHASE == "tmfm" and pr == NPAIR_RUN - 1:
            E.ops["sync"].append(lambda e, n=E.count["spd"], sem=E.sems["spd"]: e.wait_ge(sem, n))
            E.ops["gpsimd"].append(lambda e, n=E.count["gpd"], sem=E.sems["gpd"]: e.wait_ge(sem, n))
            E.finalize()
            return nc

        def ghm(lf, rf, pm, pname, extra):
            first = True
            for par in range(2):
                for ci in range(2):
                    for h in range(par, NH, 2):
                        w = (extra + BK[pname].for_write()) if first else ()
                        first = False
                        t = mm(lf[rg(h):rg(h) + 64,
                                  hp(h) * 128 + ci * 64: hp(h) * 128 + ci * 64 + 64],
                               rf[rg(h):rg(h) + 64,
                                  hp(h) * 128 + ci * 64: hp(h) * 128 + ci * 64 + 64],
                               pm[ci * 64:ci * 64 + 64, h * 64:h * 64 + 64],
                               True, True, tp=(rg(h), ci * 64), waits=w)
            BK[pname].wrote("pe", t)
            return t

        ghm(ktil_f, kg_f, p4, "p4", [("ac", t_fm)])

        if PHASE == "gmm" and pr == NPAIR_RUN - 1:
            E.ops["sync"].append(lambda e, n=E.count["spd"], sem=E.sems["spd"]: e.wait_ge(sem, n))
            E.ops["gpsimd"].append(lambda e, n=E.count["gpd"], sem=E.sems["gpd"]: e.wait_ge(sem, n))
            E.finalize()
            return nc
        ghm(kg_f, ktil_f, p5, "p5", [])

        if PHASE == "hmm" and pr == NPAIR_RUN - 1:
            E.ops["sync"].append(lambda e, n=E.count["spd"], sem=E.sems["spd"]: e.wait_ge(sem, n))
            E.ops["gpsimd"].append(lambda e, n=E.count["gpd"], sem=E.sems["gpd"]: e.wait_ge(sem, n))
            E.finalize()
            return nc
        tge = dve(lambda e: e.tensor_tensor(
            out=G_sb[0][:, :], in0=p4[:, :],
            in1=mG_s[:, :], op=ALU.mult),
            waits=BK["p4"].for_read())
        BK["p4"].read("dv", tge)
        the = dve(lambda e: e.tensor_tensor(
            out=H_sb[0][:, :], in0=p5[:, :],
            in1=mH_s[:, :], op=ALU.mult),
            waits=BK["p5"].for_read())
        BK["p5"].read("dv", the)

        if PHASE == "gev2" and pr == NPAIR_RUN - 1:
            E.ops["sync"].append(lambda e, n=E.count["spd"], sem=E.sems["spd"]: e.wait_ge(sem, n))
            E.ops["gpsimd"].append(lambda e, n=E.count["gpd"], sem=E.sems["gpd"]: e.wait_ge(sem, n))
            E.finalize()
            return nc
        ghm(ktil_f, kg_f if PHASE == "gev4" else qm_f, p4, "p4", [])

        if PHASE == "gev" and pr == NPAIR_RUN - 1:
            E.ops["sync"].append(lambda e, n=E.count["spd"], sem=E.sems["spd"]: e.wait_ge(sem, n))
            E.ops["gpsimd"].append(lambda e, n=E.count["gpd"], sem=E.sems["gpd"]: e.wait_ge(sem, n))
            E.finalize()
            return nc

        if PHASE in ("gev3", "gev4") and pr == NPAIR_RUN - 1:
            E.ops["sync"].append(lambda e, n=E.count["spd"], sem=E.sems["spd"]: e.wait_ge(sem, n))
            E.ops["gpsimd"].append(lambda e, n=E.count["gpd"], sem=E.sems["gpd"]: e.wait_ge(sem, n))
            E.finalize()
            return nc
        tme = dve(lambda e: e.tensor_tensor(
            out=MT_sb[:, :], in0=p4[:, :],
            in1=mM_s[:, :], op=ALU.mult),
            waits=BK["p4"].for_read())
        BK["p4"].read("dv", tme)
        tap("t_G", G_sb[0][:, :], [("dv", tge)])
        tap("t_MT", MT_sb[:, :], [("dv", tme)])

        if PHASE == "ghm" and pr == NPAIR_RUN - 1:
            E.ops["sync"].append(lambda e, n=E.count["spd"], sem=E.sems["spd"]: e.wait_ge(sem, n))
            E.ops["gpsimd"].append(lambda e, n=E.count["gpd"], sem=E.sems["gpd"]: e.wait_ge(sem, n))
            E.finalize()
            return nc


        cur = 0
        u_dep = [("dv", tge), ("dv", the), ("dv", tbk), ("ac", tv2)]
        for s in range(6):
            first = True
            for ci in range(2):
                for h in range(NH):
                    pm = p4 if h < 8 else p5
                    col = (h % 8) * 128
                    w = ()
                    if first:
                        w = u_dep + BK["p4"].for_write() + BK["p5"].for_write()
                        first = False
                    t = mm(G_sb[cur][ci * 64:ci * 64 + 64, h * 64:h * 64 + 64],
                           U_s[ci * 64:ci * 64 + 64, h * 128:h * 128 + 128],
                           pm[ci * 64:ci * 64 + 64, col:col + 128],
                           True, True, tp=(ci * 64, ci * 64), waits=w)
            BK["p4"].wrote("pe", t)
            BK["p5"].wrote("pe", t)
            ua = dve(lambda e: e.tensor_tensor(out=U_s[:, 0:1024], in0=p4[:, :],
                                               in1=U_s[:, 0:1024], op=ALU.add),
                     waits=BK["p4"].for_read())
            BK["p4"].read("dv", ua)
            ub = dve(lambda e: e.tensor_tensor(out=U_s[:, 1024:2048], in0=p5[:, :],
                                               in1=U_s[:, 1024:2048], op=ALU.add),
                     waits=BK["p5"].for_read())
            BK["p5"].read("dv", ub)
            u_dep = [("dv", ub)]
            if s < 5:
                first = True
                for ci in range(2):
                    for h in range(NH):
                        w = ()
                        if first:
                            w = [("dv", ua), ("dv", ub)] + BK["p4"].for_write()
                            first = False
                        t = mm(H_sb[cur][ci * 64:ci * 64 + 64, h * 64:h * 64 + 64],
                               G_sb[cur][ci * 64:ci * 64 + 64, h * 64:h * 64 + 64],
                               p4[ci * 64:ci * 64 + 64, h * 64:h * 64 + 64],
                               True, True, tp=(ci * 64, ci * 64), waits=w)
                BK["p4"].wrote("pe", t)
                tgs = act(lambda e, nx=1 - cur: e.activation(out=G_sb[nx][:, :],
                                                             in_=p4[:, :], func=AF.Copy),
                          waits=BK["p4"].for_read())
                BK["p4"].read("ac", tgs)
                if s < 4:
                    first = True
                    for ci in range(2):
                        for h in range(NH):
                            w = ()
                            if first:
                                w = BK["p5"].for_write()
                                first = False
                            t = mm(G_sb[cur][ci * 64:ci * 64 + 64, h * 64:h * 64 + 64],
                                   H_sb[cur][ci * 64:ci * 64 + 64, h * 64:h * 64 + 64],
                                   p5[ci * 64:ci * 64 + 64, h * 64:h * 64 + 64],
                                   True, True, tp=(ci * 64, ci * 64), waits=w)
                    BK["p5"].wrote("pe", t)
                    ths = act(lambda e, nx=1 - cur: e.activation(out=H_sb[nx][:, :],
                                                                 in_=p5[:, :],
                                                                 func=AF.Copy),
                              waits=BK["p5"].for_read())
                    BK["p5"].read("ac", ths)
                cur = 1 - cur
        u_done = u_dep[0][1]
        tap("t_U", U_s[:, :], [("dv", u_done)])

        if PHASE == "solve" and pr == NPAIR_RUN - 1:
            E.ops["sync"].append(lambda e, n=E.count["spd"], sem=E.sems["spd"]: e.wait_ge(sem, n))
            E.ops["gpsimd"].append(lambda e, n=E.count["gpd"], sem=E.sems["gpd"]: e.wait_ge(sem, n))
            E.finalize()
            return nc


        def bak(lsl, rsl, pm, pname, extra):
            first = True
            for ci in range(2):
                for h in range(NH):
                    w = (extra + BK[pname].for_write()) if first else ()
                    first = False
                    t = mm(lsl(ci, h), rsl(ci, h),
                           pm[rg(h):rg(h) + 64,
                              hp(h) * 128 + ci * 64: hp(h) * 128 + ci * 64 + 64],
                           True, True, tp=(ci * 64, rg(h)), waits=w)
            BK[pname].wrote("pe", t)
            return t

        bak(lambda ci, h: ktil_tm[ci * 64:ci * 64 + 64, h * 64:h * 64 + 64],
            lambda ci, h: U_s[ci * 64:ci * 64 + 64, h * 128:h * 128 + 64],
            p4, "p4", [("dv", u_done)])
        tbe = act(lambda e: e.activation(out=B_sb[:, :], in_=p4[:, :], func=AF.Copy),
                  waits=BK["p4"].for_read())
        BK["p4"].read("ac", tbe)
        bak(lambda ci, h: U_s[ci * 64:ci * 64 + 64, h * 128 + 64:h * 128 + 128],
            lambda ci, h: ktil_tm[ci * 64:ci * 64 + 64, h * 64:h * 64 + 64],
            p5, "p5", [])
        tae = act(lambda e: e.activation(out=AT_sb[:, :], in_=p5[:, :], func=AF.Copy),
                  waits=BK["p5"].for_read() + [("spd", E.tok("spd"))])
        BK["p5"].read("ac", tae)
        for ci in range(2):
            dma("spd", d_A[2 * pr + ci, :, :].rearrange("p (g i) -> p g i", g=8),
                AT_sb[:].rearrange("p (g c i) -> p g c i", g=8, c=2)[:, :, ci, :],
                waits=[("ac", tae)])
        bak(lambda ci, h: U_s[ci * 64:ci * 64 + 64, h * 128 + 64:h * 128 + 128],
            lambda ci, h: MT_sb[ci * 64:ci * 64 + 64, h * 64:h * 64 + 64],
            p4, "p4", [("dv", tme)])
        tqe = dve(lambda e: e.tensor_tensor(out=Qt_sb[:, :], in0=p4[:, :],
                                            in1=qf_f[:, :], op=ALU.add),
                  waits=BK["p4"].for_read() + [("ac", t_fm), ("spd", E.tok("spd"))])
        BK["p4"].read("dv", tqe)
        for ci in range(2):
            dma("spd", d_Qt[2 * pr + ci, :, :].rearrange("p (g i) -> p g i", g=8),
                Qt_sb[:].rearrange("p (g c i) -> p g c i", g=8, c=2)[:, :, ci, :],
                waits=[("dv", tqe)])
        tap("t_AT", AT_sb[:, :], [("ac", tae)])
        tap("t_B", B_sb[:, :], [("ac", tbe)])
        tap("t_Qt", Qt_sb[:, :], [("dv", tqe)])

        if PHASE == "bak" and pr == NPAIR_RUN - 1:
            E.ops["sync"].append(lambda e, n=E.count["spd"], sem=E.sems["spd"]: e.wait_ge(sem, n))
            E.ops["gpsimd"].append(lambda e, n=E.count["gpd"], sem=E.sems["gpd"]: e.wait_ge(sem, n))
            E.finalize()
            return nc


        for ci in range(2):
            c = 2 * pr + ci
            Scur, Snxt = S_ab[c % 2], S_ab[(c + 1) % 2]
            Pcur, Pnxt = P_ab[c % 2], P_ab[(c + 1) % 2]
            wS = [("dv", s_evac), ("ac", p_evac), ("dv", tfc)]
            fsl = fcx_p[:, c * 8:(c + 1) * 8]
            tSF = dve(lambda e, S=Scur, f=fsl: e.tensor_tensor(
                out=g8(SF_t), in0=g8(S), in1=bcast(f, 64), op=ALU.mult), waits=wS)
            tPF = dve(lambda e, P=Pcur, f=fsl: e.tensor_tensor(
                out=g8(PF_t), in0=g8(P), in1=bcast(f, 64), op=ALU.mult))
            first = True
            for h in range(NH):
                a_sl = AT_sb[rg(h):rg(h) + 64,
                             hp(h) * 128 + ci * 64: hp(h) * 128 + ci * 64 + 64]
                w = ([("dv", tPF), ("ac", tae)] + BK["p2"].for_write() +
                     BK["p0"].for_write()) if first else ()
                first = False
                mm(a_sl, Scur[rg(h):rg(h) + 64, hp(h) * 64: hp(h) * 64 + 64],
                   p2[rg(h):rg(h) + 64, hp(h) * 64: hp(h) * 64 + 64],
                   True, False, tp=(rg(h), rg(h)), waits=w)
                mm(id_s[rg(h):rg(h) + 64, rg(h):rg(h) + 64],
                   SF_t[rg(h):rg(h) + 64, hp(h) * 64: hp(h) * 64 + 64],
                   p2[rg(h):rg(h) + 64, hp(h) * 64: hp(h) * 64 + 64],
                   False, True, tp=(rg(h), rg(h)))
                mm(a_sl, Pcur[rg(h):rg(h) + 64, hp(h) * 64: hp(h) * 64 + 64],
                   p0[rg(h):rg(h) + 64, hp(h) * 64: hp(h) * 64 + 64],
                   True, False, tp=(rg(h), rg(h)))
                tpl = mm(id_s[rg(h):rg(h) + 64, rg(h):rg(h) + 64],
                         PF_t[rg(h):rg(h) + 64, hp(h) * 64: hp(h) * 64 + 64],
                         p0[rg(h):rg(h) + 64, hp(h) * 64: hp(h) * 64 + 64],
                         False, True, tp=(rg(h), rg(h)))
            BK["p2"].wrote("pe", tpl)
            BK["p0"].wrote("pe", tpl)
            s_evac = dve(lambda e, S=Snxt, ci=ci: e.tensor_tensor(
                out=g8(S), in0=g8(p2),
                in1=B_sb[:].rearrange("p (g c i) -> p g c i", g=8, c=2)[:, :, ci, :],
                op=ALU.add), waits=BK["p2"].for_read() + [("ac", tbe)])
            BK["p2"].read("dv", s_evac)
            p_evac = act(lambda e, P=Pnxt: e.activation(out=P[:, :], in_=p0[:, :],
                                                        func=AF.Copy),
                         waits=BK["p0"].for_read())
            BK["p0"].read("ac", p_evac)
            first = True
            for h in range(NH):
                w = ([("dv", tqe)] + BK["p5"].for_write()) if first else ()
                first = False
                mm(Scur[rg(h):rg(h) + 64, hp(h) * 64: hp(h) * 64 + 64],
                   Qt_sb[rg(h):rg(h) + 64,
                         hp(h) * 128 + ci * 64: hp(h) * 128 + ci * 64 + 64],
                   p5[rg(h):rg(h) + 64, hp(h) * 64: hp(h) * 64 + 64],
                   True, True, tp=(rg(h), rg(h)), waits=w)
            for h in range(NH):
                to2 = mm(U_s[ci * 64:ci * 64 + 64, h * 128:h * 128 + 64],
                         MT_sb[ci * 64:ci * 64 + 64, h * 64:h * 64 + 64],
                         p5[rg(h):rg(h) + 64,
                            512 + hp(h) * 64: 512 + hp(h) * 64 + 64],
                         True, True, tp=(ci * 64, rg(h)))
            BK["p5"].wrote("pe", to2)
            to2e = act(lambda e: e.activation(out=o2_t[:, :], in_=p5[:, 512:1024],
                                              func=AF.Copy),
                       waits=BK["p5"].for_read())
            toe = dve(lambda e: e.tensor_tensor(out=ostg[:, :], in0=p5[:, 0:512],
                                                in1=o2_t[:, :], op=ALU.add),
                      waits=BK["p5"].for_read() + [("ac", to2e), ("spd", E.tok("spd"))])
            BK["p5"].read("dv", toe)
            tos = dma("spd", d_out[c, :, :], ostg[:, :], waits=[("dv", toe)])
            g_spd = tos
        tap("t_S", S_ab[NCH % 2][:, :] if pr == NPAIR - 1 else S_ab[0][:, :],
            [("dv", s_evac)] if pr == NPAIR - 1 else [("dv", s_evac)])

        if pr + 2 < NPAIR:
            xw_tok[pr % 2] = dma(
                f"gpd{pr % 2}", xwb_[pr % 2][:].rearrange("p (kc t) -> p kc t", kc=NKC),
                xT.rearrange("(kc p) t -> p kc t", p=128)
                [:, :, (pr + 2) * 128:(pr + 3) * 128],
                waits=[("pe", E.tok("pe"))])
        g_pe, g_dv, g_ac = E.tok("pe"), E.tok("dv"), E.tok("ac")

    tap("t_fcx", fcx_p[:, :], [("ac", g_ac)])

    if PHASE == "pairs":
        E.ops["sync"].append(lambda e, n=E.count["spd"], sem=E.sems["spd"]: e.wait_ge(sem, n))
        E.ops["gpsimd"].append(lambda e, n=E.count["gpd"], sem=E.sems["gpd"]: e.wait_ge(sem, n))
        E.finalize()
        return nc

    Pfin = P_ab[NCH % 2]
    first = True
    for h in range(NH):
        w = ([("ac", p_evac)] + BK["p3"].for_write()) if first else ()
        first = False
        tl = trans(Pfin[rg(h):rg(h) + 64, hp(h) * 64: hp(h) * 64 + 64],
                   p3[rg(h):rg(h) + 64, hp(h) * 64: hp(h) * 64 + 64],
                   id_s[rg(h):rg(h) + 64, rg(h):rg(h) + 64],
                   tp=(rg(h), rg(h)), waits=w)
    BK["p3"].wrote("pe", tl)
    tpt = act(lambda e: e.activation(out=ptseg[:, :], in_=p3[:, 0:512], func=AF.Copy),
              waits=BK["p3"].for_read())
    BK["p3"].read("ac", tpt)
    dma("spd", ccin[:, 0:512], ptseg[:, :], waits=[("ac", tpt)])
    dma("spd", ccin[:, 512:1024], S_ab[NCH % 2][:, :], waits=[("dv", s_evac)])
    cc_w = E.tok("spd")
    tcc = gpe(lambda e: e.collective_compute(
        "AllGather", ALU.bypass, replica_groups=[[0, 1, 2, 3], [4, 5, 6, 7]],
        ins=[ccin[:, :]], outs=[ccout[:, :]]), waits=[("spd", cc_w)])
    tgl = dma("spd", gat_s[:].rearrange("p (r c) -> p r c", r=4),
              ccout.rearrange("(r p) c -> p r c", p=128),
              waits=[("gp", tcc)])
    tz = dve(lambda e: e.memset(rpj[0][:, :], 0.0))
    last_dv = tz
    for j in range(3):
        first = True
        for h in range(NH):
            w = ([("spd", tgl), ("dv", last_dv)] + BK["p2"].for_write()) if first else ()
            first = False
            tcj = mm(gat_s[rg(h):rg(h) + 64,
                           j * 1024 + hp(h) * 64: j * 1024 + hp(h) * 64 + 64],
                     rpj[j][rg(h):rg(h) + 64, hp(h) * 64: hp(h) * 64 + 64],
                     p2[rg(h):rg(h) + 64, hp(h) * 64: hp(h) * 64 + 64],
                     True, True, tp=(rg(h), rg(h)), waits=w)
        BK["p2"].wrote("pe", tcj)
        last_dv = dve(lambda e, j=j: e.tensor_tensor(
            out=rpj[j + 1][:, :], in0=p2[:, :],
            in1=gat_s[:, j * 1024 + 512: j * 1024 + 1024], op=ALU.add),
            waits=BK["p2"].for_read() + [("spd", tgl)])
        BK["p2"].read("dv", last_dv)
    dve(lambda e: e.tensor_scalar(out=sin_s[:, :], in0=rpj[1][:, :],
                                  scalar1=sel_s[:, 1:2], scalar2=None, op0=ALU.mult),
        waits=[("dv", last_dv), ("spd", ld_c)])
    dve(lambda e: e.scalar_tensor_tensor(out=sin_s[:, :], in0=rpj[2][:, :],
                                         scalar=sel_s[:, 2:3], in1=sin_s[:, :],
                                         op0=ALU.mult, op1=ALU.add),
        waits=[("dv", E.tok("dv"))])
    dve(lambda e: e.scalar_tensor_tensor(out=sin_s[:, :], in0=rpj[3][:, :],
                                         scalar=sel_s[:, 3:4], in1=sin_s[:, :],
                                         op0=ALU.mult, op1=ALU.add),
        waits=[("dv", E.tok("dv"))])
    tsin = dve(lambda e: e.tensor_copy(out=dlt[0][:, :], in_=sin_s[:, :]),
               waits=[("dv", E.tok("dv"))])
    tap("t_sin", sin_s[:, :], [("dv", tsin)])

    if PHASE == "gather":
        E.ops["sync"].append(lambda e, n=E.count["spd"], sem=E.sems["spd"]: e.wait_ge(sem, n))
        E.ops["gpsimd"].append(lambda e, n=E.count["gpd"], sem=E.sems["gpd"]: e.wait_ge(sem, n))
        E.finalize()
        return nc

    qa_tok = [0, 0]
    dl_ev = tsin
    for c in range(2):
        dma(f"spd{c}", qtc[c][:, :], d_Qt[c, :, :], waits=[("spd", E.tok("spd"))])
        dma(f"spd{c}", atc[c][:, :], d_A[c, :, :])
        qa_tok[c] = dma(f"spd{c}", och[c][:, :], d_out[c, :, :])
    for c in range(NCH):
        ri = c % 2
        first = True
        for h in range(NH):
            w = ([(f"spd{ri}", qa_tok[ri]), ("dv", dl_ev if c == 0 else 0),
                  ("ac", 0 if c == 0 else dl_ev)] + BK["p5"].for_write()) if first else ()
            first = False
            tocm = mm(dlt[c % 2][rg(h):rg(h) + 64, hp(h) * 64: hp(h) * 64 + 64],
                      qtc[ri][rg(h):rg(h) + 64, hp(h) * 64: hp(h) * 64 + 64],
                      p5[rg(h):rg(h) + 64, hp(h) * 64: hp(h) * 64 + 64],
                      True, True, tp=(rg(h), rg(h)), waits=w)
        BK["p5"].wrote("pe", tocm)
        tca = dve(lambda e, ri=ri: e.tensor_tensor(out=och[ri][:, :], in0=p5[:, 0:512],
                                                   in1=och[ri][:, :], op=ALU.add),
                  waits=BK["p5"].for_read())
        BK["p5"].read("dv", tca)
        tcs = dma(f"spd{ri}", d_out[c, :, :], och[ri][:, :], waits=[("dv", tca)])
        tdf = dve(lambda e, c=c: e.tensor_tensor(
            out=g8(PF_t), in0=g8(dlt[c % 2]),
            in1=bcast(fcx_p[:, c * 8:(c + 1) * 8], 64), op=ALU.mult),
            waits=[("dv", tsin)] if c == 0 else [("ac", dl_ev), ("dv", tca)])
        first = True
        for h in range(NH):
            w = ([("dv", tdf)] + BK["p2"].for_write()) if first else ()
            first = False
            mm(atc[ri][rg(h):rg(h) + 64, hp(h) * 64: hp(h) * 64 + 64],
               dlt[c % 2][rg(h):rg(h) + 64, hp(h) * 64: hp(h) * 64 + 64],
               p2[rg(h):rg(h) + 64, hp(h) * 64: hp(h) * 64 + 64],
               True, False, tp=(rg(h), rg(h)), waits=w)
            tdm = mm(id_s[rg(h):rg(h) + 64, rg(h):rg(h) + 64],
                     PF_t[rg(h):rg(h) + 64, hp(h) * 64: hp(h) * 64 + 64],
                     p2[rg(h):rg(h) + 64, hp(h) * 64: hp(h) * 64 + 64],
                     False, True, tp=(rg(h), rg(h)))
        BK["p2"].wrote("pe", tdm)
        dl_ev = act(lambda e, nx=(c + 1) % 2: e.activation(out=dlt[nx][:, :],
                                                           in_=p2[:, :], func=AF.Copy),
                    waits=BK["p2"].for_read())
        BK["p2"].read("ac", dl_ev)
        if c + 2 < NCH:
            dma(f"spd{ri}", qtc[ri][:, :], d_Qt[c + 2, :, :],
                waits=[("pe", tocm), ("dv", tca), ("ac", dl_ev),
                       (f"spd{ri}", tcs)])
            dma(f"spd{ri}", atc[ri][:, :], d_A[c + 2, :, :])
            qa_tok[ri] = dma(f"spd{ri}", och[ri][:, :], d_out[c + 2, :, :])
    corr_dma = E.tok("spd")
    corr_dma01 = (E.tok("spd0"), E.tok("spd1"))
    if PHASE == "corr":
        E.ops["sync"].append(lambda e, n=E.count["spd"], sem=E.sems["spd"]: e.wait_ge(sem, n))
        E.ops["gpsimd"].append(lambda e, n=E.count["gpd"], sem=E.sems["gpd"]: e.wait_ge(sem, n))
        E.finalize()
        return nc
    ph1 = [("pe", E.tok("pe")), ("dv", E.tok("dv")), ("ac", E.tok("ac")),
           ("spd", E.tok("spd")), ("gpd", E.tok("gpd")), ("gp", E.tok("gp"))]

    def ln_stats(buf, n, sq_ap, apply_list, wtok_pe, prev_reads):
        first = True
        for kc in range(n):
            w = (prev_reads + BK["p0"].for_write()) if first else ()
            first = False
            t1 = mm(ones_s[:, 0:1], buf[:, kc * 512:(kc + 1) * 512],
                    p0[0:1, 0:512], kc == 0, kc == n - 1, waits=w)
        BK["p0"].wrote("pe", t1)
        first = True
        for kc in range(n):
            w = (BK["p1"].for_write() + [("ac", sq_ap)]) if first else ()
            first = False
            t2 = mm(ones_s[:, 0:1], sq_w[:, kc * 512:(kc + 1) * 512],
                    p1[0:1, 0:512], kc == 0, kc == n - 1, waits=w)
        BK["p1"].wrote("pe", t2)
        tmu = dve(lambda e: e.tensor_scalar(out=row_t[0:1, 0:512], in0=p0[0:1, :],
                                            scalar1=1.0 / DM, scalar2=None,
                                            op0=ALU.mult), waits=BK["p0"].for_read())
        BK["p0"].read("dv", tmu)
        tv_ = dve(lambda e: e.tensor_scalar(out=row_t[0:1, 512:1024], in0=p1[0:1, :],
                                            scalar1=1.0 / DM, scalar2=None,
                                            op0=ALU.mult), waits=BK["p1"].for_read())
        BK["p1"].read("dv", tv_)
        t_r = dve(lambda e: e.tensor_tensor(out=row_t[0:1, 1024:1536],
                                            in0=row_t[0:1, 0:512], in1=row_t[0:1, 0:512],
                                            op=ALU.mult), waits=[("dv", tmu)])
        t_r = dve(lambda e: e.tensor_tensor(out=row_t[0:1, 512:1024],
                                            in0=row_t[0:1, 512:1024],
                                            in1=row_t[0:1, 1024:1536], op=ALU.subtract),
                  waits=[("dv", t_r)])
        tva = dve(lambda e: e.tensor_scalar(out=row_t[0:1, 512:1024],
                                            in0=row_t[0:1, 512:1024], scalar1=1e-5,
                                            scalar2=None, op0=ALU.add),
                  waits=[("dv", t_r)])
        tsd = act(lambda e: e.activation(out=row_t[0:1, 1024:1536],
                                         in_=row_t[0:1, 512:1024], func=AF.Sqrt),
                  waits=[("dv", tva)])
        tiv = dve(lambda e: e.reciprocal(out=row_t[0:1, 512:1024],
                                         in_=row_t[0:1, 1024:1536]),
                  waits=[("ac", tsd)])
        wb = BK["p4"].for_write() + [("dv", tiv)]
        mm(ones_s[0:1, :], row_t[0:1, 0:512], p4[:, 0:512], True, True, waits=wb)
        tb2 = mm(ones_s[0:1, :], row_t[0:1, 512:1024], p4[:, 512:1024], True, True)
        BK["p4"].wrote("pe", tb2)
        return tb2

    x1_dma = []
    for w in range(NWIN):
        t0 = w * 512
        tx = dma("spd", xf_w[:].rearrange("p (kc t) -> p kc t", kc=NKC),
                 xT.rearrange("(kc p) t -> p kc t", p=128)[:, :, t0:t0 + 512],
                 waits=(ph1 + [("spd0", corr_dma01[0]), ("spd1", corr_dma01[1])])
                 if w == 0 else [("spd", corr_dma)])
        tow = dma("spd", outw[:].rearrange("p (kc c i) -> p kc c i", kc=NKC, c=8),
                  d_out[w * 8:(w + 1) * 8, :, :]
                  .rearrange("c p (kc i) -> p kc c i", kc=8),
                  waits=[("spd", corr_dma)])
        first = True
        for i in range(4):
            wt = ((ph1 if w == 0 else []) + BK["p3"].for_write()) if first else ()
            first = False
            tg = trans(g1_tm[:, (w * 4 + i) * 64:(w * 4 + i + 1) * 64],
                       p3[0:64, i * 128:(i + 1) * 128], id_s[:, :], waits=wt)
        BK["p3"].wrote("pe", tg)
        tgf = act(lambda e: e.activation(out=g1f_w[:, :], in_=p3[0:64, 0:512],
                                         func=AF.Copy), waits=BK["p3"].for_read())
        BK["p3"].read("ac", tgf)
        for kc in range(NKC):
            pm, bk = (p0, "p0") if kc % 2 == 0 else (p1, "p1")
            wt = [("ac", tgf)] + BK[bk].for_write()
            tgm = mm(wg2_s[:, kc * 128:(kc + 1) * 128], g1f_w[:, :], pm[:, :],
                     True, True, waits=wt)
            BK[bk].wrote("pe", tgm)
            tsg = act(lambda e, kc=kc, pm=pm: e.activation(
                out=gate_w[:, kc * 512:(kc + 1) * 512], in_=pm[:, :],
                func=AF.Sigmoid), waits=BK[bk].for_read())
            BK[bk].read("ac", tsg)
        tgate = E.tok("ac")
        tap("t_gate", gate_w[:, :], [("ac", tgate)]) if w == 0 else None
        tog = dve(lambda e: e.tensor_tensor(out=og_w[:, :], in0=outw[:, :],
                                            in1=gate_w[:, :], op=ALU.mult),
                  waits=[("ac", tgate), ("spd", tow)])
        tsq = act(lambda e: e.activation(out=sq_w[:, :], in_=og_w[:, :],
                                         func=AF.Square), waits=[("dv", tog)])
        tb2 = ln_stats(og_w, 8, tsq, None, None, [("dv", tog)])
        for kc in range(NKC):
            t_ = dve(lambda e, kc=kc: e.tensor_tensor(
                out=og_w[:, kc * 512:(kc + 1) * 512],
                in0=og_w[:, kc * 512:(kc + 1) * 512], in1=p4[:, 0:512],
                op=ALU.subtract),
                waits=(BK["p4"].for_read() + [("ac", tsq)]) if kc == 0 else ())
            t_ = dve(lambda e, kc=kc: e.scalar_tensor_tensor(
                out=on_w[:, kc * 512:(kc + 1) * 512],
                in0=og_w[:, kc * 512:(kc + 1) * 512],
                scalar=normw_s[:, kc:kc + 1], in1=p4[:, 512:1024],
                op0=ALU.mult, op1=ALU.mult), waits=[("dv", t_)])
        BK["p4"].read("dv", t_)
        ton = t_
        for m in range(NKC):
            pm, bk = (p2, "p2") if m % 2 == 0 else (p5, "p5")
            pslice = pm[:, 0:512]
            first = True
            for kc in range(NKC):
                wt = ([("dv", ton), ("spd", tow)] + BK[bk].for_write()) if first else ()
                first = False
                twm = mm(wo_s[:, kc * DM + m * 128: kc * DM + m * 128 + 128],
                         on_w[:, kc * 512:(kc + 1) * 512], pslice,
                         kc == 0, kc == NKC - 1, waits=wt)
            BK[bk].wrote("pe", twm)
            tr1 = dve(lambda e, m=m, pslice=pslice: e.tensor_tensor(
                out=x1f_w[:, m * 512:(m + 1) * 512], in0=pslice,
                in1=xf_w[:, m * 512:(m + 1) * 512], op=ALU.add),
                waits=BK[bk].for_read())
            BK[bk].read("dv", tr1)
        tsq2 = act(lambda e: e.activation(out=sq_w[:, :], in_=x1f_w[:, :],
                                          func=AF.Square), waits=[("dv", tr1)])
        tb2 = ln_stats(x1f_w, 8, tsq2, None, None, [("dv", tr1)])
        for kc in range(NKC):
            t_ = dve(lambda e, kc=kc: e.tensor_tensor(
                out=x1f_w[:, kc * 512:(kc + 1) * 512],
                in0=x1f_w[:, kc * 512:(kc + 1) * 512], in1=p4[:, 0:512],
                op=ALU.subtract),
                waits=(BK["p4"].for_read() + [("ac", tsq2)]) if kc == 0 else ())
            t_ = dve(lambda e, kc=kc: e.tensor_tensor(
                out=x1f_w[:, kc * 512:(kc + 1) * 512],
                in0=x1f_w[:, kc * 512:(kc + 1) * 512], in1=p4[:, 512:1024],
                op=ALU.mult), waits=[("dv", t_)])
        BK["p4"].read("dv", t_)
        tx1 = t_
        tx1b = act(lambda e: e.activation(out=x1b_w[:, :], in_=x1f_w[:, :],
                                          func=AF.Copy), waits=[("dv", tx1)])
        tap("t_x1", x1f_w[:, 0:4096], [("dv", tx1)]) if w == 0 else None
        d1 = dma("spd", d_x1f[:, :, t0:t0 + 512].rearrange("kc p t -> p kc t"),
                 x1f_w[:].rearrange("p (kc t) -> p kc t", kc=NKC),
                 waits=[("dv", tx1)])
        d2 = dma("spd", d_x1b[:, :, t0:t0 + 512].rearrange("kc p t -> p kc t"),
                 x1b_w[:].rearrange("p (kc t) -> p kc t", kc=NKC),
                 waits=[("ac", tx1b)])
        x1_dma.append(d2)
        g_ac = E.tok("ac")
    tail_dma = E.tok("spd")
    if PHASE == "tail":
        E.ops["sync"].append(lambda e, n=E.count["spd"], sem=E.sems["spd"]: e.wait_ge(sem, n))
        E.ops["gpsimd"].append(lambda e, n=E.count["gpd"], sem=E.sems["gpd"]: e.wait_ge(sem, n))
        E.finalize()
        return nc

    ph2 = [("pe", E.tok("pe")), ("dv", E.tok("dv")), ("ac", E.tok("ac")),
           ("spd", E.tok("spd")), ("gpd", E.tok("gpd"))]
    y_dma = []
    y_w = sq2_w
    g_pe2, g_dv2, g_ac2 = E.tok("pe"), E.tok("dv"), E.tok("ac")
    for wi in range(4):
        woff = wi * 512
        txb = dma("spd", x1w_b[:].rearrange("p (kc t) -> p kc t", kc=NKC),
                  d_x1b[:, :, woff:woff + 512].rearrange("kc p t -> p kc t"),
                  waits=(ph2 if wi == 0 else [("pe", g_pe2), ("dv", g_dv2),
                                              ("ac", g_ac2)]))
        dmw = [0, 0]

        def l1_load(g, mg2, w0):
            dma(f"gpd{g}", hstr[g][:, 0:2048].rearrange("p (kc c) -> p kc c", kc=8),
                d_w1b.rearrange("(kc p) c -> p kc c", p=128)
                [:, :, mg2 * 256:(mg2 + 1) * 256], waits=w0)
            tok = dma(f"gpd{g}", hstr[g][:, 2048:4096].rearrange("p (kc c) -> p kc c", kc=8),
                      d_w2b.rearrange("(kc p) c -> p kc c", p=128)
                      [:, :, mg2 * 256:(mg2 + 1) * 256])
            return tok

        for g in range(2):
            dmw[g] = l1_load(g, g, ((ph2 if wi == 0 and g == 0 else [("gpd", ld_w)]) +
                                    [("cst", cast_done),
                                     (f"gpd{g}", E.tok(f"gpd{g}")), ("pe", E.tok("pe"))]))
        prev_tac = 0
        for mg in range(16):
            ri = mg % 2
            for mi in range(2):
                m = mg * 2 + mi
                pm1, bk1 = (p4, "p4") if m % 2 == 0 else (p5, "p5")
                first = True
                for kc in range(NKC):
                    wt = ([(f"gpd{ri}", dmw[ri]), ("spd", txb)] +
                          BK[bk1].for_write()) if first else ()
                    first = False
                    mm(hstr[ri][:, kc * 256 + mi * 128: kc * 256 + mi * 128 + 128],
                       x1w_b[:, kc * 512:(kc + 1) * 512],
                       pm1[:, 0:512], kc == 0, kc == NKC - 1, waits=wt)
                    t1 = mm(hstr[ri][:, 2048 + kc * 256 + mi * 128:
                                     2048 + kc * 256 + mi * 128 + 128],
                            x1w_b[:, kc * 512:(kc + 1) * 512],
                            pm1[:, 512:1024], kc == 0, kc == NKC - 1)
                BK[bk1].wrote("pe", t1)
                tsl = act(lambda e, m=m, pm1=pm1: e.activation(
                    out=silu_t[:, :], in_=pm1[:, 0:512], func=AF.Silu,
                    scale=scl_s[:, m:m + 1]),
                    waits=BK[bk1].for_read() + [("dv", prev_tac)])
                tac = dve(lambda e, m=m, pm1=pm1: e.scalar_tensor_tensor(
                    out=a_sb[:, m * 512:(m + 1) * 512], in0=pm1[:, 512:1024],
                    scalar=scl_s[:, 32 + m:33 + m], in1=silu_t[:, :],
                    op0=ALU.mult, op1=ALU.mult), waits=[("ac", tsl)])
                BK[bk1].read("dv", tac)
                prev_tac = tac
            if mg + 2 < 16:
                dmw[ri] = l1_load(ri, mg + 2,
                                  [("pe", E.tok("pe")), ("dv", E.tok("dv")),
                                   (f"gpd{ri}", E.tok(f"gpd{ri}"))])
        a_done = E.tok("dv")

        def hload(hT, hB, g, mg2, w0):
            return dma(f"gpd{g}", hstr[g][:].rearrange("p (kc c) -> p kc c", kc=32),
                       hB.rearrange("(kc p) c -> p kc c", p=128)
                       [:, :, mg2 * 256:(mg2 + 1) * 256], waits=w0)

        def hlayer(hT, hB, src, dst, scol, dep):
            dmh = [0, 0]
            for g in range(2):
                dmh[g] = hload(hT, hB, g, g,
                               [("pe", E.tok("pe")), ("dv", E.tok("dv")),
                                ("ac", E.tok("ac")),
                                (f"gpd{g}", E.tok(f"gpd{g}"))])
            for mg in range(16):
                ri = mg % 2
                for mi in range(2):
                    m = mg * 2 + mi
                    pm1, bk1 = (p4, "p4") if mi == 0 else (p5, "p5")
                    first = True
                    for kc in range(32):
                        wt = ([(f"gpd{ri}", dmh[ri])] + dep +
                              BK[bk1].for_write()) if first else ()
                        first = False
                        t1 = mm(hstr[ri][:, kc * 256 + mi * 128:
                                         kc * 256 + mi * 128 + 128],
                                src[:, kc * 512:(kc + 1) * 512],
                                pm1[:, 0:512], kc == 0, kc == 31, waits=wt)
                    BK[bk1].wrote("pe", t1)
                    tgl = act(lambda e, pm1=pm1, m=m, scol=scol: e.activation(
                        out=dst[:, m * 512:(m + 1) * 512], in_=pm1[:, 0:512],
                        func=AF.Gelu, scale=scl_s[:, scol + m: scol + m + 1]),
                        waits=BK[bk1].for_read())
                    BK[bk1].read("ac", tgl)
                if mg + 2 < 16:
                    dmh[ri] = hload(hT, hB, ri, mg + 2,
                                    [("pe", E.tok("pe")),
                                     (f"gpd{ri}", E.tok(f"gpd{ri}"))])
            return E.tok("ac")

        h1_done = hlayer(h1T, d_h1b, a_sb, b_sb, 64, [("dv", a_done)])
        h2_done = hlayer(h2T, d_h2b, b_sb, a_sb, 96, [("ac", h1_done)])

        dmh = [0, 0]
        for g in range(2):
            dmh[g] = hload(w3T, d_w3b, g, g,
                           [("pe", E.tok("pe")), ("ac", E.tok("ac")),
                            (f"gpd{g}", E.tok(f"gpd{g}"))])
        x1t = [0, 0]
        for g in range(2):
            x1t[g] = dma(f"spd{g}", x1fr[g][:, :], d_x1f[g, :, woff:woff + 512],
                         waits=[("dv", E.tok("dv")), ("spd", E.tok("spd"))])
        for mg in range(4):
            ri = mg % 2
            for mi in range(2):
                m = mg * 2 + mi
                pm1, bk1 = (p4, "p4") if mi == 0 else (p5, "p5")
                first = True
                for kc in range(32):
                    wt = ([(f"gpd{ri}", dmh[ri]), ("ac", h2_done),
                           (f"spd{m % 2}", x1t[m % 2])] + BK[bk1].for_write()) if first else ()
                    first = False
                    t1 = mm(hstr[ri][:, kc * 256 + mi * 128: kc * 256 + mi * 128 + 128],
                            a_sb[:, kc * 512:(kc + 1) * 512],
                            pm1[:, 0:512], kc == 0, kc == 31, waits=wt)
                BK[bk1].wrote("pe", t1)
                tr2 = dve(lambda e, pm1=pm1, m=m: e.scalar_tensor_tensor(
                    out=r2_w[:, m * 512:(m + 1) * 512], in0=pm1[:, 0:512],
                    scalar=scl_s[:, 128 + m:129 + m], in1=x1fr[m % 2][:, :],
                    op0=ALU.mult, op1=ALU.add), waits=BK[bk1].for_read())
                BK[bk1].read("dv", tr2)
                if m + 2 < 8:
                    x1t[m % 2] = dma(f"spd{m % 2}", x1fr[m % 2][:, :],
                                     d_x1f[m + 2, :, woff:woff + 512],
                                     waits=[("dv", tr2)])
            if mg + 2 < 4:
                dmh[ri] = hload(w3T, d_w3b, ri, mg + 2,
                                [("pe", E.tok("pe")),
                                 (f"gpd{ri}", E.tok(f"gpd{ri}"))])
        r2_done = E.tok("dv")

        tq3 = act(lambda e: e.activation(out=sq2_w[:, :], in_=r2_w[:, :],
                                         func=AF.Square), waits=[("dv", r2_done)])
        first = True
        for kc in range(NKC):
            wt = ([("dv", r2_done)] + BK["p0"].for_write()) if first else ()
            first = False
            t1 = mm(ones_s[:, 0:1], r2_w[:, kc * 512:(kc + 1) * 512],
                    p0[0:1, 0:512], kc == 0, kc == NKC - 1, waits=wt)
        BK["p0"].wrote("pe", t1)
        first = True
        for kc in range(NKC):
            wt = ([("ac", tq3)] + BK["p1"].for_write()) if first else ()
            first = False
            t2 = mm(ones_s[:, 0:1], sq2_w[:, kc * 512:(kc + 1) * 512],
                    p1[0:1, 0:512], kc == 0, kc == NKC - 1, waits=wt)
        BK["p1"].wrote("pe", t2)
        tmu = dve(lambda e: e.tensor_scalar(out=row_t[0:1, 0:512], in0=p0[0:1, :],
                                            scalar1=1.0 / DM, scalar2=None,
                                            op0=ALU.mult), waits=BK["p0"].for_read())
        BK["p0"].read("dv", tmu)
        tvv = dve(lambda e: e.tensor_scalar(out=row_t[0:1, 512:1024], in0=p1[0:1, :],
                                            scalar1=1.0 / DM, scalar2=None,
                                            op0=ALU.mult), waits=BK["p1"].for_read())
        BK["p1"].read("dv", tvv)
        t_r = dve(lambda e: e.tensor_tensor(out=row_t[0:1, 1024:1536], in0=row_t[0:1, 0:512],
                                            in1=row_t[0:1, 0:512], op=ALU.mult),
                  waits=[("dv", tmu)])
        t_r = dve(lambda e: e.tensor_tensor(out=row_t[0:1, 512:1024],
                                            in0=row_t[0:1, 512:1024],
                                            in1=row_t[0:1, 1024:1536], op=ALU.subtract),
                  waits=[("dv", t_r)])
        tva = dve(lambda e: e.tensor_scalar(out=row_t[0:1, 512:1024],
                                            in0=row_t[0:1, 512:1024], scalar1=1e-5,
                                            scalar2=None, op0=ALU.add),
                  waits=[("dv", t_r)])
        tsd = act(lambda e: e.activation(out=row_t[0:1, 1024:1536],
                                         in_=row_t[0:1, 512:1024], func=AF.Sqrt),
                  waits=[("dv", tva)])
        tiv = dve(lambda e: e.reciprocal(out=row_t[0:1, 512:1024],
                                         in_=row_t[0:1, 1024:1536]),
                  waits=[("ac", tsd)])
        wb = BK["p4"].for_write() + [("dv", tiv)]
        mm(ones_s[0:1, :], row_t[0:1, 0:512], p4[:, 0:512], True, True, waits=wb)
        tb2 = mm(ones_s[0:1, :], row_t[0:1, 512:1024], p4[:, 512:1024], True, True)
        BK["p4"].wrote("pe", tb2)
        for kc in range(NKC):
            t_ = dve(lambda e, kc=kc: e.tensor_tensor(
                out=y_w[:, kc * 512:(kc + 1) * 512],
                in0=r2_w[:, kc * 512:(kc + 1) * 512], in1=p4[:, 0:512],
                op=ALU.subtract),
                waits=(BK["p4"].for_read() + [("pe", t2)]) if kc == 0 else ())
            t_ = dve(lambda e, kc=kc: e.tensor_tensor(
                out=y_w[:, kc * 512:(kc + 1) * 512],
                in0=y_w[:, kc * 512:(kc + 1) * 512], in1=p4[:, 512:1024],
                op=ALU.mult), waits=[("dv", t_)])
        BK["p4"].read("dv", t_)
        ty = E.tok("dv")
        y_dma.append(dma("spd",
                         yT.rearrange("(kc p) t -> p kc t", p=128)[:, :, woff:woff + 512],
                         y_w[:].rearrange("p (kc t) -> p kc t", kc=NKC),
                         waits=[("dv", ty)]))
        g_pe2, g_dv2, g_ac2 = E.tok("pe"), E.tok("dv"), E.tok("ac")

    final_spd = E.count["spd"]
    def _final_wait(e, n=final_spd, sem=E.sems["spd"]):
        e.wait_ge(sem, n)
    E.ops["sync"].append(lambda e: _final_wait(e))
    E.finalize()
    return nc


_NC_CACHE = {}


def _get_nc():
    key = ("prog", tuple(DEBUG), PHASE, NPAIR_RUN)
    if key not in _NC_CACHE:
        _NC_CACHE[key] = build_program()
    return _NC_CACHE[key]


def kernel(L0_attn_x, L0_attn_Wq, L0_attn_Wk, L0_attn_Wv, L0_attn_Wgamma,
           L0_attn_Wf, L0_attn_Wg1, L0_attn_Wg2, L0_attn_Wo, L0_attn_norm_w,
           L0_ffn_w1_w, L0_ffn_w1_s, L0_ffn_w2_w, L0_ffn_w2_s, L0_ffn_h1_w,
           L0_ffn_h1_s, L0_ffn_h2_w, L0_ffn_h2_s, L0_ffn_w3_w, L0_ffn_w3_s):
    nc = _get_nc()
    x = np.asarray(L0_attn_x)
    cc = np.ascontiguousarray
    wsm_np = cc(np.concatenate([np.asarray(L0_attn_Wf), np.asarray(L0_attn_Wgamma),
                                np.asarray(L0_attn_Wg1)], axis=1))
    scl_np = np.zeros((128, 136), np.float32)
    scl_np[:, 0:32] = np.asarray(L0_ffn_w1_s).reshape(32, 128).T
    scl_np[:, 32:64] = np.asarray(L0_ffn_w2_s).reshape(32, 128).T
    scl_np[:, 64:96] = np.asarray(L0_ffn_h1_s).reshape(32, 128).T
    scl_np[:, 96:128] = np.asarray(L0_ffn_h2_s).reshape(32, 128).T
    scl_np[:, 128:136] = np.asarray(L0_ffn_w3_s).reshape(8, 128).T
    shared = {
        "wq": cc(np.asarray(L0_attn_Wq)), "wk": cc(np.asarray(L0_attn_Wk)),
        "wv": cc(np.asarray(L0_attn_Wv)), "wo": cc(np.asarray(L0_attn_Wo)),
        "wsm": wsm_np, "wg2": cc(np.asarray(L0_attn_Wg2)),
        "normw": cc(np.asarray(L0_attn_norm_w).reshape(8, 128).T),
        "w1T": cc(np.asarray(L0_ffn_w1_w).T), "w2T": cc(np.asarray(L0_ffn_w2_w).T),
        "h1T": cc(np.asarray(L0_ffn_h1_w).T), "h2T": cc(np.asarray(L0_ffn_h2_w).T),
        "w3T": cc(np.asarray(L0_ffn_w3_w).T), "scl": scl_np,
    }
    in_maps = []
    xTb = [cc(x[b].T) for b in range(2)]
    for core in range(8):
        b, s = core // 4, core % 4
        selv = np.zeros((128, 4), np.float32)
        selv[:, s] = 1.0
        m = dict(shared)
        m["xT"] = cc(xTb[b][:, s * SEG:(s + 1) * SEG])
        m["sel"] = selv
        in_maps.append(m)
    res = run_bass_kernel_spmd(nc, in_maps, core_ids=list(range(8)))
    out = np.zeros((2, 8192, DM), np.float32)
    for core in range(8):
        b, s = core // 4, core % 4
        out[b, s * SEG:(s + 1) * SEG, :] = np.asarray(res.results[core]["yT"]).T
    if DEBUG:
        global LAST_TAPS
        LAST_TAPS = [{k: np.asarray(v) for k, v in res.results[c].items()
                      if k.startswith("t_")} for c in range(8)]
    return out


# version 06
# speedup vs baseline: 1.1303x; 1.0069x over previous
import numpy as np
import ml_dtypes
import concourse.bass as bass
import concourse.mybir as mybir
from concourse.bass_utils import run_bass_kernel_spmd
from concourse.ap import AP

F32 = mybir.dt.float32
BF16 = mybir.dt.bfloat16
BFnp = ml_dtypes.bfloat16
AF = mybir.ActivationFunctionType
ALU = mybir.AluOpType

SEG, C, NH, HD, DM, DH = 2048, 64, 16, 64, 1024, 4096
NCH, NPAIR, NKC, NWIN = 32, 16, 8, 4
FFN_W = (768, 640, 640)
DEBUG = ()
PHASE = "full"
NPAIR_RUN = NPAIR


def bcast(ap, n):
    return AP(ap.tensor, ap.offset, [list(x) for x in ap.ap] + [[0, n]])


def rep(ap, n, m):
    return AP(ap.tensor, ap.offset, [list(ap.ap[0])] + [[0, n]] + [[1, m]])


class Emitter:
    ENGINES = ("sync", "tensor", "vector", "scalar", "gpsimd")

    def __init__(self, nc):
        self.nc = nc
        self.ops = {e: [] for e in self.ENGINES}
        self.sems, self.count = {}, {}
        for name in ("pe", "dv", "ac", "gp", "spd", "gpd", "spd0", "spd1",
                     "gpd0", "gpd1", "cst", "mvq", "xw0", "xw1"):
            self.sems[name] = nc.alloc_semaphore("sem_" + name)
            self.count[name] = 0

    def emit(self, eng, fn, waits=(), inc=None, inc_by=1):
        for (s, v) in waits:
            assert v <= self.count[s], f"{eng}: wait {s}>={v} > {self.count[s]}"
        tok = None
        if inc is not None:
            self.count[inc] += inc_by
            tok = self.count[inc]
        wl = [(self.sems[s], v) for (s, v) in waits if v > 0]
        sem = self.sems[inc] if inc is not None else None

        def closure(e, wl=wl, sem=sem, inc_by=inc_by, fn=fn):
            for (sh, v) in wl:
                e.wait_ge(sh, v)
            ins = fn(e)
            if sem is not None:
                ins.then_inc(sem, inc_by)
        self.ops[eng].append(closure)
        return tok

    def tok(self, s):
        return self.count[s]

    def finalize(self):
        with self.nc.Block() as block:
            for eng in self.ENGINES:
                ops = self.ops[eng]
                getattr(block, eng)(lambda e, ops=ops: [f(e) for f in ops])


class Bank:
    def __init__(self):
        self.w, self.r = [], []

    def for_write(self):
        return list(self.r) + list(self.w)

    def for_read(self):
        return list(self.w)

    def wrote(self, sem, val):
        self.w, self.r = [(sem, val)], []

    def read(self, sem, val):
        self.r = [(sem, val)]


class Arena:
    def __init__(self, ap):
        self.ap = ap
        self.off = 0

    def al(self, shape, dt):
        esz = {F32: 4, BF16: 2}[dt]
        size = shape[1] * esz
        sl = self.ap[0:shape[0], self.off:self.off + size].bitcast(dt)
        self.off += (size + 63) // 64 * 64
        return sl


def build_program():
    nc = bass.Bass("TRN2", target_bir_lowering=False, num_devices=8)
    E = Emitter(nc)

    din = lambda n, shp, dt=F32: nc.dram_tensor(n, shp, dt, kind="ExternalInput").ap()
    xT = din("xT", [DM, SEG])
    sel = din("sel", [128, 4])
    wq, wk, wv, wo = din("wq", [DM, DM]), din("wk", [DM, DM]), din("wv", [DM, DM]), din("wo", [DM, DM])
    wsm = din("wsm", [DM, 96])
    wg2 = din("wg2", [64, DM])
    normw = din("normw", [128, 8])
    w1T, w2T = din("w1T", [DM, DH]), din("w2T", [DM, DH])
    h1T, h2T = din("h1T", [DH, DH]), din("h2T", [DH, DH])
    w3T = din("w3T", [DH, DM])
    scl = din("scl", [128, 136])
    yT = nc.dram_tensor("yT", [DM, SEG], F32, kind="ExternalOutput").ap()

    d_A = nc.dram_tensor("d_A", [NCH, 128, 512], BF16, kind="Internal").ap()
    d_Qt = nc.dram_tensor("d_Qt", [NCH, 128, 512], BF16, kind="Internal").ap()
    d_out = nc.dram_tensor("d_out", [NCH, 128, 512], BF16, kind="Internal").ap()
    d_x1f = nc.dram_tensor("d_x1f", [NKC, 128, SEG], F32, kind="Internal").ap()
    d_x1b = nc.dram_tensor("d_x1b", [NKC, 128, SEG], BF16, kind="Internal").ap()
    ccin = nc.dram_tensor("ccin", [128, 1024], BF16, kind="Internal").ap()
    d_w1b = nc.dram_tensor("d_w1b", [DM, DH], BF16, kind="Internal").ap()
    d_w2b = nc.dram_tensor("d_w2b", [DM, DH], BF16, kind="Internal").ap()
    d_h1b = nc.dram_tensor("d_h1b", [DH, DH], BF16, kind="Internal").ap()
    d_h2b = nc.dram_tensor("d_h2b", [DH, DH], BF16, kind="Internal").ap()
    d_w3b = nc.dram_tensor("d_w3b", [DH, DM], BF16, kind="Internal").ap()
    ccout = nc.dram_tensor("ccout", [512, 1024], BF16, kind="Internal").ap()

    dbg = {}
    for name in DEBUG:
        shp = {"t_q": [128, 1024], "t_kn": [128, 1024], "t_nlc": [128, 16],
               "t_G": [128, 1024], "t_MT": [128, 1024], "t_U": [128, 2048],
               "t_AT": [128, 1024], "t_B": [128, 1024], "t_Qt": [128, 1024],
               "t_S": [128, 512], "t_sin": [128, 512], "t_gate": [128, 4096],
               "t_x1": [128, 4096], "t_a": [128, 4096], "t_fcx": [128, 256],
               }[name]
        dt = F32
        dbg[name] = nc.dram_tensor(name, shp, dt, kind="ExternalOutput").ap()

    def inline(name, arr):
        if arr.dtype == BFnp:
            return nc.inline_tensor(np.ascontiguousarray(arr).view(np.uint16),
                                    name=name).ap().bitcast(BF16)
        return nc.inline_tensor(np.ascontiguousarray(arr), name=name).ap()

    tri_np = np.zeros((128, 64), np.float32)
    for p in range(128):
        tri_np[p, p % 64:] = 1.0
    mG = np.zeros((128, 1024), np.float32)
    mH = np.zeros((128, 1024), np.float32)
    mM = np.zeros((128, 1024), np.float32)
    ii = np.arange(64)
    for p in range(128):
        j = p % 64
        for blk in range(16):
            mG[p, blk * 64:(blk + 1) * 64] = np.where(j < ii, -1.0, 0.0)
            mH[p, blk * 64:(blk + 1) * 64] = np.where(ii < j, -1.0, 0.0)
            mM[p, blk * 64:(blk + 1) * 64] = np.where(j <= ii, 1.0, 0.0)
    c_tri = inline("c_tri", tri_np)
    c_id = inline("c_id", np.eye(128, dtype=np.float32).astype(BFnp))
    c_irep = inline("c_irep", np.tile(np.eye(64, dtype=np.float32), (2, 8)).astype(BFnp))
    c_ones = inline("c_ones", np.ones((128, 128), np.float32))
    c_mG = inline("c_mG", mG.astype(BFnp))
    c_mH = inline("c_mH", mH.astype(BFnp))
    c_mM = inline("c_mM", mM.astype(BFnp))

    sb = lambda n, shp, dt=F32: nc.alloc_sbuf_tensor(n, shp, dt).ap()
    tri_s = sb("tri_s", [128, 64])
    id_s = sb("id_s", [128, 128], BF16)
    irep_s = sb("irep_s", [128, 512], BF16)
    ones_s = sb("ones_s", [128, 128])
    mG_s = sb("mG_s", [128, 1024], BF16)
    mH_s = sb("mH_s", [128, 1024], BF16)
    mM_s = sb("mM_s", [128, 1024], BF16)
    sel_s = sb("sel_s", [128, 4])
    normw_s = sb("normw_s", [128, 8])
    scl_s = sb("scl_s", [128, 136])
    wsm_s = sb("wsm_s", [128, 8 * 96], BF16)
    wg2_s = sb("wg2_s", [64, DM], BF16)
    fcx_p = sb("fcx_p", [128, NCH * 8])
    g1_tm = sb("g1_tm", [128, NPAIR * 64], BF16)
    S_ab = [sb(f"S_ab{i}", [128, 512], BF16) for i in range(2)]
    P_ab = [sb(f"P_ab{i}", [128, 512], BF16) for i in range(2)]
    SF_t = sb("SF_t", [128, 512], BF16)
    PF_t = sb("PF_t", [128, 512], BF16)
    o2_t = sb("o2_t", [128, 512], BF16)
    ostg = sb("ostg", [128, 512], BF16)
    ptseg = sb("ptseg", [128, 512], BF16)
    gat_s = sb("gat_s", [128, 4096], BF16)
    rpj = [sb(f"rpj{i}", [128, 512], BF16) for i in range(4)]
    sin_s = sb("sin_s", [128, 512], BF16)
    dlt = [sb(f"dlt{i}", [128, 512], BF16) for i in range(2)]
    qtc = [sb(f"qtc{i}", [128, 512], BF16) for i in range(2)]
    atc = [sb(f"atc{i}", [128, 512], BF16) for i in range(2)]
    och = [sb(f"och{i}", [128, 512], BF16) for i in range(2)]
    row_t = sb("row_t", [1, 3 * 512])
    rn16 = sb("rn16", [128, 16])
    sg_f = sb("sg_f", [128, 16])
    nlf_s = sb("nlf_s", [128, 16])
    nspg_s = sb("nspg_s", [128, 16])
    nlc_s = sb("nlc_s", [128, 16])
    nn_s = sb("nn_s", [128, 16])
    dd_s = sb("dd_s", [128, 16])
    edec_s = sb("edec_s", [128, 16])
    rowm_s = sb("rowm_s", [128, 16])
    rowg_s = sb("rowg_s", [128, 16])
    el_s = sb("el_s", [128, 16])
    bcol_s = sb("bcol_s", [128, 16])
    nlcc_s = sb("nlcc_s", [128, 32])
    fcrow_s = sb("fcrow_s", [128, 32])

    dbg_stg = sb("dbg_stg", [128, 1024]) if DEBUG else None

    AR_BYTES = 150 * 1024 - (4608 if DEBUG else 0)
    ar_t = nc.alloc_sbuf_tensor("arena", [128, AR_BYTES], mybir.dt.uint8).ap()

    A = Arena(ar_t)
    wq_s = A.al([128, 8 * DM], BF16)
    wk_s = A.al([128, 8 * DM], BF16)
    wv_s = A.al([128, 8 * DM], BF16)
    xwb_ = [A.al([128, NKC * 128], BF16) for i in range(2)]
    xst_ = [A.al([128, NKC * 128], F32) for i in range(2)]
    q_f = A.al([128, 1024], F32)
    k_f = A.al([128, 1024], F32)
    ksq = A.al([128, 1024], F32)
    kn_f = A.al([128, 1024], F32)
    ktil_tm = A.al([128, 1024], BF16)
    kg_tm = A.al([128, 1024], BF16)
    qm_tm = A.al([128, 1024], BF16)
    qf_tm = A.al([128, 1024], BF16)
    ktil_f = A.al([128, 1024], BF16)
    kg_f = A.al([128, 1024], BF16)
    qm_f = A.al([128, 1024], BF16)
    qf_f = A.al([128, 1024], BF16)
    U_s = A.al([128, 2048], BF16)
    G_sb = [A.al([128, 1024], BF16) for i in range(2)]
    H_sb = [A.al([128, 1024], BF16) for i in range(2)]
    MT_sb = A.al([128, 1024], BF16)
    B_sb = A.al([128, 1024], F32)
    AT_sb = A.al([128, 1024], BF16)
    Qt_sb = A.al([128, 1024], BF16)
    wo_s = A.al([128, 8 * DM], BF16)
    assert A.off <= AR_BYTES, A.off
    wo_off = A.off - 16 * 1024

    T = Arena(ar_t)
    gate_w = T.al([128, 8 * 512], F32)
    og_w = T.al([128, 8 * 512], F32)
    xf_w = T.al([128, 8 * 512], F32)
    x1f_w = T.al([128, 8 * 512], F32)
    sq_w = T.al([128, 8 * 512], F32)
    outw = T.al([128, 8 * 512], BF16)
    on_w = T.al([128, 8 * 512], BF16)
    x1b_w = T.al([128, 8 * 512], BF16)
    g1f_w = T.al([64, 512], BF16)
    tail_end = T.off
    assert tail_end <= wo_off, (tail_end, wo_off)
    F = Arena(ar_t)
    FW = 512
    hstr = [F.al([128, 32 * 256], BF16) for i in range(2)]
    a_sb = F.al([128, 32 * FW], BF16)
    b_sb = F.al([128, 32 * FW], BF16)
    x1w_b = F.al([128, 8 * FW], BF16)
    r2_w = F.al([128, 8 * FW], F32)
    sq2_w = F.al([128, 8 * FW], F32)
    x1fr = [F.al([128, 512], F32) for i in range(2)]
    y_stg = F.al([128, 512], F32)
    silu_t = F.al([128, 512], F32)
    assert max(F.off, tail_end) <= AR_BYTES, (F.off, tail_end)

    p0 = nc.alloc_psum_tensor("p0", [128, 512], F32).ap()
    p1 = nc.alloc_psum_tensor("p1", [128, 512], F32).ap()
    p2 = nc.alloc_psum_tensor("p2", [128, 512], F32).ap()
    p3 = nc.alloc_psum_tensor("p3", [128, 1024], BF16).ap()
    p4 = nc.alloc_psum_tensor("p4", [128, 1024], F32).ap()
    p5 = nc.alloc_psum_tensor("p5", [128, 1024], F32).ap()
    BK = {n: Bank() for n in ("p0", "p1", "p2", "p3", "p4", "p5")}

    def dma(q, dst, src, waits=()):
        eng = "sync" if (q.startswith("spd") or q in ("mvq", "xw0", "xw1")) else "gpsimd"
        return E.emit(eng, lambda e, d=dst, s=src: e.dma_start(out=d, in_=s),
                      waits=waits, inc=q, inc_by=16)

    def mm(lhsT, rhs, out, start, stop, tp=None, waits=()):
        return E.emit("tensor",
                      lambda e, l=lhsT, r=rhs, o=out, s1=start, s2=stop, t=tp:
                      e.matmul(out=o, lhsT=l, rhs=r, start=s1, stop=s2,
                               tile_position=t),
                      waits=waits, inc="pe")

    def trans(in_, out, ident, tp=None, waits=()):
        return E.emit("tensor",
                      lambda e, i=in_, o=out, d=ident, t=tp:
                      e.transpose(out=o, in_=i, identity=d, tile_position=t),
                      waits=waits, inc="pe")

    act = lambda fn, waits=(): E.emit("scalar", fn, waits=waits, inc="ac")
    dve = lambda fn, waits=(): E.emit("vector", fn, waits=waits, inc="dv")
    gpe = lambda fn, waits=(): E.emit("gpsimd", fn, waits=waits, inc="gp")

    def tap(name, src, waits):
        if name not in dbg:
            return
        if src.dtype == BF16:
            return
            n = src.shape[1]
            for c0 in range(0, n, 1024):
                cw = min(1024, n - c0)
                t = act(lambda e, src=src, c0=c0, cw=cw: e.activation(
                    out=dbg_stg[:, 0:cw], in_=src[:, c0:c0 + cw], func=AF.Copy),
                    waits=waits)
                td = dma("gpd", dbg[name][:, c0:c0 + cw],
                         dbg_stg[0:src.shape[0], 0:cw], waits=[("ac", t)])
                waits = [("gpd", td)]
        else:
            dma("gpd", dbg[name][:, :], src, waits=waits)

    rg = lambda h: (h % 2) * 64
    hp = lambda h: h // 2
    g8 = lambda t: t[:].rearrange("p (g i) -> p g i", g=8)
    g16 = lambda t: t[:].rearrange("p (g i) -> p g i", g=16)

    for dst, src in [(tri_s, c_tri), (id_s, c_id), (irep_s, c_irep),
                     (ones_s, c_ones), (mG_s, c_mG), (mH_s, c_mH), (mM_s, c_mM),
                     (sel_s, sel), (normw_s, normw), (scl_s, scl)]:
        dma("spd", dst[:, :], src[:, :])
    ld_c = E.tok("spd")
    for dst, src in [(wq_s, wq), (wk_s, wk), (wv_s, wv), (wo_s, wo)]:
        dma("gpd", dst[:].rearrange("p (kc c) -> p kc c", kc=8),
            src.rearrange("(kc p) c -> p kc c", p=128))
    dma("gpd", wsm_s[:].rearrange("p (kc c) -> p kc c", kc=8),
        wsm.rearrange("(kc p) c -> p kc c", p=128))
    dma("gpd", wg2_s[:, :], wg2[:, :])
    ld_w = E.tok("gpd")

    xw_tok = [None, None]
    xst_tok = [None, None]
    for i in range(2):
        xst_tok[i] = dma(f"xw{i}", xst_[i][:].rearrange("p (kc t) -> p kc t", kc=NKC),
                         xT.rearrange("(kc p) t -> p kc t", p=128)
                         [:, :, i * 128:(i + 1) * 128], waits=[("spd", ld_c)])
        xw_tok[i] = act(lambda e, i=i: e.activation(out=xwb_[i][:, :], in_=xst_[i][:, :],
                                                    func=AF.Copy),
                        waits=[(f"xw{i}", xst_tok[i])])

    def cast_chain():
        for (wsrc, wdst) in [(w1T, d_w1b), (w2T, d_w2b), (w3T, d_w3b),
                             (h1T, d_h1b), (h2T, d_h2b)]:
            tok = dma("cst", wdst[:, :], wsrc[:, :], waits=[("gpd", ld_w)])
        return tok

    cast_done = cast_chain()

    dve(lambda e: e.memset(S_ab[0][:, :], 0.0))
    dve(lambda e: e.tensor_copy(out=P_ab[0][:, :], in_=irep_s[:, :]),
        waits=[("spd", ld_c)])
    sp_tok = E.tok("dv")

    g_pe, g_dv, g_ac, g_spd = 0, 0, 0, 0
    s_evac = sp_tok
    p_evac = 0

    for pr in range(NPAIR_RUN):
        xwb = xwb_[pr % 2]
        Wg = [("pe", g_pe), ("dv", g_dv), ("ac", max(g_ac, xw_tok[pr % 2])),
              ("spd", E.tok("spd")), ("gpd", ld_w)]

        def proj(wt):
            for half, pm, bk in ((0, p0, "p0"), (1, p1, "p1")):
                for kc in range(NKC):
                    w = (Wg + BK[bk].for_write()) if kc == 0 else ()
                    t = mm(xwb[:, kc * 128:(kc + 1) * 128],
                           wt[:, kc * DM + half * 512: kc * DM + half * 512 + 512],
                           pm[:, :], kc == 0, kc == NKC - 1, waits=w)
                BK[bk].wrote("pe", t)

        proj(wq_s)
        tq1 = act(lambda e: e.activation(out=q_f[:, 0:512], in_=p0[:, :], func=AF.Silu),
                  waits=BK["p0"].for_read() + [("dv", g_dv)])
        BK["p0"].read("ac", tq1)
        tq2 = act(lambda e: e.activation(out=q_f[:, 512:1024], in_=p1[:, :], func=AF.Silu),
                  waits=BK["p1"].for_read())
        BK["p1"].read("ac", tq2)

        if PHASE == "projq" and pr == NPAIR_RUN - 1:
            E.ops["sync"].append(lambda e, n=E.count["spd"], sem=E.sems["spd"]: e.wait_ge(sem, n))
            E.ops["gpsimd"].append(lambda e, n=E.count["gpd"], sem=E.sems["gpd"]: e.wait_ge(sem, n))
            E.finalize()
            return nc
        proj(wk_s)
        tk1 = act(lambda e: e.activation(out=k_f[:, 0:512], in_=p0[:, :], func=AF.Silu),
                  waits=BK["p0"].for_read())
        BK["p0"].read("ac", tk1)
        tk2 = act(lambda e: e.activation(out=k_f[:, 512:1024], in_=p1[:, :], func=AF.Silu),
                  waits=BK["p1"].for_read())
        BK["p1"].read("ac", tk2)
        tap("t_q", q_f[:, :], [("ac", tq2)])
        tsq = act(lambda e: e.activation(out=ksq[:, :], in_=k_f[:, :], func=AF.Square), waits=[("ac", tk2)])
        trd = dve(lambda e: e.tensor_reduce(
            out=rn16[:].rearrange("p (g o) -> p g o", o=1),
            in_=g16(ksq), axis=mybir.AxisListType.X, op=ALU.add),
            waits=[("ac", tsq)])
        tmx = dve(lambda e: e.tensor_scalar(out=rn16[:, :], in0=rn16[:, :],
                                            scalar1=1e-24, scalar2=None, op0=ALU.max),
                  waits=[("dv", trd)])
        proj(wv_s)
        Uv3 = U_s[:].rearrange("p (h c) -> p h c", h=NH)
        tv1 = act(lambda e: e.activation(
            out=Uv3[:, 0:8, 0:64], in_=p0[:].rearrange("p (h c) -> p h c", c=64),
            func=AF.Silu), waits=BK["p0"].for_read())
        BK["p0"].read("ac", tv1)
        tv2 = act(lambda e: e.activation(
            out=Uv3[:, 8:16, 0:64], in_=p1[:].rearrange("p (h c) -> p h c", c=64),
            func=AF.Silu), waits=BK["p1"].for_read())
        BK["p1"].read("ac", tv2)

        if PHASE == "projv" and pr == NPAIR_RUN - 1:
            E.ops["sync"].append(lambda e, n=E.count["spd"], sem=E.sems["spd"]: e.wait_ge(sem, n))
            E.ops["gpsimd"].append(lambda e, n=E.count["gpd"], sem=E.sems["gpd"]: e.wait_ge(sem, n))
            E.finalize()
            return nc
        for kc in range(NKC):
            w = (Wg + BK["p2"].for_write()) if kc == 0 else ()
            tsm = mm(xwb[:, kc * 128:(kc + 1) * 128],
                     wsm_s[:, kc * 96:(kc + 1) * 96], p2[:, 0:96],
                     kc == 0, kc == NKC - 1, waits=w)
        BK["p2"].wrote("pe", tsm)
        tsg1 = act(lambda e: e.activation(out=sg_f[:, :], in_=p2[:, 0:16],
                                          func=AF.Sigmoid), waits=BK["p2"].for_read())
        tsg2 = act(lambda e: e.activation(out=nspg_s[:, :], in_=p2[:, 16:32],
                                          func=AF.Sigmoid))
        tg1 = act(lambda e, pr=pr: e.activation(out=g1_tm[:, pr * 64:(pr + 1) * 64],
                                                in_=p2[:, 32:96], func=AF.Copy))
        BK["p2"].read("ac", tg1)

        if PHASE == "projsm" and pr == NPAIR_RUN - 1:
            E.ops["sync"].append(lambda e, n=E.count["spd"], sem=E.sems["spd"]: e.wait_ge(sem, n))
            E.ops["gpsimd"].append(lambda e, n=E.count["gpd"], sem=E.sems["gpd"]: e.wait_ge(sem, n))
            E.finalize()
            return nc
        tln1 = act(lambda e: e.activation(out=nlf_s[:, :], in_=sg_f[:, :], func=AF.Ln),
                   waits=[("ac", tsg1)])
        tln2 = act(lambda e: e.activation(out=nspg_s[:, :], in_=nspg_s[:, :], func=AF.Ln),
                   waits=[("ac", tsg2)])
        tln3 = act(lambda e: e.activation(out=rn16[:, :], in_=rn16[:, :], func=AF.Ln),
                   waits=[("dv", tmx)])
        tneg1 = dve(lambda e: e.tensor_scalar(out=nlf_s[:, :], in0=nlf_s[:, :],
                                              scalar1=-1.0, scalar2=None, op0=ALU.mult),
                    waits=[("ac", tln1)])
        tneg2 = dve(lambda e: e.tensor_scalar(out=nspg_s[:, :], in0=nspg_s[:, :],
                                              scalar1=-1.0, scalar2=None, op0=ALU.mult),
                    waits=[("ac", tln2)])
        w0 = [("dv", tneg1)] + BK["p2"].for_write()
        mm(tri_s[0:64, :], nlf_s[0:64, :], p2[0:64, 96:112], True, True, waits=w0)
        mm(tri_s[64:128, :], nlf_s[64:128, :], p2[64:128, 96:112], True, True,
           tp=(64, 64))
        mm(ones_s[0:64, 0:1], nlf_s[0:64, :], p2[0:1, 112:128], True, True)
        cm4 = mm(ones_s[64:128, 0:1], nlf_s[64:128, :], p2[64:65, 128:144], True, True,
                 tp=(64, 64))
        BK["p2"].wrote("pe", cm4)
        tnlc = act(lambda e: e.activation(out=nlc_s[:, :], in_=p2[:, 96:112],
                                          func=AF.Copy), waits=BK["p2"].for_read())
        tncc = act(lambda e: e.activation(out=nlcc_s[0:1, 0:16], in_=p2[0:1, 112:128],
                                          func=AF.Copy))
        tncc = act(lambda e: e.activation(out=nlcc_s[64:65, 16:32],
                                          in_=p2[64:65, 128:144], func=AF.Copy))
        tnmv = dma("mvq", nlcc_s[0:1, 16:32], nlcc_s[64:65, 16:32],
                   waits=[("ac", tncc)])
        BK["p2"].read("ac", tncc)
        tap("t_nlc", nlc_s[:, :], [("ac", tnlc)])

        if PHASE == "cums" and pr == NPAIR_RUN - 1:
            E.ops["sync"].append(lambda e, n=E.count["spd"], sem=E.sems["spd"]: e.wait_ge(sem, n))
            E.ops["gpsimd"].append(lambda e, n=E.count["gpd"], sem=E.sems["gpd"]: e.wait_ge(sem, n))
            E.finalize()
            return nc

        wb = BK["p2"].for_write() + [("ac", tncc), ("mvq", tnmv)]
        mm(ones_s[0:1, 0:64], nlcc_s[0:1, 0:16], p2[0:64, 112:128], True, True, waits=wb)
        bc2 = mm(ones_s[0:1, 0:64], nlcc_s[0:1, 16:32], p2[64:128, 112:128], True,
                 True, tp=(0, 64))
        BK["p2"].wrote("pe", bc2)
        tdd = dve(lambda e: e.tensor_tensor(out=dd_s[:, :], in0=p2[:, 112:128],
                                            in1=nlc_s[:, :], op=ALU.subtract),
                  waits=BK["p2"].for_read() + [("ac", tncc)])
        tnn1 = dve(lambda e: e.tensor_tensor(out=nn_s[:, :], in0=nlc_s[:, :],
                                             in1=nlf_s[:, :], op=ALU.add),
                   waits=[("dv", tneg2)])
        tnn2 = dve(lambda e: e.tensor_tensor(out=nn_s[:, :], in0=nn_s[:, :],
                                             in1=nspg_s[:, :], op=ALU.add),
                   waits=[("dv", tnn1)])
        trg = dve(lambda e: e.tensor_tensor(out=rowg_s[:, :], in0=p2[:, 112:128],
                                            in1=nn_s[:, :], op=ALU.subtract),
                  waits=[("dv", tnn2)])
        BK["p2"].read("dv", trg)

        if PHASE == "rows" and pr == NPAIR_RUN - 1:
            E.ops["sync"].append(lambda e, n=E.count["spd"], sem=E.sems["spd"]: e.wait_ge(sem, n))
            E.ops["gpsimd"].append(lambda e, n=E.count["gpd"], sem=E.sems["gpd"]: e.wait_ge(sem, n))
            E.finalize()
            return nc
        te1 = act(lambda e: e.activation(out=edec_s[:, :], in_=dd_s[:, :], func=AF.Exp,
                                         scale=-1.0), waits=[("dv", trg), ("ac", tln3)])
        te2 = act(lambda e: e.activation(out=rowm_s[:, :], in_=dd_s[:, :], func=AF.Exp))
        te3 = act(lambda e: e.activation(out=el_s[:, :], in_=nlc_s[:, :], func=AF.Exp,
                                         scale=-1.0))
        te4 = act(lambda e: e.activation(out=bcol_s[:, :], in_=nn_s[:, :], func=AF.Exp,
                                         scale=-1.0))
        te5 = act(lambda e: e.activation(out=rowg_s[:, :], in_=rowg_s[:, :], func=AF.Exp))
        te6 = act(lambda e: e.activation(out=rn16[:, :], in_=rn16[:, :], func=AF.Exp,
                                         scale=-0.5))
        tfce = act(lambda e: e.activation(out=nlcc_s[0:1, 0:32], in_=nlcc_s[0:1, 0:32],
                                          func=AF.Exp, scale=-1.0),
                   waits=[("mvq", tnmv)])
        tbneg = dve(lambda e: e.tensor_scalar(out=bcol_s[:, :], in0=bcol_s[:, :],
                                              scalar1=-1.0, scalar2=None, op0=ALU.mult),
                    waits=[("ac", te4)])

        if PHASE == "exp" and pr == NPAIR_RUN - 1:
            E.ops["sync"].append(lambda e, n=E.count["spd"], sem=E.sems["spd"]: e.wait_ge(sem, n))
            E.ops["gpsimd"].append(lambda e, n=E.count["gpd"], sem=E.sems["gpd"]: e.wait_ge(sem, n))
            E.finalize()
            return nc
        tfr = dve(lambda e: e.tensor_copy(
            out=fcrow_s[0:1, 0:32].rearrange("p (c u e) -> p c u e", c=2, u=2),
            in_=nlcc_s[0:1, 0:32].rearrange("p (c e u) -> p c u e", c=2, e=8)),
            waits=[("ac", tfce)])

        if PHASE == "fcx1" and pr == NPAIR_RUN - 1:
            E.ops["sync"].append(lambda e, n=E.count["spd"], sem=E.sems["spd"]: e.wait_ge(sem, n))
            E.ops["gpsimd"].append(lambda e, n=E.count["gpd"], sem=E.sems["gpd"]: e.wait_ge(sem, n))
            E.finalize()
            return nc
        wf = BK["p2"].for_write() + [("dv", tfr)]
        mm(ones_s[0:1, 0:64], fcrow_s[0:1, 0:8], p2[0:64, 96:104], True, True,
           waits=wf)
        mm(ones_s[0:1, 0:64], fcrow_s[0:1, 8:16], p2[64:128, 96:104], True, True,
           tp=(0, 64))

        if PHASE == "fcx2" and pr == NPAIR_RUN - 1:
            E.ops["sync"].append(lambda e, n=E.count["spd"], sem=E.sems["spd"]: e.wait_ge(sem, n))
            E.ops["gpsimd"].append(lambda e, n=E.count["gpd"], sem=E.sems["gpd"]: e.wait_ge(sem, n))
            E.finalize()
            return nc
        mm(ones_s[0:1, 0:64], fcrow_s[0:1, 16:24], p2[0:64, 104:112], True, True)
        fb4 = mm(ones_s[0:1, 0:64], fcrow_s[0:1, 24:32], p2[64:128, 104:112],
                 True, True, tp=(0, 64))
        BK["p2"].wrote("pe", fb4)

        if PHASE == "fcx3" and pr == NPAIR_RUN - 1:
            E.ops["sync"].append(lambda e, n=E.count["spd"], sem=E.sems["spd"]: e.wait_ge(sem, n))
            E.ops["gpsimd"].append(lambda e, n=E.count["gpd"], sem=E.sems["gpd"]: e.wait_ge(sem, n))
            E.finalize()
            return nc
        tfc = dve(lambda e, pr=pr: e.tensor_copy(out=fcx_p[:, pr * 16:pr * 16 + 16],
                                                 in_=p2[:, 96:112]),
                  waits=BK["p2"].for_read())
        BK["p2"].read("dv", tfc)

        if PHASE == "proj" and pr == NPAIR_RUN - 1:
            E.ops["sync"].append(lambda e, n=E.count["spd"], sem=E.sems["spd"]: e.wait_ge(sem, n))
            E.ops["gpsimd"].append(lambda e, n=E.count["gpd"], sem=E.sems["gpd"]: e.wait_ge(sem, n))
            E.finalize()
            return nc
        tkn = dve(lambda e: e.tensor_tensor(out=g16(kn_f), in0=g16(k_f),
                                            in1=bcast(rn16[:], HD), op=ALU.mult),
                  waits=[("ac", te6)])
        dve(lambda e: e.tensor_tensor(out=g16(ktil_tm), in0=g16(kn_f),
                                      in1=bcast(edec_s[:], HD), op=ALU.mult),
            waits=[("ac", te1), ("dv", tkn)])
        dve(lambda e: e.tensor_tensor(out=g16(kg_tm), in0=g16(kn_f),
                                      in1=bcast(rowg_s[:], HD), op=ALU.mult),
            waits=[("ac", te5), ("dv", tkn)])
        tbk = dve(lambda e: e.tensor_tensor(out=Uv3[:, :, 64:128], in0=g16(kn_f),
                                            in1=bcast(bcol_s[:], HD), op=ALU.mult),
                  waits=[("dv", tbneg), ("dv", tkn), ("ac", tv2)])
        dve(lambda e: e.tensor_tensor(out=g16(qm_tm), in0=g16(q_f),
                                      in1=bcast(rowm_s[:], HD), op=ALU.mult),
            waits=[("ac", te2)])
        dve(lambda e: e.tensor_tensor(out=g16(qf_tm), in0=g16(q_f),
                                      in1=bcast(el_s[:], HD), op=ALU.mult),
            waits=[("ac", te3)])
        tscal = E.tok("dv")
        tap("t_kn", kn_f[:, :], [("dv", tkn)])

        if PHASE == "scal" and pr == NPAIR_RUN - 1:
            E.ops["sync"].append(lambda e, n=E.count["spd"], sem=E.sems["spd"]: e.wait_ge(sem, n))
            E.ops["gpsimd"].append(lambda e, n=E.count["gpd"], sem=E.sems["gpd"]: e.wait_ge(sem, n))
            E.finalize()
            return nc

        for src, dstf in [(ktil_tm, ktil_f), (kg_tm, kg_f), (qm_tm, qm_f),
                          (qf_tm, qf_f)]:
            for hh in range(8):
                w = ([("dv", tscal)] + BK["p3"].for_write()) if hh == 0 else ()
                tt = trans(src[:, hh * 128:(hh + 1) * 128],
                           p3[:, hh * 128:(hh + 1) * 128], id_s[:, :], waits=w)
            BK["p3"].wrote("pe", tt)
            tev = act(lambda e, d=dstf: e.activation(out=d[:, :], in_=p3[:, :],
                                                     func=AF.Copy),
                      waits=BK["p3"].for_read())
            BK["p3"].read("ac", tev)
        t_fm = E.tok("ac")

        if PHASE == "tmfm" and pr == NPAIR_RUN - 1:
            E.ops["sync"].append(lambda e, n=E.count["spd"], sem=E.sems["spd"]: e.wait_ge(sem, n))
            E.ops["gpsimd"].append(lambda e, n=E.count["gpd"], sem=E.sems["gpd"]: e.wait_ge(sem, n))
            E.finalize()
            return nc

        def ghm(lf, rf, pm, pname, extra):
            first = True
            for par in range(2):
                for ci in range(2):
                    for h in range(par, NH, 2):
                        w = (extra + BK[pname].for_write()) if first else ()
                        first = False
                        t = mm(lf[rg(h):rg(h) + 64,
                                  hp(h) * 128 + ci * 64: hp(h) * 128 + ci * 64 + 64],
                               rf[rg(h):rg(h) + 64,
                                  hp(h) * 128 + ci * 64: hp(h) * 128 + ci * 64 + 64],
                               pm[ci * 64:ci * 64 + 64, h * 64:h * 64 + 64],
                               True, True, tp=(rg(h), ci * 64), waits=w)
            BK[pname].wrote("pe", t)
            return t

        ghm(ktil_f, kg_f, p4, "p4", [("ac", t_fm)])

        if PHASE == "gmm" and pr == NPAIR_RUN - 1:
            E.ops["sync"].append(lambda e, n=E.count["spd"], sem=E.sems["spd"]: e.wait_ge(sem, n))
            E.ops["gpsimd"].append(lambda e, n=E.count["gpd"], sem=E.sems["gpd"]: e.wait_ge(sem, n))
            E.finalize()
            return nc
        ghm(kg_f, ktil_f, p5, "p5", [])

        if PHASE == "hmm" and pr == NPAIR_RUN - 1:
            E.ops["sync"].append(lambda e, n=E.count["spd"], sem=E.sems["spd"]: e.wait_ge(sem, n))
            E.ops["gpsimd"].append(lambda e, n=E.count["gpd"], sem=E.sems["gpd"]: e.wait_ge(sem, n))
            E.finalize()
            return nc
        tge = dve(lambda e: e.tensor_tensor(
            out=G_sb[0][:, :], in0=p4[:, :],
            in1=mG_s[:, :], op=ALU.mult),
            waits=BK["p4"].for_read())
        BK["p4"].read("dv", tge)
        the = dve(lambda e: e.tensor_tensor(
            out=H_sb[0][:, :], in0=p5[:, :],
            in1=mH_s[:, :], op=ALU.mult),
            waits=BK["p5"].for_read())
        BK["p5"].read("dv", the)

        if PHASE == "gev2" and pr == NPAIR_RUN - 1:
            E.ops["sync"].append(lambda e, n=E.count["spd"], sem=E.sems["spd"]: e.wait_ge(sem, n))
            E.ops["gpsimd"].append(lambda e, n=E.count["gpd"], sem=E.sems["gpd"]: e.wait_ge(sem, n))
            E.finalize()
            return nc
        ghm(ktil_f, kg_f if PHASE == "gev4" else qm_f, p4, "p4", [])

        if PHASE == "gev" and pr == NPAIR_RUN - 1:
            E.ops["sync"].append(lambda e, n=E.count["spd"], sem=E.sems["spd"]: e.wait_ge(sem, n))
            E.ops["gpsimd"].append(lambda e, n=E.count["gpd"], sem=E.sems["gpd"]: e.wait_ge(sem, n))
            E.finalize()
            return nc

        if PHASE in ("gev3", "gev4") and pr == NPAIR_RUN - 1:
            E.ops["sync"].append(lambda e, n=E.count["spd"], sem=E.sems["spd"]: e.wait_ge(sem, n))
            E.ops["gpsimd"].append(lambda e, n=E.count["gpd"], sem=E.sems["gpd"]: e.wait_ge(sem, n))
            E.finalize()
            return nc
        tme = dve(lambda e: e.tensor_tensor(
            out=MT_sb[:, :], in0=p4[:, :],
            in1=mM_s[:, :], op=ALU.mult),
            waits=BK["p4"].for_read())
        BK["p4"].read("dv", tme)
        tap("t_G", G_sb[0][:, :], [("dv", tge)])
        tap("t_MT", MT_sb[:, :], [("dv", tme)])

        if PHASE == "ghm" and pr == NPAIR_RUN - 1:
            E.ops["sync"].append(lambda e, n=E.count["spd"], sem=E.sems["spd"]: e.wait_ge(sem, n))
            E.ops["gpsimd"].append(lambda e, n=E.count["gpd"], sem=E.sems["gpd"]: e.wait_ge(sem, n))
            E.finalize()
            return nc


        cur = 0
        u_dep = [("dv", tge), ("dv", the), ("dv", tbk), ("ac", tv2)]
        for s in range(6):
            first = True
            for ci in range(2):
                for h in range(NH):
                    pm = p4 if h < 8 else p5
                    col = (h % 8) * 128
                    w = ()
                    if first:
                        w = u_dep + BK["p4"].for_write() + BK["p5"].for_write()
                        first = False
                    t = mm(G_sb[cur][ci * 64:ci * 64 + 64, h * 64:h * 64 + 64],
                           U_s[ci * 64:ci * 64 + 64, h * 128:h * 128 + 128],
                           pm[ci * 64:ci * 64 + 64, col:col + 128],
                           True, True, tp=(ci * 64, ci * 64), waits=w)
            BK["p4"].wrote("pe", t)
            BK["p5"].wrote("pe", t)
            ua = dve(lambda e: e.tensor_tensor(out=U_s[:, 0:1024], in0=p4[:, :],
                                               in1=U_s[:, 0:1024], op=ALU.add),
                     waits=BK["p4"].for_read())
            BK["p4"].read("dv", ua)
            ub = dve(lambda e: e.tensor_tensor(out=U_s[:, 1024:2048], in0=p5[:, :],
                                               in1=U_s[:, 1024:2048], op=ALU.add),
                     waits=BK["p5"].for_read())
            BK["p5"].read("dv", ub)
            u_dep = [("dv", ub)]
            if s < 5:
                first = True
                for ci in range(2):
                    for h in range(NH):
                        w = ()
                        if first:
                            w = [("dv", ua), ("dv", ub)] + BK["p4"].for_write()
                            first = False
                        t = mm(H_sb[cur][ci * 64:ci * 64 + 64, h * 64:h * 64 + 64],
                               G_sb[cur][ci * 64:ci * 64 + 64, h * 64:h * 64 + 64],
                               p4[ci * 64:ci * 64 + 64, h * 64:h * 64 + 64],
                               True, True, tp=(ci * 64, ci * 64), waits=w)
                BK["p4"].wrote("pe", t)
                tgs = act(lambda e, nx=1 - cur: e.activation(out=G_sb[nx][:, :],
                                                             in_=p4[:, :], func=AF.Copy),
                          waits=BK["p4"].for_read())
                BK["p4"].read("ac", tgs)
                if s < 4:
                    first = True
                    for ci in range(2):
                        for h in range(NH):
                            w = ()
                            if first:
                                w = BK["p5"].for_write()
                                first = False
                            t = mm(G_sb[cur][ci * 64:ci * 64 + 64, h * 64:h * 64 + 64],
                                   H_sb[cur][ci * 64:ci * 64 + 64, h * 64:h * 64 + 64],
                                   p5[ci * 64:ci * 64 + 64, h * 64:h * 64 + 64],
                                   True, True, tp=(ci * 64, ci * 64), waits=w)
                    BK["p5"].wrote("pe", t)
                    ths = act(lambda e, nx=1 - cur: e.activation(out=H_sb[nx][:, :],
                                                                 in_=p5[:, :],
                                                                 func=AF.Copy),
                              waits=BK["p5"].for_read())
                    BK["p5"].read("ac", ths)
                cur = 1 - cur
        u_done = u_dep[0][1]
        tap("t_U", U_s[:, :], [("dv", u_done)])

        if PHASE == "solve" and pr == NPAIR_RUN - 1:
            E.ops["sync"].append(lambda e, n=E.count["spd"], sem=E.sems["spd"]: e.wait_ge(sem, n))
            E.ops["gpsimd"].append(lambda e, n=E.count["gpd"], sem=E.sems["gpd"]: e.wait_ge(sem, n))
            E.finalize()
            return nc


        def bak(lsl, rsl, pm, pname, extra):
            first = True
            for ci in range(2):
                for h in range(NH):
                    w = (extra + BK[pname].for_write()) if first else ()
                    first = False
                    t = mm(lsl(ci, h), rsl(ci, h),
                           pm[rg(h):rg(h) + 64,
                              hp(h) * 128 + ci * 64: hp(h) * 128 + ci * 64 + 64],
                           True, True, tp=(ci * 64, rg(h)), waits=w)
            BK[pname].wrote("pe", t)
            return t

        bak(lambda ci, h: ktil_tm[ci * 64:ci * 64 + 64, h * 64:h * 64 + 64],
            lambda ci, h: U_s[ci * 64:ci * 64 + 64, h * 128:h * 128 + 64],
            p4, "p4", [("dv", u_done)])
        tbe = act(lambda e: e.activation(out=B_sb[:, :], in_=p4[:, :], func=AF.Copy),
                  waits=BK["p4"].for_read())
        BK["p4"].read("ac", tbe)
        bak(lambda ci, h: U_s[ci * 64:ci * 64 + 64, h * 128 + 64:h * 128 + 128],
            lambda ci, h: ktil_tm[ci * 64:ci * 64 + 64, h * 64:h * 64 + 64],
            p5, "p5", [])
        tae = act(lambda e: e.activation(out=AT_sb[:, :], in_=p5[:, :], func=AF.Copy),
                  waits=BK["p5"].for_read() + [("spd", E.tok("spd"))])
        BK["p5"].read("ac", tae)
        for ci in range(2):
            dma("spd", d_A[2 * pr + ci, :, :].rearrange("p (g i) -> p g i", g=8),
                AT_sb[:].rearrange("p (g c i) -> p g c i", g=8, c=2)[:, :, ci, :],
                waits=[("ac", tae)])
        bak(lambda ci, h: U_s[ci * 64:ci * 64 + 64, h * 128 + 64:h * 128 + 128],
            lambda ci, h: MT_sb[ci * 64:ci * 64 + 64, h * 64:h * 64 + 64],
            p4, "p4", [("dv", tme)])
        tqe = dve(lambda e: e.tensor_tensor(out=Qt_sb[:, :], in0=p4[:, :],
                                            in1=qf_f[:, :], op=ALU.add),
                  waits=BK["p4"].for_read() + [("ac", t_fm), ("spd", E.tok("spd"))])
        BK["p4"].read("dv", tqe)
        for ci in range(2):
            dma("spd", d_Qt[2 * pr + ci, :, :].rearrange("p (g i) -> p g i", g=8),
                Qt_sb[:].rearrange("p (g c i) -> p g c i", g=8, c=2)[:, :, ci, :],
                waits=[("dv", tqe)])
        tap("t_AT", AT_sb[:, :], [("ac", tae)])
        tap("t_B", B_sb[:, :], [("ac", tbe)])
        tap("t_Qt", Qt_sb[:, :], [("dv", tqe)])

        if PHASE == "bak" and pr == NPAIR_RUN - 1:
            E.ops["sync"].append(lambda e, n=E.count["spd"], sem=E.sems["spd"]: e.wait_ge(sem, n))
            E.ops["gpsimd"].append(lambda e, n=E.count["gpd"], sem=E.sems["gpd"]: e.wait_ge(sem, n))
            E.finalize()
            return nc


        for ci in range(2):
            c = 2 * pr + ci
            Scur, Snxt = S_ab[c % 2], S_ab[(c + 1) % 2]
            Pcur, Pnxt = P_ab[c % 2], P_ab[(c + 1) % 2]
            wS = [("dv", s_evac), ("ac", p_evac), ("dv", tfc)]
            fsl = fcx_p[:, c * 8:(c + 1) * 8]
            tSF = dve(lambda e, S=Scur, f=fsl: e.tensor_tensor(
                out=g8(SF_t), in0=g8(S), in1=bcast(f, 64), op=ALU.mult), waits=wS)
            tPF = dve(lambda e, P=Pcur, f=fsl: e.tensor_tensor(
                out=g8(PF_t), in0=g8(P), in1=bcast(f, 64), op=ALU.mult))
            first = True
            for h in range(NH):
                a_sl = AT_sb[rg(h):rg(h) + 64,
                             hp(h) * 128 + ci * 64: hp(h) * 128 + ci * 64 + 64]
                w = ([("dv", tPF), ("ac", tae)] + BK["p2"].for_write() +
                     BK["p0"].for_write()) if first else ()
                first = False
                mm(a_sl, Scur[rg(h):rg(h) + 64, hp(h) * 64: hp(h) * 64 + 64],
                   p2[rg(h):rg(h) + 64, hp(h) * 64: hp(h) * 64 + 64],
                   True, False, tp=(rg(h), rg(h)), waits=w)
                mm(id_s[rg(h):rg(h) + 64, rg(h):rg(h) + 64],
                   SF_t[rg(h):rg(h) + 64, hp(h) * 64: hp(h) * 64 + 64],
                   p2[rg(h):rg(h) + 64, hp(h) * 64: hp(h) * 64 + 64],
                   False, True, tp=(rg(h), rg(h)))
                mm(a_sl, Pcur[rg(h):rg(h) + 64, hp(h) * 64: hp(h) * 64 + 64],
                   p0[rg(h):rg(h) + 64, hp(h) * 64: hp(h) * 64 + 64],
                   True, False, tp=(rg(h), rg(h)))
                tpl = mm(id_s[rg(h):rg(h) + 64, rg(h):rg(h) + 64],
                         PF_t[rg(h):rg(h) + 64, hp(h) * 64: hp(h) * 64 + 64],
                         p0[rg(h):rg(h) + 64, hp(h) * 64: hp(h) * 64 + 64],
                         False, True, tp=(rg(h), rg(h)))
            BK["p2"].wrote("pe", tpl)
            BK["p0"].wrote("pe", tpl)
            s_evac = dve(lambda e, S=Snxt, ci=ci: e.tensor_tensor(
                out=g8(S), in0=g8(p2),
                in1=B_sb[:].rearrange("p (g c i) -> p g c i", g=8, c=2)[:, :, ci, :],
                op=ALU.add), waits=BK["p2"].for_read() + [("ac", tbe)])
            BK["p2"].read("dv", s_evac)
            p_evac = act(lambda e, P=Pnxt: e.activation(out=P[:, :], in_=p0[:, :],
                                                        func=AF.Copy),
                         waits=BK["p0"].for_read())
            BK["p0"].read("ac", p_evac)
            first = True
            for h in range(NH):
                w = ([("dv", tqe)] + BK["p5"].for_write()) if first else ()
                first = False
                mm(Scur[rg(h):rg(h) + 64, hp(h) * 64: hp(h) * 64 + 64],
                   Qt_sb[rg(h):rg(h) + 64,
                         hp(h) * 128 + ci * 64: hp(h) * 128 + ci * 64 + 64],
                   p5[rg(h):rg(h) + 64, hp(h) * 64: hp(h) * 64 + 64],
                   True, True, tp=(rg(h), rg(h)), waits=w)
            for h in range(NH):
                to2 = mm(U_s[ci * 64:ci * 64 + 64, h * 128:h * 128 + 64],
                         MT_sb[ci * 64:ci * 64 + 64, h * 64:h * 64 + 64],
                         p5[rg(h):rg(h) + 64,
                            512 + hp(h) * 64: 512 + hp(h) * 64 + 64],
                         True, True, tp=(ci * 64, rg(h)))
            BK["p5"].wrote("pe", to2)
            to2e = act(lambda e: e.activation(out=o2_t[:, :], in_=p5[:, 512:1024],
                                              func=AF.Copy),
                       waits=BK["p5"].for_read())
            toe = dve(lambda e: e.tensor_tensor(out=ostg[:, :], in0=p5[:, 0:512],
                                                in1=o2_t[:, :], op=ALU.add),
                      waits=BK["p5"].for_read() + [("ac", to2e), ("spd", E.tok("spd"))])
            BK["p5"].read("dv", toe)
            tos = dma("spd", d_out[c, :, :], ostg[:, :], waits=[("dv", toe)])
            g_spd = tos
        tap("t_S", S_ab[NCH % 2][:, :] if pr == NPAIR - 1 else S_ab[0][:, :],
            [("dv", s_evac)] if pr == NPAIR - 1 else [("dv", s_evac)])

        if pr + 2 < NPAIR:
            i = pr % 2
            xst_tok[i] = dma(
                f"xw{i}", xst_[i][:].rearrange("p (kc t) -> p kc t", kc=NKC),
                xT.rearrange("(kc p) t -> p kc t", p=128)
                [:, :, (pr + 2) * 128:(pr + 3) * 128],
                waits=[("pe", E.tok("pe")), ("ac", xw_tok[i])])
            xw_tok[i] = act(lambda e, i=i: e.activation(
                out=xwb_[i][:, :], in_=xst_[i][:, :], func=AF.Copy),
                waits=[(f"xw{i}", xst_tok[i]), ("pe", E.tok("pe"))])
        g_pe, g_dv, g_ac = E.tok("pe"), E.tok("dv"), E.tok("ac")

    tap("t_fcx", fcx_p[:, :], [("ac", g_ac)])

    if PHASE == "pairs":
        E.ops["sync"].append(lambda e, n=E.count["spd"], sem=E.sems["spd"]: e.wait_ge(sem, n))
        E.ops["gpsimd"].append(lambda e, n=E.count["gpd"], sem=E.sems["gpd"]: e.wait_ge(sem, n))
        E.finalize()
        return nc

    Pfin = P_ab[NCH % 2]
    first = True
    for h in range(NH):
        w = ([("ac", p_evac)] + BK["p3"].for_write()) if first else ()
        first = False
        tl = trans(Pfin[rg(h):rg(h) + 64, hp(h) * 64: hp(h) * 64 + 64],
                   p3[rg(h):rg(h) + 64, hp(h) * 64: hp(h) * 64 + 64],
                   id_s[rg(h):rg(h) + 64, rg(h):rg(h) + 64],
                   tp=(rg(h), rg(h)), waits=w)
    BK["p3"].wrote("pe", tl)
    tpt = act(lambda e: e.activation(out=ptseg[:, :], in_=p3[:, 0:512], func=AF.Copy),
              waits=BK["p3"].for_read())
    BK["p3"].read("ac", tpt)
    dma("spd", ccin[:, 0:512], ptseg[:, :], waits=[("ac", tpt)])
    dma("spd", ccin[:, 512:1024], S_ab[NCH % 2][:, :], waits=[("dv", s_evac)])
    cc_w = E.tok("spd")
    tcc = gpe(lambda e: e.collective_compute(
        "AllGather", ALU.bypass, replica_groups=[[0, 1, 2, 3], [4, 5, 6, 7]],
        ins=[ccin[:, :]], outs=[ccout[:, :]]), waits=[("spd", cc_w)])
    tgl = dma("spd", gat_s[:].rearrange("p (r c) -> p r c", r=4),
              ccout.rearrange("(r p) c -> p r c", p=128),
              waits=[("gp", tcc)])
    tz = dve(lambda e: e.memset(rpj[0][:, :], 0.0))
    last_dv = tz
    for j in range(3):
        first = True
        for h in range(NH):
            w = ([("spd", tgl), ("dv", last_dv)] + BK["p2"].for_write()) if first else ()
            first = False
            tcj = mm(gat_s[rg(h):rg(h) + 64,
                           j * 1024 + hp(h) * 64: j * 1024 + hp(h) * 64 + 64],
                     rpj[j][rg(h):rg(h) + 64, hp(h) * 64: hp(h) * 64 + 64],
                     p2[rg(h):rg(h) + 64, hp(h) * 64: hp(h) * 64 + 64],
                     True, True, tp=(rg(h), rg(h)), waits=w)
        BK["p2"].wrote("pe", tcj)
        last_dv = dve(lambda e, j=j: e.tensor_tensor(
            out=rpj[j + 1][:, :], in0=p2[:, :],
            in1=gat_s[:, j * 1024 + 512: j * 1024 + 1024], op=ALU.add),
            waits=BK["p2"].for_read() + [("spd", tgl)])
        BK["p2"].read("dv", last_dv)
    dve(lambda e: e.tensor_scalar(out=sin_s[:, :], in0=rpj[1][:, :],
                                  scalar1=sel_s[:, 1:2], scalar2=None, op0=ALU.mult),
        waits=[("dv", last_dv), ("spd", ld_c)])
    dve(lambda e: e.scalar_tensor_tensor(out=sin_s[:, :], in0=rpj[2][:, :],
                                         scalar=sel_s[:, 2:3], in1=sin_s[:, :],
                                         op0=ALU.mult, op1=ALU.add),
        waits=[("dv", E.tok("dv"))])
    dve(lambda e: e.scalar_tensor_tensor(out=sin_s[:, :], in0=rpj[3][:, :],
                                         scalar=sel_s[:, 3:4], in1=sin_s[:, :],
                                         op0=ALU.mult, op1=ALU.add),
        waits=[("dv", E.tok("dv"))])
    tsin = dve(lambda e: e.tensor_copy(out=dlt[0][:, :], in_=sin_s[:, :]),
               waits=[("dv", E.tok("dv"))])
    tap("t_sin", sin_s[:, :], [("dv", tsin)])

    if PHASE == "gather":
        E.ops["sync"].append(lambda e, n=E.count["spd"], sem=E.sems["spd"]: e.wait_ge(sem, n))
        E.ops["gpsimd"].append(lambda e, n=E.count["gpd"], sem=E.sems["gpd"]: e.wait_ge(sem, n))
        E.finalize()
        return nc

    qa_tok = [0, 0]
    dl_ev = tsin
    for c in range(2):
        dma(f"spd{c}", qtc[c][:, :], d_Qt[c, :, :], waits=[("spd", E.tok("spd"))])
        dma(f"spd{c}", atc[c][:, :], d_A[c, :, :])
        qa_tok[c] = dma(f"spd{c}", och[c][:, :], d_out[c, :, :])
    for c in range(NCH):
        ri = c % 2
        first = True
        for h in range(NH):
            w = ([(f"spd{ri}", qa_tok[ri]), ("dv", dl_ev if c == 0 else 0),
                  ("ac", 0 if c == 0 else dl_ev)] + BK["p5"].for_write()) if first else ()
            first = False
            tocm = mm(dlt[c % 2][rg(h):rg(h) + 64, hp(h) * 64: hp(h) * 64 + 64],
                      qtc[ri][rg(h):rg(h) + 64, hp(h) * 64: hp(h) * 64 + 64],
                      p5[rg(h):rg(h) + 64, hp(h) * 64: hp(h) * 64 + 64],
                      True, True, tp=(rg(h), rg(h)), waits=w)
        BK["p5"].wrote("pe", tocm)
        tca = dve(lambda e, ri=ri: e.tensor_tensor(out=och[ri][:, :], in0=p5[:, 0:512],
                                                   in1=och[ri][:, :], op=ALU.add),
                  waits=BK["p5"].for_read())
        BK["p5"].read("dv", tca)
        tcs = dma(f"spd{ri}", d_out[c, :, :], och[ri][:, :], waits=[("dv", tca)])
        tdf = dve(lambda e, c=c: e.tensor_tensor(
            out=g8(PF_t), in0=g8(dlt[c % 2]),
            in1=bcast(fcx_p[:, c * 8:(c + 1) * 8], 64), op=ALU.mult),
            waits=[("dv", tsin)] if c == 0 else [("ac", dl_ev), ("dv", tca)])
        first = True
        for h in range(NH):
            w = ([("dv", tdf)] + BK["p2"].for_write()) if first else ()
            first = False
            mm(atc[ri][rg(h):rg(h) + 64, hp(h) * 64: hp(h) * 64 + 64],
               dlt[c % 2][rg(h):rg(h) + 64, hp(h) * 64: hp(h) * 64 + 64],
               p2[rg(h):rg(h) + 64, hp(h) * 64: hp(h) * 64 + 64],
               True, False, tp=(rg(h), rg(h)), waits=w)
            tdm = mm(id_s[rg(h):rg(h) + 64, rg(h):rg(h) + 64],
                     PF_t[rg(h):rg(h) + 64, hp(h) * 64: hp(h) * 64 + 64],
                     p2[rg(h):rg(h) + 64, hp(h) * 64: hp(h) * 64 + 64],
                     False, True, tp=(rg(h), rg(h)))
        BK["p2"].wrote("pe", tdm)
        dl_ev = act(lambda e, nx=(c + 1) % 2: e.activation(out=dlt[nx][:, :],
                                                           in_=p2[:, :], func=AF.Copy),
                    waits=BK["p2"].for_read())
        BK["p2"].read("ac", dl_ev)
        if c + 2 < NCH:
            dma(f"spd{ri}", qtc[ri][:, :], d_Qt[c + 2, :, :],
                waits=[("pe", tocm), ("dv", tca), ("ac", dl_ev),
                       (f"spd{ri}", tcs)])
            dma(f"spd{ri}", atc[ri][:, :], d_A[c + 2, :, :])
            qa_tok[ri] = dma(f"spd{ri}", och[ri][:, :], d_out[c + 2, :, :])
    corr_dma = E.tok("spd")
    corr_dma01 = (E.tok("spd0"), E.tok("spd1"))
    if PHASE == "corr":
        E.ops["sync"].append(lambda e, n=E.count["spd"], sem=E.sems["spd"]: e.wait_ge(sem, n))
        E.ops["gpsimd"].append(lambda e, n=E.count["gpd"], sem=E.sems["gpd"]: e.wait_ge(sem, n))
        E.finalize()
        return nc
    ph1 = [("pe", E.tok("pe")), ("dv", E.tok("dv")), ("ac", E.tok("ac")),
           ("spd", E.tok("spd")), ("gpd", E.tok("gpd")), ("gp", E.tok("gp"))]

    def ln_stats(buf, n, sq_ap, apply_list, wtok_pe, prev_reads):
        first = True
        for kc in range(n):
            w = (prev_reads + BK["p0"].for_write()) if first else ()
            first = False
            t1 = mm(ones_s[:, 0:1], buf[:, kc * 512:(kc + 1) * 512],
                    p0[0:1, 0:512], kc == 0, kc == n - 1, waits=w)
        BK["p0"].wrote("pe", t1)
        first = True
        for kc in range(n):
            w = (BK["p1"].for_write() + [("ac", sq_ap)]) if first else ()
            first = False
            t2 = mm(ones_s[:, 0:1], sq_w[:, kc * 512:(kc + 1) * 512],
                    p1[0:1, 0:512], kc == 0, kc == n - 1, waits=w)
        BK["p1"].wrote("pe", t2)
        tmu = dve(lambda e: e.tensor_scalar(out=row_t[0:1, 0:512], in0=p0[0:1, :],
                                            scalar1=1.0 / DM, scalar2=None,
                                            op0=ALU.mult), waits=BK["p0"].for_read())
        BK["p0"].read("dv", tmu)
        tv_ = dve(lambda e: e.tensor_scalar(out=row_t[0:1, 512:1024], in0=p1[0:1, :],
                                            scalar1=1.0 / DM, scalar2=None,
                                            op0=ALU.mult), waits=BK["p1"].for_read())
        BK["p1"].read("dv", tv_)
        t_r = dve(lambda e: e.tensor_tensor(out=row_t[0:1, 1024:1536],
                                            in0=row_t[0:1, 0:512], in1=row_t[0:1, 0:512],
                                            op=ALU.mult), waits=[("dv", tmu)])
        t_r = dve(lambda e: e.tensor_tensor(out=row_t[0:1, 512:1024],
                                            in0=row_t[0:1, 512:1024],
                                            in1=row_t[0:1, 1024:1536], op=ALU.subtract),
                  waits=[("dv", t_r)])
        tva = dve(lambda e: e.tensor_scalar(out=row_t[0:1, 512:1024],
                                            in0=row_t[0:1, 512:1024], scalar1=1e-5,
                                            scalar2=None, op0=ALU.add),
                  waits=[("dv", t_r)])
        tsd = act(lambda e: e.activation(out=row_t[0:1, 1024:1536],
                                         in_=row_t[0:1, 512:1024], func=AF.Sqrt),
                  waits=[("dv", tva)])
        tiv = dve(lambda e: e.reciprocal(out=row_t[0:1, 512:1024],
                                         in_=row_t[0:1, 1024:1536]),
                  waits=[("ac", tsd)])
        wb = BK["p4"].for_write() + [("dv", tiv)]
        mm(ones_s[0:1, :], row_t[0:1, 0:512], p4[:, 0:512], True, True, waits=wb)
        tb2 = mm(ones_s[0:1, :], row_t[0:1, 512:1024], p4[:, 512:1024], True, True)
        BK["p4"].wrote("pe", tb2)
        return tb2

    x1_dma = []
    for w in range(NWIN):
        t0 = w * 512
        tx = dma("spd", xf_w[:].rearrange("p (kc t) -> p kc t", kc=NKC),
                 xT.rearrange("(kc p) t -> p kc t", p=128)[:, :, t0:t0 + 512],
                 waits=(ph1 + [("spd0", corr_dma01[0]), ("spd1", corr_dma01[1])])
                 if w == 0 else [("spd", corr_dma)])
        tow = dma("spd", outw[:].rearrange("p (kc c i) -> p kc c i", kc=NKC, c=8),
                  d_out[w * 8:(w + 1) * 8, :, :]
                  .rearrange("c p (kc i) -> p kc c i", kc=8),
                  waits=[("spd", corr_dma)])
        first = True
        for i in range(4):
            wt = ((ph1 if w == 0 else []) + BK["p3"].for_write()) if first else ()
            first = False
            tg = trans(g1_tm[:, (w * 4 + i) * 64:(w * 4 + i + 1) * 64],
                       p3[0:64, i * 128:(i + 1) * 128], id_s[:, :], waits=wt)
        BK["p3"].wrote("pe", tg)
        tgf = act(lambda e: e.activation(out=g1f_w[:, :], in_=p3[0:64, 0:512],
                                         func=AF.Copy), waits=BK["p3"].for_read())
        BK["p3"].read("ac", tgf)
        for kc in range(NKC):
            pm, bk = (p0, "p0") if kc % 2 == 0 else (p1, "p1")
            wt = [("ac", tgf)] + BK[bk].for_write()
            tgm = mm(wg2_s[:, kc * 128:(kc + 1) * 128], g1f_w[:, :], pm[:, :],
                     True, True, waits=wt)
            BK[bk].wrote("pe", tgm)
            tsg = act(lambda e, kc=kc, pm=pm: e.activation(
                out=gate_w[:, kc * 512:(kc + 1) * 512], in_=pm[:, :],
                func=AF.Sigmoid), waits=BK[bk].for_read())
            BK[bk].read("ac", tsg)
        tgate = E.tok("ac")
        tap("t_gate", gate_w[:, :], [("ac", tgate)]) if w == 0 else None
        tog = dve(lambda e: e.tensor_tensor(out=og_w[:, :], in0=outw[:, :],
                                            in1=gate_w[:, :], op=ALU.mult),
                  waits=[("ac", tgate), ("spd", tow)])
        tsq = act(lambda e: e.activation(out=sq_w[:, :], in_=og_w[:, :],
                                         func=AF.Square), waits=[("dv", tog)])
        tb2 = ln_stats(og_w, 8, tsq, None, None, [("dv", tog)])
        for kc in range(NKC):
            t_ = dve(lambda e, kc=kc: e.tensor_tensor(
                out=og_w[:, kc * 512:(kc + 1) * 512],
                in0=og_w[:, kc * 512:(kc + 1) * 512], in1=p4[:, 0:512],
                op=ALU.subtract),
                waits=(BK["p4"].for_read() + [("ac", tsq)]) if kc == 0 else ())
            t_ = dve(lambda e, kc=kc: e.scalar_tensor_tensor(
                out=on_w[:, kc * 512:(kc + 1) * 512],
                in0=og_w[:, kc * 512:(kc + 1) * 512],
                scalar=normw_s[:, kc:kc + 1], in1=p4[:, 512:1024],
                op0=ALU.mult, op1=ALU.mult), waits=[("dv", t_)])
        BK["p4"].read("dv", t_)
        ton = t_
        for m in range(NKC):
            pm, bk = (p2, "p2") if m % 2 == 0 else (p5, "p5")
            pslice = pm[:, 0:512]
            first = True
            for kc in range(NKC):
                wt = ([("dv", ton), ("spd", tow)] + BK[bk].for_write()) if first else ()
                first = False
                twm = mm(wo_s[:, kc * DM + m * 128: kc * DM + m * 128 + 128],
                         on_w[:, kc * 512:(kc + 1) * 512], pslice,
                         kc == 0, kc == NKC - 1, waits=wt)
            BK[bk].wrote("pe", twm)
            tr1 = dve(lambda e, m=m, pslice=pslice: e.tensor_tensor(
                out=x1f_w[:, m * 512:(m + 1) * 512], in0=pslice,
                in1=xf_w[:, m * 512:(m + 1) * 512], op=ALU.add),
                waits=BK[bk].for_read())
            BK[bk].read("dv", tr1)
        tsq2 = act(lambda e: e.activation(out=sq_w[:, :], in_=x1f_w[:, :],
                                          func=AF.Square), waits=[("dv", tr1)])
        tb2 = ln_stats(x1f_w, 8, tsq2, None, None, [("dv", tr1)])
        for kc in range(NKC):
            t_ = dve(lambda e, kc=kc: e.tensor_tensor(
                out=x1f_w[:, kc * 512:(kc + 1) * 512],
                in0=x1f_w[:, kc * 512:(kc + 1) * 512], in1=p4[:, 0:512],
                op=ALU.subtract),
                waits=(BK["p4"].for_read() + [("ac", tsq2)]) if kc == 0 else ())
            t_ = dve(lambda e, kc=kc: e.tensor_tensor(
                out=x1f_w[:, kc * 512:(kc + 1) * 512],
                in0=x1f_w[:, kc * 512:(kc + 1) * 512], in1=p4[:, 512:1024],
                op=ALU.mult), waits=[("dv", t_)])
        BK["p4"].read("dv", t_)
        tx1 = t_
        tx1b = act(lambda e: e.activation(out=x1b_w[:, :], in_=x1f_w[:, :],
                                          func=AF.Copy), waits=[("dv", tx1)])
        tap("t_x1", x1f_w[:, 0:4096], [("dv", tx1)]) if w == 0 else None
        d1 = dma("spd", d_x1f[:, :, t0:t0 + 512].rearrange("kc p t -> p kc t"),
                 x1f_w[:].rearrange("p (kc t) -> p kc t", kc=NKC),
                 waits=[("dv", tx1)])
        d2 = dma("spd", d_x1b[:, :, t0:t0 + 512].rearrange("kc p t -> p kc t"),
                 x1b_w[:].rearrange("p (kc t) -> p kc t", kc=NKC),
                 waits=[("ac", tx1b)])
        x1_dma.append(d2)
        g_ac = E.tok("ac")
    tail_dma = E.tok("spd")
    if PHASE == "tail":
        E.ops["sync"].append(lambda e, n=E.count["spd"], sem=E.sems["spd"]: e.wait_ge(sem, n))
        E.ops["gpsimd"].append(lambda e, n=E.count["gpd"], sem=E.sems["gpd"]: e.wait_ge(sem, n))
        E.finalize()
        return nc

    ph2 = [("pe", E.tok("pe")), ("dv", E.tok("dv")), ("ac", E.tok("ac")),
           ("spd", E.tok("spd")), ("gpd", E.tok("gpd"))]
    y_dma = []
    y_w = sq2_w
    g_pe2, g_dv2, g_ac2 = E.tok("pe"), E.tok("dv"), E.tok("ac")
    for wi in range(4):
        woff = wi * 512
        txb = dma("spd", x1w_b[:].rearrange("p (kc t) -> p kc t", kc=NKC),
                  d_x1b[:, :, woff:woff + 512].rearrange("kc p t -> p kc t"),
                  waits=(ph2 if wi == 0 else [("pe", g_pe2), ("dv", g_dv2),
                                              ("ac", g_ac2)]))
        dmw = [0, 0]

        def l1_load(g, mg2, w0):
            dma(f"gpd{g}", hstr[g][:, 0:2048].rearrange("p (kc c) -> p kc c", kc=8),
                d_w1b.rearrange("(kc p) c -> p kc c", p=128)
                [:, :, mg2 * 256:(mg2 + 1) * 256], waits=w0)
            tok = dma(f"gpd{g}", hstr[g][:, 2048:4096].rearrange("p (kc c) -> p kc c", kc=8),
                      d_w2b.rearrange("(kc p) c -> p kc c", p=128)
                      [:, :, mg2 * 256:(mg2 + 1) * 256])
            return tok

        for g in range(2):
            dmw[g] = l1_load(g, g, ((ph2 if wi == 0 and g == 0 else [("gpd", ld_w)]) +
                                    [("cst", cast_done),
                                     (f"gpd{g}", E.tok(f"gpd{g}")), ("pe", E.tok("pe"))]))
        prev_tac = 0
        for mg in range(16):
            ri = mg % 2
            for mi in range(2):
                m = mg * 2 + mi
                pm1, bk1 = (p4, "p4") if m % 2 == 0 else (p5, "p5")
                first = True
                for kc in range(NKC):
                    wt = ([(f"gpd{ri}", dmw[ri]), ("spd", txb)] +
                          BK[bk1].for_write()) if first else ()
                    first = False
                    mm(hstr[ri][:, kc * 256 + mi * 128: kc * 256 + mi * 128 + 128],
                       x1w_b[:, kc * 512:(kc + 1) * 512],
                       pm1[:, 0:512], kc == 0, kc == NKC - 1, waits=wt)
                    t1 = mm(hstr[ri][:, 2048 + kc * 256 + mi * 128:
                                     2048 + kc * 256 + mi * 128 + 128],
                            x1w_b[:, kc * 512:(kc + 1) * 512],
                            pm1[:, 512:1024], kc == 0, kc == NKC - 1)
                BK[bk1].wrote("pe", t1)
                tsl = act(lambda e, m=m, pm1=pm1: e.activation(
                    out=silu_t[:, :], in_=pm1[:, 0:512], func=AF.Silu,
                    scale=scl_s[:, m:m + 1]),
                    waits=BK[bk1].for_read() + [("dv", prev_tac)])
                tac = dve(lambda e, m=m, pm1=pm1: e.scalar_tensor_tensor(
                    out=a_sb[:, m * 512:(m + 1) * 512], in0=pm1[:, 512:1024],
                    scalar=scl_s[:, 32 + m:33 + m], in1=silu_t[:, :],
                    op0=ALU.mult, op1=ALU.mult), waits=[("ac", tsl)])
                BK[bk1].read("dv", tac)
                prev_tac = tac
            if mg + 2 < 16:
                dmw[ri] = l1_load(ri, mg + 2,
                                  [("pe", E.tok("pe")), ("dv", E.tok("dv")),
                                   (f"gpd{ri}", E.tok(f"gpd{ri}"))])
        a_done = E.tok("dv")

        def hload(hT, hB, g, mg2, w0):
            return dma(f"gpd{g}", hstr[g][:].rearrange("p (kc c) -> p kc c", kc=32),
                       hB.rearrange("(kc p) c -> p kc c", p=128)
                       [:, :, mg2 * 256:(mg2 + 1) * 256], waits=w0)

        def hlayer(hT, hB, src, dst, scol, dep):
            dmh = [0, 0]
            for g in range(2):
                dmh[g] = hload(hT, hB, g, g,
                               [("pe", E.tok("pe")), ("dv", E.tok("dv")),
                                ("ac", E.tok("ac")),
                                (f"gpd{g}", E.tok(f"gpd{g}"))])
            for mg in range(16):
                ri = mg % 2
                for mi in range(2):
                    m = mg * 2 + mi
                    pm1, bk1 = (p4, "p4") if mi == 0 else (p5, "p5")
                    first = True
                    for kc in range(32):
                        wt = ([(f"gpd{ri}", dmh[ri])] + dep +
                              BK[bk1].for_write()) if first else ()
                        first = False
                        t1 = mm(hstr[ri][:, kc * 256 + mi * 128:
                                         kc * 256 + mi * 128 + 128],
                                src[:, kc * 512:(kc + 1) * 512],
                                pm1[:, 0:512], kc == 0, kc == 31, waits=wt)
                    BK[bk1].wrote("pe", t1)
                    tgl = act(lambda e, pm1=pm1, m=m, scol=scol: e.activation(
                        out=dst[:, m * 512:(m + 1) * 512], in_=pm1[:, 0:512],
                        func=AF.Gelu, scale=scl_s[:, scol + m: scol + m + 1]),
                        waits=BK[bk1].for_read())
                    BK[bk1].read("ac", tgl)
                if mg + 2 < 16:
                    dmh[ri] = hload(hT, hB, ri, mg + 2,
                                    [("pe", E.tok("pe")),
                                     (f"gpd{ri}", E.tok(f"gpd{ri}"))])
            return E.tok("ac")

        h1_done = hlayer(h1T, d_h1b, a_sb, b_sb, 64, [("dv", a_done)])
        h2_done = hlayer(h2T, d_h2b, b_sb, a_sb, 96, [("ac", h1_done)])

        dmh = [0, 0]
        for g in range(2):
            dmh[g] = hload(w3T, d_w3b, g, g,
                           [("pe", E.tok("pe")), ("ac", E.tok("ac")),
                            (f"gpd{g}", E.tok(f"gpd{g}"))])
        x1t = [0, 0]
        for g in range(2):
            x1t[g] = dma(f"spd{g}", x1fr[g][:, :], d_x1f[g, :, woff:woff + 512],
                         waits=[("dv", E.tok("dv")), ("spd", E.tok("spd"))])
        for mg in range(4):
            ri = mg % 2
            for mi in range(2):
                m = mg * 2 + mi
                pm1, bk1 = (p4, "p4") if mi == 0 else (p5, "p5")
                first = True
                for kc in range(32):
                    wt = ([(f"gpd{ri}", dmh[ri]), ("ac", h2_done),
                           (f"spd{m % 2}", x1t[m % 2])] + BK[bk1].for_write()) if first else ()
                    first = False
                    t1 = mm(hstr[ri][:, kc * 256 + mi * 128: kc * 256 + mi * 128 + 128],
                            a_sb[:, kc * 512:(kc + 1) * 512],
                            pm1[:, 0:512], kc == 0, kc == 31, waits=wt)
                BK[bk1].wrote("pe", t1)
                tr2 = dve(lambda e, pm1=pm1, m=m: e.scalar_tensor_tensor(
                    out=r2_w[:, m * 512:(m + 1) * 512], in0=pm1[:, 0:512],
                    scalar=scl_s[:, 128 + m:129 + m], in1=x1fr[m % 2][:, :],
                    op0=ALU.mult, op1=ALU.add), waits=BK[bk1].for_read())
                BK[bk1].read("dv", tr2)
                if m + 2 < 8:
                    x1t[m % 2] = dma(f"spd{m % 2}", x1fr[m % 2][:, :],
                                     d_x1f[m + 2, :, woff:woff + 512],
                                     waits=[("dv", tr2)])
            if mg + 2 < 4:
                dmh[ri] = hload(w3T, d_w3b, ri, mg + 2,
                                [("pe", E.tok("pe")),
                                 (f"gpd{ri}", E.tok(f"gpd{ri}"))])
        r2_done = E.tok("dv")

        tq3 = act(lambda e: e.activation(out=sq2_w[:, :], in_=r2_w[:, :],
                                         func=AF.Square), waits=[("dv", r2_done)])
        first = True
        for kc in range(NKC):
            wt = ([("dv", r2_done)] + BK["p0"].for_write()) if first else ()
            first = False
            t1 = mm(ones_s[:, 0:1], r2_w[:, kc * 512:(kc + 1) * 512],
                    p0[0:1, 0:512], kc == 0, kc == NKC - 1, waits=wt)
        BK["p0"].wrote("pe", t1)
        first = True
        for kc in range(NKC):
            wt = ([("ac", tq3)] + BK["p1"].for_write()) if first else ()
            first = False
            t2 = mm(ones_s[:, 0:1], sq2_w[:, kc * 512:(kc + 1) * 512],
                    p1[0:1, 0:512], kc == 0, kc == NKC - 1, waits=wt)
        BK["p1"].wrote("pe", t2)
        tmu = dve(lambda e: e.tensor_scalar(out=row_t[0:1, 0:512], in0=p0[0:1, :],
                                            scalar1=1.0 / DM, scalar2=None,
                                            op0=ALU.mult), waits=BK["p0"].for_read())
        BK["p0"].read("dv", tmu)
        tvv = dve(lambda e: e.tensor_scalar(out=row_t[0:1, 512:1024], in0=p1[0:1, :],
                                            scalar1=1.0 / DM, scalar2=None,
                                            op0=ALU.mult), waits=BK["p1"].for_read())
        BK["p1"].read("dv", tvv)
        t_r = dve(lambda e: e.tensor_tensor(out=row_t[0:1, 1024:1536], in0=row_t[0:1, 0:512],
                                            in1=row_t[0:1, 0:512], op=ALU.mult),
                  waits=[("dv", tmu)])
        t_r = dve(lambda e: e.tensor_tensor(out=row_t[0:1, 512:1024],
                                            in0=row_t[0:1, 512:1024],
                                            in1=row_t[0:1, 1024:1536], op=ALU.subtract),
                  waits=[("dv", t_r)])
        tva = dve(lambda e: e.tensor_scalar(out=row_t[0:1, 512:1024],
                                            in0=row_t[0:1, 512:1024], scalar1=1e-5,
                                            scalar2=None, op0=ALU.add),
                  waits=[("dv", t_r)])
        tsd = act(lambda e: e.activation(out=row_t[0:1, 1024:1536],
                                         in_=row_t[0:1, 512:1024], func=AF.Sqrt),
                  waits=[("dv", tva)])
        tiv = dve(lambda e: e.reciprocal(out=row_t[0:1, 512:1024],
                                         in_=row_t[0:1, 1024:1536]),
                  waits=[("ac", tsd)])
        wb = BK["p4"].for_write() + [("dv", tiv)]
        mm(ones_s[0:1, :], row_t[0:1, 0:512], p4[:, 0:512], True, True, waits=wb)
        tb2 = mm(ones_s[0:1, :], row_t[0:1, 512:1024], p4[:, 512:1024], True, True)
        BK["p4"].wrote("pe", tb2)
        for kc in range(NKC):
            t_ = dve(lambda e, kc=kc: e.tensor_tensor(
                out=y_w[:, kc * 512:(kc + 1) * 512],
                in0=r2_w[:, kc * 512:(kc + 1) * 512], in1=p4[:, 0:512],
                op=ALU.subtract),
                waits=(BK["p4"].for_read() + [("pe", t2)]) if kc == 0 else ())
            t_ = dve(lambda e, kc=kc: e.tensor_tensor(
                out=y_w[:, kc * 512:(kc + 1) * 512],
                in0=y_w[:, kc * 512:(kc + 1) * 512], in1=p4[:, 512:1024],
                op=ALU.mult), waits=[("dv", t_)])
        BK["p4"].read("dv", t_)
        ty = E.tok("dv")
        y_dma.append(dma("spd",
                         yT.rearrange("(kc p) t -> p kc t", p=128)[:, :, woff:woff + 512],
                         y_w[:].rearrange("p (kc t) -> p kc t", kc=NKC),
                         waits=[("dv", ty)]))
        g_pe2, g_dv2, g_ac2 = E.tok("pe"), E.tok("dv"), E.tok("ac")

    final_spd = E.count["spd"]
    def _final_wait(e, n=final_spd, sem=E.sems["spd"]):
        e.wait_ge(sem, n)
    E.ops["sync"].append(lambda e: _final_wait(e))
    E.finalize()
    return nc


_NC_CACHE = {}


def _get_nc():
    key = ("prog", tuple(DEBUG), PHASE, NPAIR_RUN)
    if key not in _NC_CACHE:
        _NC_CACHE[key] = build_program()
    return _NC_CACHE[key]


def kernel(L0_attn_x, L0_attn_Wq, L0_attn_Wk, L0_attn_Wv, L0_attn_Wgamma,
           L0_attn_Wf, L0_attn_Wg1, L0_attn_Wg2, L0_attn_Wo, L0_attn_norm_w,
           L0_ffn_w1_w, L0_ffn_w1_s, L0_ffn_w2_w, L0_ffn_w2_s, L0_ffn_h1_w,
           L0_ffn_h1_s, L0_ffn_h2_w, L0_ffn_h2_s, L0_ffn_w3_w, L0_ffn_w3_s):
    nc = _get_nc()
    x = np.asarray(L0_attn_x)
    cc = np.ascontiguousarray
    wsm_np = cc(np.concatenate([np.asarray(L0_attn_Wf), np.asarray(L0_attn_Wgamma),
                                np.asarray(L0_attn_Wg1)], axis=1))
    scl_np = np.zeros((128, 136), np.float32)
    scl_np[:, 0:32] = np.asarray(L0_ffn_w1_s).reshape(32, 128).T
    scl_np[:, 32:64] = np.asarray(L0_ffn_w2_s).reshape(32, 128).T
    scl_np[:, 64:96] = np.asarray(L0_ffn_h1_s).reshape(32, 128).T
    scl_np[:, 96:128] = np.asarray(L0_ffn_h2_s).reshape(32, 128).T
    scl_np[:, 128:136] = np.asarray(L0_ffn_w3_s).reshape(8, 128).T
    shared = {
        "wq": cc(np.asarray(L0_attn_Wq)), "wk": cc(np.asarray(L0_attn_Wk)),
        "wv": cc(np.asarray(L0_attn_Wv)), "wo": cc(np.asarray(L0_attn_Wo)),
        "wsm": wsm_np, "wg2": cc(np.asarray(L0_attn_Wg2)),
        "normw": cc(np.asarray(L0_attn_norm_w).reshape(8, 128).T),
        "w1T": cc(np.asarray(L0_ffn_w1_w).T), "w2T": cc(np.asarray(L0_ffn_w2_w).T),
        "h1T": cc(np.asarray(L0_ffn_h1_w).T), "h2T": cc(np.asarray(L0_ffn_h2_w).T),
        "w3T": cc(np.asarray(L0_ffn_w3_w).T), "scl": scl_np,
    }
    in_maps = []
    xTb = [cc(x[b].T) for b in range(2)]
    for core in range(8):
        b, s = core // 4, core % 4
        selv = np.zeros((128, 4), np.float32)
        selv[:, s] = 1.0
        m = dict(shared)
        m["xT"] = cc(xTb[b][:, s * SEG:(s + 1) * SEG])
        m["sel"] = selv
        in_maps.append(m)
    res = run_bass_kernel_spmd(nc, in_maps, core_ids=list(range(8)))
    out = np.zeros((2, 8192, DM), np.float32)
    for core in range(8):
        b, s = core // 4, core % 4
        out[b, s * SEG:(s + 1) * SEG, :] = np.asarray(res.results[core]["yT"]).T
    if DEBUG:
        global LAST_TAPS
        LAST_TAPS = [{k: np.asarray(v) for k, v in res.results[c].items()
                      if k.startswith("t_")} for c in range(8)]
    return out
